# version 01
# baseline (speedup 1.0000x reference)
import numpy as np
import ml_dtypes
import concourse.bass as bass
import concourse.mybir as mybir
from concourse.bass_utils import run_bass_kernel_spmd

F32 = mybir.dt.float32
BF16 = mybir.dt.bfloat16
I32 = mybir.dt.int32
AF = mybir.ActivationFunctionType
ALU = mybir.AluOpType
AX = mybir.AxisListType
ET = mybir.EngineType

BF = ml_dtypes.bfloat16

D = 2048
S = 2048
B_CORE = 2
NTOK = B_CORE * S
EPS = 1e-5
RSQRT_MAGIC = 0x5f3759df

ALL_ENG = [ET.SP, ET.Pool, ET.PE, ET.DVE, ET.Activation]
L1F_NSUP = 8
USE_FORI = True
SKIP_PREP = False
DUMMY_PREP = False
DIAG_SERIAL_DMA = True
DEBUG_FEATS = False
DEBUG_L1A = False


class _PyLoop:
    def __init__(self, n):
        self.n = n

    def __enter__(self):
        return 0

    def __exit__(self, *a):
        return False


def loop_ctx(nc, n):
    if USE_FORI:
        return nc.Fori(0, n, engines=ALL_ENG)
    return _PyLoop(n)



class Ctx:
    def __init__(self, nc, es):
        self.nc = nc
        self.es = es
        self.sems = {}
        self.n = {}
        for name in ["lx", "lw", "sx", "pe", "dv", "ac", "g2",
                     "lwA", "lwB", "lwC", "lwD"]:
            self.sems[name] = nc.alloc_semaphore(name)
            self.n[name] = 0
        self._uq = 0

    def uniq(self):
        self._uq += 1
        return f"u{self._uq}_"

    def inc(self, instr, name, k=1):
        instr.then_inc(self.sems[name], k)
        self.n[name] += k
        return instr

    def linc(self, instr, name, k=1):
        instr.then_inc(self.sems[name], k)
        return instr

    def bump(self, name, k):
        self.n[name] += k

    def wait(self, eng, name, val=None):
        eng.wait_ge(self.sems[name], self.n[name] if val is None else val)

    def barrier(self):
        self.nc.all_engine_barrier()


class Site:

    def __init__(self, C, eng, sem_name, init_off, delta, name):
        self.eng = eng
        self.sem = C.sems[sem_name]
        self.delta = delta
        self.reg = eng.alloc_register(name)
        eng.reg_mov(self.reg, C.n[sem_name] + init_off)

    def wait(self):
        self.eng.wait_ge(self.sem, self.reg)
        if self.delta:
            self.eng.reg_add(self.reg, self.reg, self.delta)

    def adjust(self, k):
        if k:
            self.eng.reg_add(self.reg, self.reg, k)

    def free(self):
        self.eng.free_register(self.reg)


class Off:

    def __init__(self, eng, init, delta, name):
        self.eng = eng
        self.delta = delta
        self.reg = eng.alloc_register(name)
        eng.reg_mov(self.reg, init)

    def next(self):
        self.eng.reg_add(self.reg, self.reg, self.delta)

    def free(self):
        self.eng.free_register(self.reg)


def rap(tensor_ap, reg, pairs):
    t = tensor_ap.tensor if isinstance(tensor_ap, bass.AP) else tensor_ap
    return bass.AP(t, reg, pairs)


def newton_rsqrt(C, v_ap, out_ap, ia_ap, ib_ap, tmp_ap, tmp2_ap, iters=3):
    v = C.nc.vector
    v.tensor_single_scalar(ia_ap, v_ap.bitcast(I32), 1, ALU.arith_shift_right)
    v.drain()
    v.tensor_scalar(ib_ap, ia_ap, -1, RSQRT_MAGIC, ALU.mult, ALU.add)
    v.drain()
    cur, nxt = ib_ap, ia_ap
    for _ in range(iters):
        r = cur.bitcast(F32)
        v.tensor_mul(tmp_ap, r, r)
        v.drain()
        v.tensor_mul(tmp2_ap, tmp_ap, v_ap)
        v.drain()
        v.tensor_scalar(tmp_ap, tmp2_ap, -0.5, 1.5, ALU.mult, ALU.add)
        v.drain()
        v.tensor_mul(nxt.bitcast(F32), r, tmp_ap)
        v.drain()
        cur, nxt = nxt, cur
    v.tensor_copy(out_ap, cur.bitcast(F32))
    v.drain()



def emit_prep(C, jobs):
    nc = C.nc
    from contextlib import ExitStack
    with ExitStack() as ps:
        buf = [ps.enter_context(nc.sbuf_tensor(f"prep{i}", [128, 4096], BF16))
               for i in range(2)]
        k = 0
        for src, dst in jobs:
            R, Cc = src.shape
            assert R % 128 == 0, (R, Cc)
            for r0 in range(0, R, 128):
                for c0 in range(0, Cc, 4096):
                    w = min(4096, Cc - c0)
                    b = buf[k % 2]
                    if k >= 2:
                        C.wait(nc.gpsimd, "sx", C.n["sx"])
                    C.inc(nc.gpsimd.dma_start(b[:, 0:w], src[r0:r0 + 128, c0:c0 + w]),
                          "lw", 16)
                    C.wait(nc.sync, "lw")
                    C.inc(nc.sync.dma_start(dst[r0:r0 + 128, c0:c0 + w], b[:, 0:w]),
                          "sx", 16)
                    k += 1
        C.wait(nc.sync, "sx")



def emit_l1ffn(C, x_in, x_out, wgT, wuT, wdT, wsem="lw"):
    nc = C.nc
    uq = C.uniq()
    from contextlib import ExitStack
    with ExitStack() as ps:
        sb = lambda n, sh, dt: ps.enter_context(nc.sbuf_tensor(uq + n, sh, dt))
        xt = sb("f_xt", [128, 4 * D], F32)
        yT = sb("f_yT", [128, 16 * 512], BF16)
        gup = sb("f_gup", [128, 64 * 512], BF16)
        wg_b = [sb(f"f_wg{i}", [128, 16 * 128], BF16) for i in range(2)]
        wu_b = [sb(f"f_wu{i}", [128, 16 * 128], BF16) for i in range(2)]
        wd_b = [sb(f"f_wd{i}", [128, 512], BF16) for i in range(8)]
        sq = sb("f_sq", [128, D], F32)
        ybf4 = sb("f_ybf4", [128, 4 * D], BF16)
        gs = [sb(f"f_gs{i}", [128, 512], BF16) for i in range(2)]
        st4 = sb("f_st4", [128, 16], F32)
        sti = sb("f_sti", [128, 4], I32)
        sti2 = sb("f_sti2", [128, 4], I32)
        stt1 = sb("f_stt1", [128, 4], F32)
        stt2 = sb("f_stt2", [128, 4], F32)

        tp = ps.enter_context(nc.psum_tensor(uq + "f_tp", [128, 4096], BF16))
        p = [ps.enter_context(nc.psum_tensor(f"{uq}f_p{i}", [128, 512], F32))
             for i in range(4)]

        mu, var, rstd, nmr = (st4[:, 0:4], st4[:, 4:8], st4[:, 8:12], st4[:, 12:16])
        idn = C.idn_bf
        yT3 = yT[:].rearrange("p (k s) -> p k s", k=16)

        NSUP = L1F_NSUP
        DV, AC, PE_, LX, LW, SX = 72, 66, 386, 16, 6144, 16

        S1 = Site(C, nc.sync, "sx", 0, SX, uq + "fS1")
        S2 = Site(C, nc.sync, "dv", DV, DV, uq + "fS2")
        G1 = Site(C, nc.gpsimd, "pe", 2 if DIAG_SERIAL_DMA else 0, 2, uq + "fG1")
        G2 = Site(C, nc.gpsimd, "pe", 130 if DIAG_SERIAL_DMA else 125, 1, uq + "fG2")
        P1b = Site(C, nc.tensor, "dv", 0, DV, uq + "fP1b")
        P1 = Site(C, nc.tensor, "ac", 2, AC, uq + "fP1")
        P2 = Site(C, nc.tensor, "dv", 3, DV, uq + "fP2")
        P3 = Site(C, nc.tensor, wsem, 32, 32, uq + "fP3")
        P3b = Site(C, nc.tensor, "dv", 4, DV, uq + "fP3b")
        P3c = Site(C, nc.tensor, "ac", 1, 1, uq + "fP3c")
        P3d = Site(C, nc.tensor, "dv", 3, 1, uq + "fP3d")
        P4a = Site(C, nc.tensor, "ac", AC, AC, uq + "fP4a")
        P4b = Site(C, nc.tensor, "dv", 68, DV, uq + "fP4b")
        P5 = Site(C, nc.tensor, wsem, 2048 + 16, 16, uq + "fP5")
        P6 = Site(C, nc.tensor, "dv", 69, 1, uq + "fP6")
        V1 = Site(C, nc.vector, "lx", 16, LX, uq + "fV1")
        V2 = Site(C, nc.vector, "ac", 1, AC, uq + "fV2")
        V3 = Site(C, nc.vector, "pe", 1, PE_, uq + "fV3")
        V4 = Site(C, nc.vector, "pe", 2, PE_, uq + "fV4")
        V5a = Site(C, nc.vector, "pe", 4, 2, uq + "fV5a")
        V5b = Site(C, nc.vector, "ac", 3, 1, uq + "fV5b")
        V6 = Site(C, nc.vector, "sx", 0, SX, uq + "fV6")
        V7 = Site(C, nc.vector, "pe", 194, 64, uq + "fV7")
        A1 = Site(C, nc.scalar, "dv", 1, DV, uq + "fA1")
        A2 = Site(C, nc.scalar, "dv", 2, DV, uq + "fA2")
        A3 = Site(C, nc.scalar, "pe", 3, 2, uq + "fA3")
        A4 = Site(C, nc.scalar, "dv", 3, 1, uq + "fA4")
        sites = [S1, S2, G1, G2, P1b, P1, P2, P3, P3b, P3c, P3d, P4a, P4b, P5, P6,
                 V1, V2, V3, V4, V5a, V5b, V6, V7, A1, A2, A3, A4]

        ox_ld = Off(nc.sync, 0, 512 * D, uq + "fox_ld")
        ox_st = Off(nc.sync, 0, 512 * D, uq + "fox_st")

        for _rep in range(1 if USE_FORI else NSUP):
         if _rep == 0:
          ctxmgr = loop_ctx(nc, NSUP)
         else:
          ctxmgr = _PyLoop(NSUP)
         with ctxmgr:
            S1.wait()
            C.linc(nc.sync.dma_start(
                xt[:].rearrange("p (t j) -> p t j", t=4),
                rap(x_in, ox_ld.reg, [[D, 128], [128 * D, 4], [1, D]])),
                "lx", 16)
            ox_ld.next()
            S2.wait()
            C.linc(nc.sync.dma_start(
                rap(x_out, ox_st.reg, [[D, 128], [128 * D, 4], [1, D]]),
                xt[:].rearrange("p (t j) -> p t j", t=4)),
                "sx", 16)
            ox_st.next()

            for hc in range(64):
                G1.wait()
                C.linc(nc.gpsimd.dma_start(
                    wg_b[hc % 2][:],
                    wgT[hc * 128:(hc + 1) * 128, :]), wsem, 16)
                C.linc(nc.gpsimd.dma_start(
                    wu_b[hc % 2][:],
                    wuT[hc * 128:(hc + 1) * 128, :]), wsem, 16)
            G1.adjust(PE_ - 128)
            for dc in range(4):
                for hc in range(64):
                    G2.wait()
                    C.linc(nc.gpsimd.dma_start(
                        wd_b[(dc * 64 + hc) % 8][:],
                        bass.AP(wdT.tensor, hc * 128 * D + dc * 512,
                                [[D, 128], [1, 512]])), wsem, 16)
            G2.adjust(130)

            V1.wait()
            for t in range(4):
                nc.vector.reduce_sum(mu[:, t:t + 1], xt[:, t * D:(t + 1) * D],
                                     axis=AX.X, negate=True)
            nc.vector.drain()
            C.linc(nc.vector.tensor_scalar_mul(mu, mu, 1.0 / D), "dv", 1)
            V2.wait()
            nc.vector.tensor_scalar(var, var, 1.0 / D, EPS, ALU.mult, ALU.add)
            nc.vector.drain()
            newton_rsqrt(C, var, rstd, sti[:], sti2[:], stt1[:], stt2[:])
            C.linc(nc.vector.tensor_mul(nmr, mu, rstd), "dv", 1)
            for g, site in ((0, V3), (1, V4)):
                site.wait()
                instr = None
                for t in (2 * g, 2 * g + 1):
                    instr = nc.vector.tensor_copy(
                        yT3[:, :, t * 128:(t + 1) * 128],
                        tp[:, (t % 2) * 2048:(t % 2) * 2048 + 2048]
                        .rearrange("p (k m) -> p k m", k=16))
                C.linc(instr, "dv", 1)
            for hc in range(64):
                V5a.wait()
                V5b.wait()
                C.linc(nc.vector.tensor_tensor(
                    gup[:, hc * 512:(hc + 1) * 512], p[2 + hc % 2][:],
                    gs[hc % 2][:], ALU.mult), "dv", 1)
            V5a.adjust(PE_ - 128)
            V5b.adjust(AC - 64)
            V6.wait()
            for dc in range(4):
                V7.wait()
                instr = None
                for t in range(4):
                    sl = slice(t * D + dc * 512, t * D + (dc + 1) * 512)
                    instr = nc.vector.scalar_tensor_tensor(
                        xt[:, sl], p[t][:], 0.5, xt[:, sl], ALU.mult, ALU.add)
                C.linc(instr, "dv", 1)
            V7.adjust(130)

            A1.wait()
            instr = None
            for t in range(4):
                instr = nc.scalar.activation(sq[:], xt[:, t * D:(t + 1) * D],
                                             AF.Square, bias=mu[:, t:t + 1],
                                             accum_out=var[:, t:t + 1])
            C.linc(instr, "ac", 1)
            A2.wait()
            instr = None
            for t in range(4):
                instr = nc.scalar.activation(ybf4[:, t * D:(t + 1) * D],
                                             xt[:, t * D:(t + 1) * D],
                                             AF.Identity, scale=rstd[:, t:t + 1],
                                             bias=nmr[:, t:t + 1])
            C.linc(instr, "ac", 1)
            for hc in range(64):
                A3.wait()
                A4.wait()
                C.linc(nc.scalar.activation(gs[hc % 2][:], p[hc % 2][:], AF.Silu),
                       "ac", 1)
            A3.adjust(PE_ - 128)
            A4.adjust(DV - 64)

            P1b.wait()
            P1.wait()
            for g in range(2):
                if g == 1:
                    P2.wait()
                instr = None
                for t in (2 * g, 2 * g + 1):
                    for k in range(16):
                        instr = nc.tensor.transpose(
                            tp[:, (t % 2) * 2048 + k * 128:(t % 2) * 2048 + (k + 1) * 128],
                            ybf4[:, t * D + k * 128:t * D + (k + 1) * 128], idn[:])
                C.linc(instr, "pe", 1)
            P3b.wait()
            for hc in range(64):
                P3.wait()
                P3c.wait()
                P3d.wait()
                instr = None
                for k in range(16):
                    instr = nc.tensor.matmul(p[hc % 2][:],
                                             wg_b[hc % 2][:, k * 128:(k + 1) * 128],
                                             yT[:, k * 512:(k + 1) * 512],
                                             start=(k == 0), stop=(k == 15))
                C.linc(instr, "pe", 1)
                instr = None
                for k in range(16):
                    instr = nc.tensor.matmul(p[2 + hc % 2][:],
                                             wu_b[hc % 2][:, k * 128:(k + 1) * 128],
                                             yT[:, k * 512:(k + 1) * 512],
                                             start=(k == 0), stop=(k == 15))
                C.linc(instr, "pe", 1)
            P3.adjust(LW - 32 * 64)
            P3c.adjust(AC - 64)
            P3d.adjust(DV - 64)
            P4a.wait()
            P4b.wait()
            for dc in range(4):
                if dc > 0:
                    P6.wait()
                for hc in range(64):
                    P5.wait()
                    instr = None
                    for t in range(4):
                        instr = nc.tensor.matmul(
                            p[t][:],
                            gup[:, hc * 512 + t * 128: hc * 512 + (t + 1) * 128],
                            wd_b[(dc * 64 + hc) % 8][:],
                            start=(hc == 0), stop=(hc == 63))
                    C.linc(instr, "pe", 1)
            P5.adjust(LW - 16 * 256)
            P6.adjust(DV - 3)

        for name, tot in (("lx", LX), (wsem, LW), ("sx", SX), ("pe", PE_),
                          ("dv", DV), ("ac", AC)):
            C.bump(name, tot * NSUP)

        C.wait(nc.sync, "sx")
        for s_ in sites:
            s_.free()
        ox_ld.free()
        ox_st.free()



def emit_moe(C, x_in, x_out, wg8, wi0s, wi1s, wos):
    nc = C.nc
    uq = C.uniq()
    from contextlib import ExitStack
    with ExitStack() as ps:
        sb = lambda n, sh, dt: ps.enter_context(nc.sbuf_tensor(uq + n, sh, dt))
        xt = sb("m_xt", [128, 4 * D], F32)
        yT = sb("m_yT", [128, 16 * 512], BF16)
        inter = [sb(f"m_it{e}", [128, 8 * 512], BF16) for e in range(8)]
        wi0b = [sb(f"m_w0{i}", [128, 2048], BF16) for i in range(2)]
        wi1b = [sb(f"m_w1{i}", [128, 2048], BF16) for i in range(2)]
        wob = [sb(f"m_wo{i}", [128, 8 * 512], BF16) for i in range(2)]
        sq = sb("m_sq", [128, D], F32)
        ybf4 = sb("m_ybf4", [128, 4 * D], BF16)
        ss = [sb(f"m_ss{i}", [128, 512], BF16) for i in range(2)]
        itmp = sb("m_itmp", [128, 512], F32)
        st4 = sb("m_st4", [128, 16], F32)
        sti = sb("m_sti", [128, 4], I32)
        sti2 = sb("m_sti2", [128, 4], I32)
        stt1 = sb("m_stt1", [128, 4], F32)
        stt2 = sb("m_stt2", [128, 4], F32)
        m1p4 = sb("m_m1p", [128, 4], F32)
        m1n4 = sb("m_m1n", [128, 4], F32)
        m2p4 = sb("m_m2p", [128, 4], F32)
        d214 = sb("m_d21", [128, 4], F32)
        den4 = sb("m_den", [128, 4], F32)
        rec4 = sb("m_rec", [128, 4], F32)
        msk = sb("m_msk", [128, 32], F32)
        lgm = sb("m_lgm", [128, 32], F32)
        e1s = sb("m_e1", [128, 32], F32)
        ge2 = sb("m_ge2", [128, 32], F32)
        wtmp = sb("m_wtmp", [128, 8], F32)
        wtk = sb("m_wtk", [128, 32], F32)
        wbf = sb("m_wbf", [128, 32], BF16)
        wT = sb("m_wT", [8, 512], BF16)
        wb8 = sb("m_wb8", [128, 8 * 512], F32)

        tp = ps.enter_context(nc.psum_tensor(uq + "m_tp", [128, 4096], BF16))
        p = [ps.enter_context(nc.psum_tensor(f"{uq}m_p{i}", [128, 512], F32))
             for i in range(4)]

        mu, var, rstd, nmr = (st4[:, 0:4], st4[:, 4:8], st4[:, 8:12], st4[:, 12:16])
        idn = C.idn_bf
        oh8 = C.oh8_bf
        yT3 = yT[:].rearrange("p (k s) -> p k s", k=16)

        NSUP = 8
        DV, AC, PE_, LX, SX = 92, 70, 175, 16, 16
        LAB = 32 * 32
        LCD = 16 * 16

        S1 = Site(C, nc.sync, "sx", 0, SX, uq + "mS1")
        S2 = Site(C, nc.sync, "dv", DV, DV, uq + "mS2")
        G1a = Site(C, nc.gpsimd, "pe", 13, 2, uq + "mG1a")
        G2a = Site(C, nc.gpsimd, "pe", 142, 1, uq + "mG2a")
        P1b = Site(C, nc.tensor, "dv", 0, DV, uq + "mP1b")
        P1 = Site(C, nc.tensor, "ac", 2, AC, uq + "mP1")
        P2 = Site(C, nc.tensor, "dv", 3, DV, uq + "mP2")
        PB1 = Site(C, nc.tensor, "dv", 4, DV, uq + "mPB1")
        PB2 = [Site(C, nc.tensor, "dv", v, DV, f"{uq}mPB2_{t}")
               for t, v in enumerate((9, 13, 14, 15))]
        PB4 = Site(C, nc.tensor, "dv", 16, DV, uq + "mPB4")
        PB5 = Site(C, nc.tensor, "dv", 17, 1, uq + "mPB5")
        PCa = Site(C, nc.tensor, "lwA", 32, 32, uq + "mPCa")
        PCb = Site(C, nc.tensor, "lwB", 32, 32, uq + "mPCb")
        PC0a = Site(C, nc.tensor, "ac", 6, AC, uq + "mPC0a")
        PC0b = Site(C, nc.tensor, "dv", 24, DV, uq + "mPC0b")
        PC2 = Site(C, nc.tensor, "ac", 5, 1, uq + "mPC2")
        PC3 = Site(C, nc.tensor, "dv", 23, 1, uq + "mPC3")
        PDa = Site(C, nc.tensor, "lwC", 16, 16, uq + "mPDa")
        PDb = Site(C, nc.tensor, "lwD", 16, 16, uq + "mPDb")
        PD2 = Site(C, nc.tensor, "dv", 89, 1, uq + "mPD2")
        PD3a = Site(C, nc.tensor, "ac", 70, AC, uq + "mPD3a")
        PD3b = Site(C, nc.tensor, "dv", 88, DV, uq + "mPD3b")
        V1 = Site(C, nc.vector, "lx", 16, LX, uq + "mV1")
        V2 = Site(C, nc.vector, "ac", 1, AC, uq + "mV2")
        V3 = Site(C, nc.vector, "pe", 1, PE_, uq + "mV3")
        V4 = Site(C, nc.vector, "pe", 2, PE_, uq + "mV4")
        VB1 = Site(C, nc.vector, "pe", 3, PE_, uq + "mVB1")
        VB2 = [Site(C, nc.vector, "ac", 3 + t, AC, f"{uq}mVB2_{t}") for t in range(4)]
        VB3 = [Site(C, nc.vector, "pe", 4 + t, PE_, f"{uq}mVB3_{t}") for t in range(4)]
        VB4 = Site(C, nc.vector, "pe", 8, 1, uq + "mVB4")
        VC1a = Site(C, nc.vector, "pe", 17, 2, uq + "mVC1a")
        VC1b = Site(C, nc.vector, "ac", 7, 1, uq + "mVC1b")
        VD1 = Site(C, nc.vector, "pe", 151, 8, uq + "mVD1")
        V6 = Site(C, nc.vector, "sx", 0, SX, uq + "mV6")
        A1 = Site(C, nc.scalar, "dv", 1, DV, uq + "mA1")
        A2 = Site(C, nc.scalar, "dv", 2, DV, uq + "mA2")
        AB1 = [Site(C, nc.scalar, "dv", 5 + t, DV, f"{uq}mAB1_{t}") for t in range(4)]
        AC1 = Site(C, nc.scalar, "pe", 16, 2, uq + "mAC1")
        AC2 = Site(C, nc.scalar, "dv", 23, 1, uq + "mAC2")
        sites = ([S1, S2, G1a, G2a, P1b, P1, P2, PB1, PB4, PB5, PCa, PCb,
                  PC0a, PC0b, PC2, PC3, PDa, PDb, PD2, PD3a, PD3b, V1, V2, V3,
                  V4, VB1, VB4, VC1a, VC1b, VD1, V6, A1, A2, AC1, AC2]
                 + PB2 + VB2 + VB3 + AB1)

        ox_ld = Off(nc.sync, 0, 512 * D, uq + "mox_ld")
        ox_st = Off(nc.sync, 0, 512 * D, uq + "mox_st")

        with nc.Fori(0, NSUP, engines=ALL_ENG):
            S1.wait()
            C.linc(nc.sync.dma_start(
                xt[:].rearrange("p (t j) -> p t j", t=4),
                rap(x_in, ox_ld.reg, [[D, 128], [128 * D, 4], [1, D]])), "lx", 16)
            ox_ld.next()
            S2.wait()
            C.linc(nc.sync.dma_start(
                rap(x_out, ox_st.reg, [[D, 128], [128 * D, 4], [1, D]]),
                xt[:].rearrange("p (t j) -> p t j", t=4)), "sx", 16)
            ox_st.next()

            for j in range(64):
                e, mc = j // 8, j % 8
                G1a.wait()
                sem = "lwA" if j % 2 == 0 else "lwB"
                C.linc(nc.gpsimd.dma_start(wi0b[j % 2][:], wi0s[e, mc]), sem, 16)
                C.linc(nc.gpsimd.dma_start(wi1b[j % 2][:], wi1s[e, mc]), sem, 16)
            G1a.adjust(PE_ - 128)
            for q in range(32):
                dcq, eq = q // 8, q % 8
                G2a.wait()
                sem = "lwC" if q % 2 == 0 else "lwD"
                C.linc(nc.gpsimd.dma_start(
                    wob[q % 2][:],
                    bass.AP(wos.tensor, eq * 1024 * 2048 + dcq * 512,
                            [[2048, 128], [128 * 2048, 8], [1, 512]])), sem, 16)
            G2a.adjust(PE_ - 32)

            V1.wait()
            for t in range(4):
                nc.vector.reduce_sum(mu[:, t:t + 1], xt[:, t * D:(t + 1) * D],
                                     axis=AX.X, negate=True)
            nc.vector.drain()
            C.linc(nc.vector.tensor_scalar_mul(mu, mu, 1.0 / D), "dv", 1)
            V2.wait()
            nc.vector.tensor_scalar(var, var, 1.0 / D, EPS, ALU.mult, ALU.add)
            nc.vector.drain()
            newton_rsqrt(C, var, rstd, sti[:], sti2[:], stt1[:], stt2[:])
            C.linc(nc.vector.tensor_mul(nmr, mu, rstd), "dv", 1)
            for g, site in ((0, V3), (1, V4)):
                site.wait()
                instr = None
                for t in (2 * g, 2 * g + 1):
                    instr = nc.vector.tensor_copy(
                        yT3[:, :, t * 128:(t + 1) * 128],
                        tp[:, (t % 2) * 2048:(t % 2) * 2048 + 2048]
                        .rearrange("p (k m) -> p k m", k=16))
                C.linc(instr, "dv", 1)
            VB1.wait()
            for t in range(4):
                sl = slice(8 * t, 8 * t + 8)
                pl = p[0][:, sl]
                nc.vector.reduce_max(m1p4[:, t:t + 1], pl, axis=AX.X)
                nc.vector.reduce_max(m1n4[:, t:t + 1], pl, axis=AX.X, negate=True)
                nc.vector.drain()
                nc.vector.tensor_single_scalar(msk[:, sl], pl, m1p4[:, t:t + 1],
                                               ALU.is_ge)
                nc.vector.drain()
                nc.vector.scalar_tensor_tensor(lgm[:, sl], msk[:, sl], -1e30, pl,
                                               ALU.mult, ALU.add)
                nc.vector.drain()
                C.linc(nc.vector.reduce_max(m2p4[:, t:t + 1], lgm[:, sl], axis=AX.X),
                       "dv", 1)
            for t in range(4):
                sl = slice(8 * t, 8 * t + 8)
                VB2[t].wait()
                nc.vector.tensor_single_scalar(ge2[:, sl], p[0][:, sl],
                                               m2p4[:, t:t + 1], ALU.is_ge)
                nc.vector.tensor_scalar_add(den4[:, t:t + 1], d214[:, t:t + 1], 1.0)
                nc.vector.drain()
                nc.vector.reciprocal(rec4[:, t:t + 1], den4[:, t:t + 1])
                nc.vector.drain()
                nc.vector.tensor_mul(wtmp[:], e1s[:, sl], ge2[:, sl])
                nc.vector.drain()
                nc.vector.tensor_scalar(wtk[:, sl], wtmp[:], rec4[:, t:t + 1], 0.5,
                                        ALU.mult, ALU.mult)
                nc.vector.drain()
                C.linc(nc.vector.tensor_copy(wbf[:, sl], wtk[:, sl]), "dv", 1)
            for t in range(4):
                VB3[t].wait()
                C.linc(nc.vector.tensor_copy(
                    wT[:, t * 128:(t + 1) * 128], tp[0:8, t * 128:(t + 1) * 128]),
                    "dv", 1)
            for e in range(8):
                VB4.wait()
                C.linc(nc.vector.tensor_copy(wb8[:, e * 512:(e + 1) * 512], p[1][:]),
                       "dv", 1)
            VB4.adjust(PE_ - 8)
            for j in range(64):
                e, mc = j // 8, j % 8
                VC1a.wait()
                VC1b.wait()
                ph1 = p[1 + 2 * (j % 2)][:]
                nc.vector.tensor_tensor(itmp[:], ph1, ss[j % 2][:], ALU.mult)
                nc.vector.drain()
                C.linc(nc.vector.tensor_tensor(
                    inter[e][:, mc * 512:(mc + 1) * 512], itmp[:],
                    wb8[:, e * 512:(e + 1) * 512], ALU.mult), "dv", 1)
            VC1a.adjust(PE_ - 128)
            VC1b.adjust(AC - 64)
            V6.wait()
            for dcq in range(4):
                VD1.wait()
                instr = None
                for t in range(4):
                    sl = slice(t * D + dcq * 512, t * D + (dcq + 1) * 512)
                    instr = nc.vector.scalar_tensor_tensor(
                        xt[:, sl], p[t][:], 1.0, xt[:, sl], ALU.mult, ALU.add)
                C.linc(instr, "dv", 1)
            VD1.adjust(PE_ - 32)

            A1.wait()
            instr = None
            for t in range(4):
                instr = nc.scalar.activation(sq[:], xt[:, t * D:(t + 1) * D],
                                             AF.Square, bias=mu[:, t:t + 1],
                                             accum_out=var[:, t:t + 1])
            C.linc(instr, "ac", 1)
            A2.wait()
            instr = None
            for t in range(4):
                instr = nc.scalar.activation(ybf4[:, t * D:(t + 1) * D],
                                             xt[:, t * D:(t + 1) * D],
                                             AF.Identity, scale=rstd[:, t:t + 1],
                                             bias=nmr[:, t:t + 1])
            C.linc(instr, "ac", 1)
            for t in range(4):
                sl = slice(8 * t, 8 * t + 8)
                AB1[t].wait()
                nc.scalar.activation(e1s[:, sl], p[0][:, sl], AF.Exp,
                                     bias=m1n4[:, t:t + 1])
                C.linc(nc.scalar.activation(d214[:, t:t + 1], m2p4[:, t:t + 1],
                                            AF.Exp, bias=m1n4[:, t:t + 1]),
                       "ac", 1)
            for j in range(64):
                AC1.wait()
                AC2.wait()
                ph0 = p[2 * (j % 2)][:]
                C.linc(nc.scalar.activation(ss[j % 2][:], ph0, AF.Silu), "ac", 1)
            AC1.adjust(PE_ - 128)
            AC2.adjust(DV - 64)

            P1b.wait()
            P1.wait()
            for g in range(2):
                if g == 1:
                    P2.wait()
                instr = None
                for t in (2 * g, 2 * g + 1):
                    for k in range(16):
                        instr = nc.tensor.transpose(
                            tp[:, (t % 2) * 2048 + k * 128:(t % 2) * 2048 + (k + 1) * 128],
                            ybf4[:, t * D + k * 128:t * D + (k + 1) * 128], idn[:])
                C.linc(instr, "pe", 1)
            PB1.wait()
            instr = None
            for t in range(4):
                for k in range(16):
                    instr = nc.tensor.matmul(
                        p[0][:, 8 * t:8 * t + 8],
                        yT[:, k * 512 + t * 128: k * 512 + (t + 1) * 128],
                        C.wg8_bf[:, k * 8:(k + 1) * 8],
                        start=(t == 0 and k == 0), stop=(t == 3 and k == 15))
            C.linc(instr, "pe", 1)
            for t in range(4):
                PB2[t].wait()
                C.linc(nc.tensor.transpose(
                    tp[0:8, t * 128:(t + 1) * 128],
                    wbf[:, 8 * t:8 * t + 8], idn[:]), "pe", 1)
            PB4.wait()
            for e in range(8):
                if e > 0:
                    PB5.wait()
                C.linc(nc.tensor.matmul(
                    p[1][:], oh8[:, e * 128:(e + 1) * 128], wT[:],
                    start=True, stop=True), "pe", 1)
            PB5.adjust(DV - 7)
            PC0a.wait()
            PC0b.wait()
            for j in range(64):
                e, mc = j // 8, j % 8
                (PCa if j % 2 == 0 else PCb).wait()
                PC2.wait()
                PC3.wait()
                ph0 = p[2 * (j % 2)][:]
                ph1 = p[1 + 2 * (j % 2)][:]
                instr = None
                for k in range(16):
                    instr = nc.tensor.matmul(ph0,
                                             wi0b[j % 2][:, k * 128:(k + 1) * 128],
                                             yT[:, k * 512:(k + 1) * 512],
                                             start=(k == 0), stop=(k == 15))
                C.linc(instr, "pe", 1)
                instr = None
                for k in range(16):
                    instr = nc.tensor.matmul(ph1,
                                             wi1b[j % 2][:, k * 128:(k + 1) * 128],
                                             yT[:, k * 512:(k + 1) * 512],
                                             start=(k == 0), stop=(k == 15))
                C.linc(instr, "pe", 1)
            PC2.adjust(AC - 64)
            PC3.adjust(DV - 64)
            PD3a.wait()
            PD3b.wait()
            for q in range(32):
                dcq, eq = q // 8, q % 8
                if eq == 0 and dcq > 0:
                    PD2.wait()
                (PDa if q % 2 == 0 else PDb).wait()
                instr = None
                for t in range(4):
                    for k in range(8):
                        instr = nc.tensor.matmul(
                            p[t][:],
                            inter[eq][:, k * 512 + t * 128: k * 512 + (t + 1) * 128],
                            wob[q % 2][:, k * 512:(k + 1) * 512],
                            start=(eq == 0 and k == 0), stop=(eq == 7 and k == 7))
                C.linc(instr, "pe", 1)
            PD2.adjust(DV - 3)

        for name, tot in (("lx", LX), ("sx", SX), ("pe", PE_), ("dv", DV),
                          ("ac", AC), ("lwA", LAB), ("lwB", LAB),
                          ("lwC", LCD), ("lwD", LCD)):
            C.bump(name, tot * NSUP)

        C.wait(nc.sync, "sx")
        for s_ in sites:
            s_.free()
        ox_ld.free()
        ox_st.free()



def ln_tile(C, x_ap, out_ap, sb1, rstd_out=None):
    nc = C.nc
    F = x_ap.shape[-1]
    nc.vector.reduce_sum(sb1["mu"], x_ap, axis=AX.X, negate=True)
    nc.vector.drain()
    C.inc(nc.vector.tensor_scalar_mul(sb1["mu"], sb1["mu"], 1.0 / F), "dv", 1)
    C.wait(nc.scalar, "dv")
    C.inc(nc.scalar.activation(sb1["sq"], x_ap, AF.Square, bias=sb1["mu"],
                               accum_out=sb1["var"]), "ac", 1)
    C.wait(nc.vector, "ac")
    nc.vector.tensor_scalar(sb1["var"], sb1["var"], 1.0 / F, EPS, ALU.mult, ALU.add)
    nc.vector.drain()
    newton_rsqrt(C, sb1["var"], sb1["rstd"], sb1["ia"], sb1["ib"],
                 sb1["t1"], sb1["t2"])
    C.inc(nc.vector.tensor_mul(sb1["nmr"], sb1["mu"], sb1["rstd"]), "dv", 1)
    C.wait(nc.scalar, "dv")
    C.inc(nc.scalar.activation(out_ap, x_ap, AF.Identity, scale=sb1["rstd"],
                               bias=sb1["nmr"]), "ac", 1)


def emit_l0a(C, x_in, x_out, W):
    nc = C.nc
    uq = C.uniq()
    from contextlib import ExitStack
    with ExitStack() as ps:
        sb = lambda n, sh, dt: ps.enter_context(nc.sbuf_tensor(uq + n, sh, dt))
        xt = sb("a_xt", [128, D], F32)
        y0f = sb("a_y0f", [128, D], F32)
        n2bf = sb("a_n2bf", [128, D], BF16)
        nT = sb("a_nT", [128, 16 * 128], BF16)
        zf = sb("a_zf", [128, D], F32)
        t4 = sb("a_t4", [128, D], F32)
        sq = sb("a_sq", [128, D], F32)
        u_sb = sb("a_u", [8, S], F32)
        decb = sb("a_decb", [8, S], F32)
        hs = sb("a_hs", [8, S], F32)
        hsb = sb("a_hsb", [8, S], BF16)
        hfbf = sb("a_hf", [32, 128], BF16)
        winf = sb("a_winf", [128, 128], F32)
        winb = sb("a_winb", [128, 128], BF16)
        winpb = sb("a_winpb", [128, 128], BF16)
        wdnf = sb("a_wdnf", [128, 512], F32)
        wdnb = sb("a_wdnb", [128, 512], BF16)
        wdnpb = sb("a_wdnpb", [128, 512], BF16)
        gb_f = sb("a_gb", [128, 32], F32)
        bb_bf = sb("a_bbf", [128, 16], BF16)
        woutb = sb("a_woutb", [8, D], BF16)
        wupb = sb("a_wupb", [32, D], BF16)
        dlg = sb("a_dlg", [8, 1], F32)
        dec = sb("a_dec", [8, 1], F32)
        binp = sb("a_binp", [8, 1], F32)
        bin1 = sb("a_bin1", [8, 1], F32)
        bdnp = sb("a_bdnp", [32, 1], F32)
        bdn1 = sb("a_bdn1", [32, 1], F32)
        r_bout = sb("a_rbo", [1, D], F32)
        r_dsk = sb("a_rds", [1, D], F32)
        r_bup = sb("a_rbu", [1, D], F32)
        r_t = sb("a_rt", [1, D], F32)
        r_bf = sb("a_rbf", [1, D], BF16)
        ones_bf = sb("a_ones", [1, 128], BF16)
        boutb = sb("a_boutb", [128, D], F32)
        gdb = sb("a_gdb", [128, D], BF16)
        bupb = sb("a_bupb", [128, D], F32)
        st1 = {k: sb(f"a_s_{k}", [128, 1], F32 if k not in ("ia", "ib") else I32)[:]
               for k in ("mu", "var", "rstd", "nmr", "t1", "t2", "ia", "ib")}
        st1["sq"] = sq[:]

        tp = ps.enter_context(nc.psum_tensor(uq + "a_tp", [128, 2048], BF16))
        pb = ps.enter_context(nc.psum_tensor(uq + "a_pb", [128, 2048], F32))
        p4 = ps.enter_context(nc.psum_tensor(uq + "a_p4", [128, 512], F32))

        g = nc.gpsimd
        C.inc(g.dma_start(winf[:], W["winp"][:].rearrange("p a b -> p (a b)")), "lw", 16)
        C.inc(g.dma_start(winb[:], W["winp"][:].rearrange("p a b -> p (a b)")), "lw", 16)
        C.inc(g.dma_start(wdnf[:], W["wdnp"][:].rearrange("p a b -> p (a b)")), "lw", 16)
        C.inc(g.dma_start(wdnb[:], W["wdnp"][:].rearrange("p a b -> p (a b)")), "lw", 16)
        C.inc(g.dma_start(gb_f[:, 0:16], W["gbp"][0]), "lw", 16)
        C.inc(g.dma_start(gb_f[:, 16:32], W["gbp"][1]), "lw", 16)
        C.inc(g.dma_start(bb_bf[:], W["gbp"][1]), "lw", 16)
        C.inc(g.dma_start(woutb[:], W["wout"][:]), "lw", 16)
        C.inc(g.dma_start(wupb[:], W["wup"][:]), "lw", 16)
        C.inc(g.dma_start(dlg[:], W["dlg"][:]), "lw", 16)
        C.inc(g.dma_start(binp[:], W["bin"][:]), "lw", 16)
        C.inc(g.dma_start(bdnp[:], W["bdn"][:]), "lw", 16)
        C.inc(g.dma_start(r_bout[:], W["rows"][0:1, :]), "lw", 16)
        C.inc(g.dma_start(r_dsk[:], W["rows"][1:2, :]), "lw", 16)
        C.inc(g.dma_start(r_bup[:], W["rows"][2:3, :]), "lw", 16)
        C.inc(g.dma_start(ones_bf[:], W["ones"][:]), "lw", 16)
        C.wait(nc.vector, "lw")
        C.wait(nc.scalar, "lw")
        C.wait(nc.tensor, "lw")
        g16 = gb_f[:, 0:16]
        nc.vector.tensor_tensor(
            winpb[:].rearrange("p (a b) -> p a b", a=16),
            winf[:].rearrange("p (a b) -> p a b", a=16),
            g16.unsqueeze(2).broadcast_to([128, 16, 8]), ALU.mult)
        nc.vector.tensor_tensor(
            wdnpb[:].rearrange("p (a b) -> p a b", a=16),
            wdnf[:].rearrange("p (a b) -> p a b", a=16),
            g16.unsqueeze(2).broadcast_to([128, 16, 32]), ALU.mult)
        C.inc(nc.scalar.activation(dec[:], dlg[:], AF.Sigmoid), "ac", 1)
        instr = None
        for k in range(16):
            instr = nc.tensor.matmul(p4[0:8, 0:1], winb[:, k * 8:(k + 1) * 8],
                                     bb_bf[:, k:k + 1], start=(k == 0), stop=(k == 15))
        C.inc(instr, "pe", 1)
        C.wait(nc.vector, "pe")
        C.inc(nc.vector.tensor_tensor(bin1[:], p4[0:8, 0:1], binp[:], ALU.add), "dv", 1)
        C.wait(nc.tensor, "dv")
        instr = None
        for k in range(16):
            instr = nc.tensor.matmul(p4[0:32, 1:2], wdnb[:, k * 32:(k + 1) * 32],
                                     bb_bf[:, k:k + 1], start=(k == 0), stop=(k == 15))
        C.inc(instr, "pe", 1)
        C.wait(nc.vector, "pe")
        C.inc(nc.vector.tensor_tensor(bdn1[:], p4[0:32, 1:2], bdnp[:], ALU.add), "dv", 1)
        bcasts = [("bout", boutb, None), ("gd", gdb, None), ("bup", bupb, None)]
        r_gam = sb("a_rgam", [1, D], F32)
        r_bet = sb("a_rbet", [1, D], F32)
        C.inc(g.dma_start(r_gam[:], W["rows"][3:4, :]), "lw", 16)
        C.inc(g.dma_start(r_bet[:], W["rows"][4:5, :]), "lw", 16)
        C.wait(nc.vector, "lw")
        for name, dstb, _ in bcasts:
            if name == "bout":
                nc.vector.tensor_mul(r_t[:], r_bet[:], r_dsk[:])
                nc.vector.drain()
                nc.vector.tensor_add(r_t[:], r_t[:], r_bout[:])
            elif name == "gd":
                nc.vector.tensor_mul(r_t[:], r_gam[:], r_dsk[:])
            else:
                nc.vector.tensor_copy(r_t[:], r_bup[:])
            nc.vector.drain()
            C.inc(nc.vector.tensor_copy(r_bf[:], r_t[:]), "dv", 1)
            C.wait(nc.tensor, "dv")
            instr = None
            for cc in range(4):
                instr = nc.tensor.matmul(pb[:, cc * 512:(cc + 1) * 512], ones_bf[:],
                                         r_bf[:, cc * 512:(cc + 1) * 512],
                                         start=True, stop=True)
            C.inc(instr, "pe", 1)
            C.wait(nc.vector, "pe")
            C.inc(nc.vector.tensor_copy(dstb[:], pb[:]), "dv", 1)
            C.wait(nc.tensor, "dv")

        for b in range(B_CORE):
            for t in range(16):
                row = b * S + t * 128
                C.wait(nc.sync, "dv")
                C.wait(nc.sync, "ac")
                C.inc(nc.sync.dma_start(xt[:], x_in[row:row + 128, :]), "lx", 16)
                C.wait(nc.vector, "lx")
                C.wait(nc.scalar, "lx")
                ln_tile(C, xt[:], y0f[:], st1)
                C.wait(nc.vector, "ac")
                ln_tile(C, y0f[:], n2bf[:], st1)
                C.wait(nc.tensor, "ac")
                C.wait(nc.tensor, "dv")
                instr = None
                for k in range(16):
                    instr = nc.tensor.transpose(tp[:, k * 128:(k + 1) * 128],
                                                n2bf[:, k * 128:(k + 1) * 128],
                                                C.idn_bf[:])
                C.inc(instr, "pe", 1)
                C.wait(nc.vector, "pe")
                C.inc(nc.vector.tensor_copy(nT[:], tp[:]), "dv", 1)
                C.wait(nc.tensor, "dv")
                instr = None
                for k in range(16):
                    instr = nc.tensor.matmul(p4[0:8, 0:128],
                                             winpb[:, k * 8:(k + 1) * 8],
                                             nT[:, k * 128:(k + 1) * 128],
                                             start=(k == 0), stop=(k == 15))
                C.inc(instr, "pe", 1)
                C.wait(nc.vector, "pe")
                C.inc(nc.vector.tensor_scalar(u_sb[:, t * 128:(t + 1) * 128],
                                              p4[0:8, 0:128], bin1[:], None,
                                              ALU.add), "dv", 1)
            C.wait(nc.vector, "dv")
            nc.vector.tensor_scalar(decb[:], u_sb[:], 0.0, dec[:], ALU.mult, ALU.add)
            nc.vector.drain()
            nc.vector.tensor_tensor_scan(hs[:], decb[:], u_sb[:], 0.0,
                                         ALU.mult, ALU.add)
            nc.vector.drain()
            C.inc(nc.vector.tensor_copy(hsb[:], hs[:]), "dv", 1)

            for t in range(16):
                row = b * S + t * 128
                C.wait(nc.sync, "dv")
                C.wait(nc.sync, "ac")
                C.inc(nc.sync.dma_start(xt[:], x_in[row:row + 128, :]), "lx", 16)
                C.wait(nc.vector, "lx")
                C.wait(nc.scalar, "lx")
                ln_tile(C, xt[:], y0f[:], st1)
                C.wait(nc.vector, "ac")
                ln_tile(C, y0f[:], n2bf[:], st1)
                C.wait(nc.tensor, "dv")
                C.wait(nc.tensor, "ac")
                instr = None
                for cc in range(4):
                    instr = nc.tensor.matmul(pb[:, cc * 512:(cc + 1) * 512],
                                             hsb[:, t * 128:(t + 1) * 128],
                                             woutb[:, cc * 512:(cc + 1) * 512],
                                             start=True, stop=True)
                C.inc(instr, "pe", 1)
                C.wait(nc.vector, "pe")
                nc.vector.tensor_tensor(t4[:], n2bf[:], gdb[:], ALU.mult)
                nc.vector.drain()
                nc.vector.tensor_add(t4[:], t4[:], y0f[:])
                nc.vector.drain()
                nc.vector.tensor_add(t4[:], t4[:], boutb[:])
                nc.vector.drain()
                C.inc(nc.vector.scalar_tensor_tensor(zf[:], pb[:], 1.0, t4[:],
                                                     ALU.mult, ALU.add), "dv", 1)
                C.wait(nc.scalar, "dv")
                C.wait(nc.vector, "dv")
                ln_tile(C, zf[:], n2bf[:], st1)
                C.wait(nc.tensor, "ac")
                C.wait(nc.tensor, "dv")
                instr = None
                for k in range(16):
                    instr = nc.tensor.transpose(tp[:, k * 128:(k + 1) * 128],
                                                n2bf[:, k * 128:(k + 1) * 128],
                                                C.idn_bf[:])
                C.inc(instr, "pe", 1)
                C.wait(nc.vector, "pe")
                C.inc(nc.vector.tensor_copy(nT[:], tp[:]), "dv", 1)
                C.wait(nc.tensor, "dv")
                instr = None
                for k in range(16):
                    instr = nc.tensor.matmul(p4[0:32, 0:128],
                                             wdnpb[:, k * 32:(k + 1) * 32],
                                             nT[:, k * 128:(k + 1) * 128],
                                             start=(k == 0), stop=(k == 15))
                C.inc(instr, "pe", 1)
                C.wait(nc.scalar, "pe")
                C.inc(nc.scalar.activation(hfbf[:], p4[0:32, 0:128], AF.Relu,
                                           bias=bdn1[:]), "ac", 1)
                C.wait(nc.tensor, "ac")
                instr = None
                for cc in range(4):
                    instr = nc.tensor.matmul(pb[:, cc * 512:(cc + 1) * 512],
                                             hfbf[:],
                                             wupb[:, cc * 512:(cc + 1) * 512],
                                             start=True, stop=True)
                C.inc(instr, "pe", 1)
                C.wait(nc.vector, "pe")
                nc.vector.tensor_add(t4[:], zf[:], bupb[:])
                nc.vector.drain()
                nc.vector.tensor_add(t4[:], t4[:], xt[:])
                nc.vector.drain()
                C.inc(nc.vector.scalar_tensor_tensor(xt[:], pb[:], 1.0, t4[:],
                                                     ALU.mult, ALU.add), "dv", 1)
                C.wait(nc.sync, "dv")
                C.inc(nc.sync.dma_start(x_out[row:row + 128, :], xt[:]), "sx", 16)
                C.wait(nc.sync, "sx")

        C.wait(nc.sync, "sx")



def emit_l1a(C, x_in, x_out, W, feats_scr):
    nc = C.nc
    uq = C.uniq()
    from contextlib import ExitStack
    with ExitStack() as ps:
        sb = lambda n, sh, dt: ps.enter_context(nc.sbuf_tensor(uq + n, sh, dt))
        xt = sb("q_xt", [128, D], F32)
        ybf = sb("q_ybf", [128, D], BF16)
        sq = sb("q_sq", [128, D], F32)
        st1 = {k: sb(f"q_s_{k}", [128, 1], F32 if k not in ("ia", "ib") else I32)[:]
               for k in ("mu", "var", "rstd", "nmr", "t1", "t2", "ia", "ib")}
        st1["sq"] = sq[:]
        bqkp = sb("q_bqkp", [128, 32], F32)
        r_bv = sb("q_rbv", [1, D], F32)
        r_bo = sb("q_rbo", [1, D], F32)
        r_bf = sb("q_rbf", [1, D], BF16)
        ones_bf = sb("q_ones", [1, 128], BF16)
        scl = sb("q_scl", [128, 1], F32)
        bvb = sb("q_bvb", [128, D], F32)
        bob = sb("q_bob", [128, D], F32)

        psc = [ps.enter_context(nc.psum_tensor(f"{uq}q_psc{i}", [128, 1024], F32))
               for i in range(2)]
        pf = ps.enter_context(nc.psum_tensor(uq + "q_pf", [96, 1024], F32))
        pnorm = ps.enter_context(nc.psum_tensor(uq + "q_pn", [128, 1024], BF16))
        prs = ps.enter_context(nc.psum_tensor(uq + "q_prs", [128, 512], BF16))

        g = nc.gpsimd
        C.inc(g.dma_start(bqkp[:, 0:16], W["bqp"][:]), "lw", 16)
        C.inc(g.dma_start(bqkp[:, 16:32], W["bkp"][:]), "lw", 16)
        C.inc(g.dma_start(r_bv[:], W["bv_row"][:]), "lw", 16)
        C.inc(g.dma_start(r_bo[:], W["bo_row"][:]), "lw", 16)
        C.inc(g.dma_start(ones_bf[:], W["ones"][:]), "lw", 16)
        if DEBUG_L1A:
            C.dbg_qT = nc.dram_tensor("dbg_qT", [128, 2 * 2048], BF16, kind="ExternalOutput").ap()
            C.dbg_kT = nc.dram_tensor("dbg_kT", [128, 2 * 2048], BF16, kind="ExternalOutput").ap()
            C.dbg_Vb = nc.dram_tensor("dbg_Vb", [128, 16 * 384], BF16, kind="ExternalOutput").ap()
            C.dbg_fm = nc.dram_tensor("dbg_fm", [64, 2048], BF16, kind="ExternalOutput").ap()
            C.dbg_rs = nc.dram_tensor("dbg_rs", [32, 2048], BF16, kind="ExternalOutput").ap()
            C.dbg_rsT = nc.dram_tensor("dbg_rsT", [128, 16], F32, kind="ExternalOutput").ap()
            C.dbg_rcT = nc.dram_tensor("dbg_rcT", [128, 16], BF16, kind="ExternalOutput").ap()
            C.dbg_ftg = nc.dram_tensor("dbg_ftg", [128, 16 * 256], BF16, kind="ExternalOutput").ap()
        C.wait(nc.vector, "lw")
        nc.vector.memset(scl[:], 1.0 / np.sqrt(32.0))
        nc.vector.drain()
        C.inc(nc.vector.tensor_copy(scl[:], scl[:]), "dv", 1)
        C.wait(nc.scalar, "dv")
        C.wait(nc.tensor, "dv")
        for row, dstb in ((r_bv, bvb), (r_bo, bob)):
            C.inc(nc.vector.tensor_copy(r_bf[:], row[:]), "dv", 1)
            C.wait(nc.tensor, "dv")
            instr = None
            for cc in range(2):
                instr = nc.tensor.matmul(psc[0][:, cc * 512:(cc + 1) * 512], ones_bf[:],
                                         r_bf[:, cc * 512:(cc + 1) * 512],
                                         start=True, stop=True)
                instr = nc.tensor.matmul(psc[1][:, cc * 512:(cc + 1) * 512], ones_bf[:],
                                         r_bf[:, 1024 + cc * 512:1024 + (cc + 1) * 512],
                                         start=True, stop=True)
            C.inc(instr, "pe", 1)
            C.wait(nc.vector, "pe")
            nc.vector.tensor_copy(dstb[:, 0:1024], psc[0][:])
            C.inc(nc.vector.tensor_copy(dstb[:, 1024:2048], psc[1][:]), "dv", 1)
            C.wait(nc.tensor, "dv")

        for b in range(B_CORE):
            with ExitStack() as bs:
                yT = bs.enter_context(nc.sbuf_tensor(f"{uq}q_yT_{b}", [128, 16 * 2048], BF16))
                qT = bs.enter_context(nc.sbuf_tensor(f"{uq}q_qT_{b}", [128, 2 * 2048], BF16))
                kT = bs.enter_context(nc.sbuf_tensor(f"{uq}q_kT_{b}", [128, 2 * 2048], BF16))
                Vb = bs.enter_context(nc.sbuf_tensor(f"{uq}q_Vb_{b}", [128, 16 * 384], BF16))
                pt = bs.enter_context(nc.sbuf_tensor(f"{uq}q_pt_{b}", [128, 1024], BF16))
                fm = bs.enter_context(nc.sbuf_tensor(f"{uq}q_fm_{b}", [64, 2048], BF16))
                rs32 = bs.enter_context(nc.sbuf_tensor(f"{uq}q_rs32_{b}", [32, 2048], BF16))
                rsT = bs.enter_context(nc.sbuf_tensor(f"{uq}q_rsT_{b}", [128, 16], F32))
                rcT = bs.enter_context(nc.sbuf_tensor(f"{uq}q_rcT_{b}", [128, 16], F32))
                rcTb = bs.enter_context(nc.sbuf_tensor(f"{uq}q_rcTb_{b}", [128, 16], BF16))
                ftg = bs.enter_context(nc.sbuf_tensor(f"{uq}q_ftg_{b}", [128, 16 * 256], BF16))
                wq_b = bs.enter_context(nc.sbuf_tensor(f"{uq}q_wq_{b}", [128, 16 * 256], BF16))
                wk_b = bs.enter_context(nc.sbuf_tensor(f"{uq}q_wk_{b}", [128, 16 * 256], BF16))
                wv_b = bs.enter_context(nc.sbuf_tensor(f"{uq}q_wv_{b}", [128, 16 * 256], BF16))
                yT3 = yT[:].rearrange("p (k s) -> p k s", k=16)

                for t in range(16):
                    row = b * S + t * 128
                    C.wait(nc.sync, "ac")
                    C.inc(nc.sync.dma_start(xt[:], x_in[row:row + 128, :]), "lx", 16)
                    C.wait(nc.vector, "lx")
                    C.wait(nc.scalar, "lx")
                    ln_tile(C, xt[:], ybf[:], st1)
                    C.wait(nc.tensor, "ac")
                    C.wait(nc.tensor, "dv")
                    for grp in range(2):
                        instr = None
                        for k in range(8):
                            kk = grp * 8 + k
                            instr = nc.tensor.transpose(
                                pnorm[:, k * 128:(k + 1) * 128],
                                ybf[:, kk * 128:(kk + 1) * 128], C.idn_bf[:])
                        C.inc(instr, "pe", 1)
                        C.wait(nc.vector, "pe")
                        C.inc(nc.vector.tensor_copy(
                            yT3[:, grp * 8:(grp + 1) * 8, t * 128:(t + 1) * 128],
                            pnorm[:].rearrange("p (k m) -> p k m", k=8)), "dv", 1)
                        C.wait(nc.tensor, "dv")

                for gi in range(8):
                    col = gi * 256
                    C.wait(nc.gpsimd, "pe")
                    for wb_, wsrc in ((wq_b, W["wq"]), (wk_b, W["wk"]), (wv_b, W["wv"])):
                        C.inc(g.dma_start(
                            wb_[:],
                            bass.AP(wsrc.tensor, col,
                                    [[D, 128], [128 * D, 16], [1, 256]])), "lw", 16)
                    C.wait(nc.tensor, "lw")
                    for dst, wb_, bcol in ((qT, wq_b, 0), (kT, wk_b, 16)):
                        for mc in range(2):
                            for ns in range(4):
                                instr = None
                                for kc in range(16):
                                    instr = nc.tensor.matmul(
                                        psc[0][:, 0:512],
                                        wb_[:, kc * 256 + mc * 128: kc * 256 + (mc + 1) * 128],
                                        yT[:, kc * 2048 + ns * 512: kc * 2048 + (ns + 1) * 512],
                                        start=(kc == 0), stop=(kc == 15))
                                C.inc(instr, "pe", 1)
                                C.wait(nc.scalar, "pe")
                                C.inc(nc.scalar.activation(
                                    dst[:, mc * 2048 + ns * 512: mc * 2048 + (ns + 1) * 512],
                                    psc[0][:, 0:512], AF.Identity,
                                    bias=bqkp[:, bcol + 2 * gi + mc: bcol + 2 * gi + mc + 1]),
                                    "ac", 1)
                                C.wait(nc.tensor, "ac")
                    C.wait(nc.vector, "ac")
                    C.inc(nc.vector.memset(Vb[:], 1.0), "dv", 1)
                    C.wait(nc.tensor, "dv")
                    for t in range(16):
                        instr = None
                        for kc in range(16):
                            instr = nc.tensor.matmul(
                                psc[1][:, 0:256],
                                yT[:, kc * 2048 + t * 128: kc * 2048 + (t + 1) * 128],
                                wv_b[:, kc * 256:(kc + 1) * 256],
                                start=(kc == 0), stop=(kc == 15))
                        C.inc(instr, "pe", 1)
                        C.wait(nc.vector, "pe")
                        C.inc(nc.vector.tensor_tensor(
                            Vb[:].rearrange("p (t h m) -> p t h m", t=16, h=4)
                            [:, t, :, 0:64].rearrange("p h m -> p (h m)")
                            if False else
                            bass.AP(Vb, Vb[:].offset + t * 384,
                                    [[Vb[:].ap[0][0], 128], [96, 4], [1, 64]]),
                            psc[1][:, 0:256].rearrange("p (h m) -> p h m", h=4),
                            bvb[:, col:col + 256].rearrange("p (h m) -> p h m", h=4),
                            ALU.add), "dv", 1)
                        C.wait(nc.tensor, "dv")
                    if DEBUG_L1A and b == 0 and gi == 0:
                        C.wait(nc.sync, "dv")
                        C.wait(nc.sync, "ac")
                        C.inc(nc.sync.dma_start(C.dbg_qT[:], qT[:]), "sx", 16)
                        C.inc(nc.sync.dma_start(C.dbg_kT[:], kT[:]), "sx", 16)
                        C.inc(nc.sync.dma_start(C.dbg_Vb[:], Vb[:]), "sx", 16)
                        C.wait(nc.sync, "sx")
                    C.wait(nc.tensor, "dv")
                    for h in range(4):
                        mc = h // 2
                        kb = 64 * (h % 2)
                        for sh in range(2):
                            for tc in range(16):
                                C.wait(nc.tensor, "ac")
                                instr = None
                                for nn in range(2):
                                    instr = nc.tensor.matmul(
                                        psc[tc % 2][:, nn * 512:(nn + 1) * 512],
                                        kT[kb:kb + 64,
                                           mc * 2048 + tc * 128: mc * 2048 + (tc + 1) * 128],
                                        qT[kb:kb + 64,
                                           mc * 2048 + sh * 1024 + nn * 512:
                                           mc * 2048 + sh * 1024 + (nn + 1) * 512],
                                        start=True, stop=True)
                                C.inc(instr, "pe", 1)
                                C.wait(nc.scalar, "pe")
                                C.inc(nc.scalar.activation(pt[:], psc[tc % 2][:],
                                                           AF.Exp, scale=scl[:]),
                                      "ac", 1)
                                C.wait(nc.tensor, "ac")
                                instr = None
                                for nn in range(2):
                                    instr = nc.tensor.matmul(
                                        pf[:, nn * 512:(nn + 1) * 512],
                                        Vb[:, tc * 384 + h * 96: tc * 384 + (h + 1) * 96],
                                        pt[:, nn * 512:(nn + 1) * 512],
                                        start=(tc == 0), stop=(tc == 15))
                                C.inc(instr, "pe", 1)
                            C.wait(nc.vector, "pe")
                            nc.vector.tensor_copy(fm[:, sh * 1024:(sh + 1) * 1024],
                                                  pf[0:64, :])
                            C.inc(nc.vector.tensor_copy(rs32[:, sh * 1024:(sh + 1) * 1024],
                                                        pf[64:96, :]), "dv", 1)
                            C.wait(nc.tensor, "dv")
                        for grp in range(2):
                            instr = None
                            for t in range(8):
                                tt = grp * 8 + t
                                instr = nc.tensor.transpose(
                                    pnorm[:, t * 128:t * 128 + 64],
                                    fm[:, tt * 128:(tt + 1) * 128],
                                    C.idn_bf[0:64, 0:64])
                            C.inc(instr, "pe", 1)
                            C.wait(nc.vector, "pe")
                            C.inc(nc.vector.tensor_copy(
                                bass.AP(ftg,
                                        ftg[:].offset + grp * 8 * 256 + h * 64,
                                        [[ftg[:].ap[0][0], 128], [256, 8], [1, 64]]),
                                pnorm[:].rearrange("p (t m) -> p t m", t=8)[:, :, 0:64]),
                                "dv", 1)
                            C.wait(nc.tensor, "dv")
                        if DEBUG_L1A and b == 0 and gi == 0 and h == 0:
                            C.wait(nc.sync, "dv")
                            C.inc(nc.sync.dma_start(C.dbg_fm[:], fm[:]), "sx", 16)
                            C.inc(nc.sync.dma_start(C.dbg_rs[:], rs32[:]), "sx", 16)
                            C.wait(nc.sync, "sx")
                        instr = None
                        for t in range(16):
                            instr = nc.tensor.transpose(
                                prs[:, t * 32:(t + 1) * 32],
                                rs32[:, t * 128:(t + 1) * 128],
                                C.idn_bf[0:32, 0:32])
                        C.inc(instr, "pe", 1)
                        C.wait(nc.vector, "pe")
                        nc.vector.tensor_copy(
                            rsT[:], prs[:].rearrange("p (t m) -> p t m", t=16)[:, :, 0:1]
                            .rearrange("p t m -> p (t m)"))
                        nc.vector.drain()
                        nc.vector.reciprocal(rcT[:], rsT[:])
                        nc.vector.drain()
                        C.inc(nc.vector.tensor_copy(rcTb[:], rcT[:]), "dv", 1)
                        nc.vector.drain()
                        if DEBUG_L1A and b == 0 and gi == 0 and h == 0:
                            C.wait(nc.sync, "dv")
                            C.inc(nc.sync.dma_start(C.dbg_rsT[:], rsT[:]), "sx", 16)
                            C.inc(nc.sync.dma_start(C.dbg_rcT[:], rcTb[:]), "sx", 16)
                            C.wait(nc.sync, "sx")
                        C.inc(nc.vector.tensor_tensor(
                            bass.AP(ftg, ftg[:].offset + h * 64,
                                    [[ftg[:].ap[0][0], 128], [256, 16], [1, 64]]),
                            bass.AP(ftg, ftg[:].offset + h * 64,
                                    [[ftg[:].ap[0][0], 128], [256, 16], [1, 64]]),
                            rcTb[:].unsqueeze(2).broadcast_to([128, 16, 64]),
                            ALU.mult), "dv", 1)
                        C.wait(nc.tensor, "dv")
                    if DEBUG_L1A and b == 0 and gi == 0:
                        C.wait(nc.sync, "dv")
                        C.inc(nc.sync.dma_start(C.dbg_ftg[:], ftg[:]), "sx", 16)
                        C.wait(nc.sync, "sx")
                    C.wait(nc.sync, "dv")
                    C.inc(nc.sync.dma_start(
                        bass.AP(feats_scr.tensor, b * S * D + col,
                                [[D, 128], [128 * D, 16], [1, 256]]),
                        ftg[:].rearrange("p (t m) -> p t m", t=16)), "sx", 16)
                    C.wait(nc.sync, "sx")
                    C.wait(nc.vector, "sx")

            with ExitStack() as bs:
                ftT = bs.enter_context(nc.sbuf_tensor(f"{uq}q_ftT_{b}", [128, 16 * 2048], BF16))
                wo_cc = bs.enter_context(nc.sbuf_tensor(f"{uq}q_wocc_{b}", [128, 16 * 512], BF16))
                ft_t = bs.enter_context(nc.sbuf_tensor(f"{uq}q_ftt_{b}", [128, 2048], BF16))
                xsl = bs.enter_context(nc.sbuf_tensor(f"{uq}q_xsl_{b}", [128, 512], F32))
                osl = bs.enter_context(nc.sbuf_tensor(f"{uq}q_osl_{b}", [128, 512], F32))
                fT3 = ftT[:].rearrange("p (k s) -> p k s", k=16)
                for t in range(16):
                    row = b * S + t * 128
                    C.wait(nc.sync, "dv")
                    C.inc(nc.sync.dma_start(ft_t[:], feats_scr[row:row + 128, :]),
                          "lx", 16)
                    C.wait(nc.tensor, "lx")
                    C.wait(nc.tensor, "dv")
                    for grp in range(2):
                        instr = None
                        for k in range(8):
                            kk = grp * 8 + k
                            instr = nc.tensor.transpose(
                                pnorm[:, k * 128:(k + 1) * 128],
                                ft_t[:, kk * 128:(kk + 1) * 128], C.idn_bf[:])
                        C.inc(instr, "pe", 1)
                        C.wait(nc.vector, "pe")
                        C.inc(nc.vector.tensor_copy(
                            fT3[:, grp * 8:(grp + 1) * 8, t * 128:(t + 1) * 128],
                            pnorm[:].rearrange("p (k m) -> p k m", k=8)), "dv", 1)
                        C.wait(nc.tensor, "dv")
                for cc in range(4):
                    C.wait(nc.gpsimd, "pe")
                    C.inc(g.dma_start(
                        wo_cc[:],
                        bass.AP(W["wo"].tensor, cc * 512,
                                [[D, 128], [128 * D, 16], [1, 512]])), "lw", 16)
                    C.wait(nc.tensor, "lw")
                    for t in range(16):
                        row = b * S + t * 128
                        C.wait(nc.sync, "dv")
                        C.inc(nc.sync.dma_start(
                            xsl[:], x_in[row:row + 128, cc * 512:(cc + 1) * 512]),
                            "lx", 16)
                        C.wait(nc.tensor, "lx")
                        instr = None
                        for kc in range(16):
                            instr = nc.tensor.matmul(
                                psc[0][:, 0:512],
                                fT3[:, kc, t * 128:(t + 1) * 128],
                                wo_cc[:, kc * 512:(kc + 1) * 512],
                                start=(kc == 0), stop=(kc == 15))
                        C.inc(instr, "pe", 1)
                        C.wait(nc.vector, "pe")
                        C.wait(nc.vector, "lx")
                        nc.vector.tensor_add(osl[:], xsl[:], bob[:, cc * 512:(cc + 1) * 512])
                        nc.vector.drain()
                        C.inc(nc.vector.scalar_tensor_tensor(
                            osl[:], psc[0][:, 0:512], 1.0, osl[:], ALU.mult, ALU.add),
                            "dv", 1)
                        C.wait(nc.sync, "dv")
                        C.inc(nc.sync.dma_start(
                            x_out[row:row + 128, cc * 512:(cc + 1) * 512], osl[:]),
                            "sx", 16)
                        C.wait(nc.sync, "sx")

        C.wait(nc.sync, "sx")



PHASES_ALL = ["l0f_a", "l0a", "l0f_b", "l1f_a", "l1a", "l1f_b"]


def build_program(phases=None):
    phases = phases or PHASES_ALL
    from contextlib import ExitStack
    es = ExitStack()
    nc = bass.Bass("TRN2", target_bir_lowering=False)

    x_ext = nc.dram_tensor("x", [NTOK, D], F32, kind="ExternalInput").ap()
    out = nc.dram_tensor("out", [NTOK, D], F32, kind="ExternalOutput").ap()

    need = set(phases)
    inputs_used = ["x", "c_ident_bf"]

    ident_bf = nc.dram_tensor("c_ident_bf", [128, 128], BF16, kind="ExternalInput").ap()

    if need & {"l1f_a", "l1f_b"}:
        if SKIP_PREP:
            wgT = nc.dram_tensor("l1f_wgTb", [64 * 128, D], BF16, kind="ExternalInput").ap()
            wuT = nc.dram_tensor("l1f_wuTb", [64 * 128, D], BF16, kind="ExternalInput").ap()
            wdT = nc.dram_tensor("l1f_wdTb", [8192, D], BF16, kind="ExternalInput").ap()
            inputs_used += ["l1f_wgTb", "l1f_wuTb", "l1f_wdTb"]
        else:
            wgT_f = nc.dram_tensor("l1f_wgT", [64 * 128, D], F32, kind="ExternalInput").ap()
            wuT_f = nc.dram_tensor("l1f_wuT", [64 * 128, D], F32, kind="ExternalInput").ap()
            wdT_f = nc.dram_tensor("l1f_wdT", [8192, D], F32, kind="ExternalInput").ap()
            wgT = nc.dram_tensor("s_wgT", [64 * 128, D], BF16, kind="Internal").ap()
            wuT = nc.dram_tensor("s_wuT", [64 * 128, D], BF16, kind="Internal").ap()
            wdT = nc.dram_tensor("s_wdT", [8192, D], BF16, kind="Internal").ap()
            inputs_used += ["l1f_wgT", "l1f_wuT", "l1f_wdT"]

    if need & {"l0f_a", "l0f_b"}:
        wg8p = nc.dram_tensor("l0f_wg8p", [128, 16, 8], F32, kind="ExternalInput").ap()
        oh8p = nc.dram_tensor("c_oh8", [8, 8 * 128], BF16, kind="ExternalInput").ap()
        wi0_f = nc.dram_tensor("l0f_wi0", [8, 8, 128, D], F32, kind="ExternalInput").ap()
        wi1_f = nc.dram_tensor("l0f_wi1", [8, 8, 128, D], F32, kind="ExternalInput").ap()
        wo_f = nc.dram_tensor("l0f_wo", [8, 1024, D], F32, kind="ExternalInput").ap()
        wi0s = nc.dram_tensor("s_wi0", [8, 8, 128, D], BF16, kind="Internal").ap()
        wi1s = nc.dram_tensor("s_wi1", [8, 8, 128, D], BF16, kind="Internal").ap()
        wos = nc.dram_tensor("s_wo", [8, 1024, D], BF16, kind="Internal").ap()
        inputs_used += ["l0f_wg8p", "c_oh8", "l0f_wi0", "l0f_wi1", "l0f_wo"]

    if "l0a" in need:
        l0a_W = {
            "winp": nc.dram_tensor("l0a_winp", [128, 16, 8], F32, kind="ExternalInput").ap(),
            "wdnp": nc.dram_tensor("l0a_wdnp", [128, 16, 32], F32, kind="ExternalInput").ap(),
            "gbp": nc.dram_tensor("l0a_gbp", [2, 128, 16], F32, kind="ExternalInput").ap(),
            "wout": nc.dram_tensor("l0a_wout", [8, D], F32, kind="ExternalInput").ap(),
            "wup": nc.dram_tensor("l0a_wup", [32, D], F32, kind="ExternalInput").ap(),
            "dlg": nc.dram_tensor("l0a_dlg", [8, 1], F32, kind="ExternalInput").ap(),
            "bin": nc.dram_tensor("l0a_bin", [8, 1], F32, kind="ExternalInput").ap(),
            "bdn": nc.dram_tensor("l0a_bdn", [32, 1], F32, kind="ExternalInput").ap(),
            "rows": nc.dram_tensor("l0a_rows", [5, D], F32, kind="ExternalInput").ap(),
            "ones": nc.dram_tensor("c_ones_bf", [1, 128], BF16, kind="ExternalInput").ap(),
        }
        inputs_used += ["l0a_winp", "l0a_wdnp", "l0a_gbp", "l0a_wout", "l0a_wup",
                        "l0a_dlg", "l0a_bin", "l0a_bdn", "l0a_rows", "c_ones_bf"]

    if "l1a" in need:
        l1a_W = {
            "wq": nc.dram_tensor("l1a_wq", [D, D], F32, kind="ExternalInput").ap(),
            "wk": nc.dram_tensor("l1a_wk", [D, D], F32, kind="ExternalInput").ap(),
            "wv": nc.dram_tensor("l1a_wv", [D, D], F32, kind="ExternalInput").ap(),
            "wo": nc.dram_tensor("l1a_wo", [D, D], F32, kind="ExternalInput").ap(),
            "bqp": nc.dram_tensor("l1a_bqp", [128, 16], F32, kind="ExternalInput").ap(),
            "bkp": nc.dram_tensor("l1a_bkp", [128, 16], F32, kind="ExternalInput").ap(),
            "bv_row": nc.dram_tensor("l1a_bv", [1, D], F32, kind="ExternalInput").ap(),
            "bo_row": nc.dram_tensor("l1a_bo", [1, D], F32, kind="ExternalInput").ap(),
            "ones": nc.dram_tensor("c_ones1a", [1, 128], BF16, kind="ExternalInput").ap(),
        }
        feats_scr = nc.dram_tensor("s_feats", [NTOK, D], BF16,
                                   kind="ExternalOutput" if DEBUG_FEATS else "Internal").ap()
        if DEBUG_L1A:
            pass
        inputs_used += ["l1a_wq", "l1a_wk", "l1a_wv", "l1a_wo", "l1a_bqp",
                        "l1a_bkp", "l1a_bv", "l1a_bo", "c_ones1a"]

    xA = nc.dram_tensor("s_xA", [NTOK, D], F32, kind="Internal").ap()
    xB = nc.dram_tensor("s_xB", [NTOK, D], F32, kind="Internal").ap()

    with es:
        C = Ctx(nc, es)
        C.idn_bf = es.enter_context(nc.sbuf_tensor("c_idn", [128, 128], BF16))
        C.inc(nc.sync.dma_start(C.idn_bf[:], ident_bf[:]), "lx", 16)
        if need & {"l0f_a", "l0f_b"}:
            C.oh8_bf = es.enter_context(nc.sbuf_tensor("c_oh8s", [8, 8 * 128], BF16))
            C.wg8_bf = es.enter_context(nc.sbuf_tensor("c_wg8s", [128, 128], BF16))
            C.inc(nc.sync.dma_start(C.oh8_bf[:], oh8p[:]), "lx", 16)
            C.inc(nc.gpsimd.dma_start(
                C.wg8_bf[:], wg8p[:].rearrange("p a b -> p (a b)")), "lw", 16)
            C.wait(nc.tensor, "lw")
        C.wait(nc.tensor, "lx")
        C.barrier()

        jobs = []
        if need & {"l1f_a", "l1f_b"} and not SKIP_PREP:
            jobs += [(wgT_f, wgT), (wuT_f, wuT), (wdT_f, wdT)]
        if need & {"l0f_a", "l0f_b"}:
            jobs += [(wi0_f.rearrange("a b p d -> (a b p) d"),
                      wi0s.rearrange("a b p d -> (a b p) d")),
                     (wi1_f.rearrange("a b p d -> (a b p) d"),
                      wi1s.rearrange("a b p d -> (a b p) d")),
                     (wo_f.rearrange("a m d -> (a m) d"),
                      wos.rearrange("a m d -> (a m) d"))]
        if DUMMY_PREP:
            dmy = nc.dram_tensor("s_dmy", [NTOK, D], BF16, kind="Internal").ap()
            jobs += [(x_ext, dmy)] * 3
        if jobs:
            emit_prep(C, jobs)
            C.barrier()

        cur = x_ext
        nxt = [xA, xB]
        ni = 0
        for ph in phases:
            dst = out if ph == phases[-1] else nxt[ni]
            if ph in ("l1f_a", "l1f_b"):
                emit_l1ffn(C, cur, dst, wgT, wuT, wdT,
                           wsem="lw" if ph == "l1f_a" else "g2")
            elif ph in ("l0f_a", "l0f_b"):
                emit_moe(C, cur, dst, None, wi0s, wi1s, wos)
            elif ph == "l0a":
                emit_l0a(C, cur, dst, l0a_W)
            elif ph == "l1a":
                emit_l1a(C, cur, dst, l1a_W, feats_scr)
            else:
                raise NotImplementedError(ph)
            C.barrier()
            cur = dst
            ni ^= 1

    return nc, inputs_used



def swizzle_gu(w):
    wT = w.T.reshape(16, 128, 64, 128)
    return np.ascontiguousarray(wT.transpose(2, 1, 0, 3).reshape(64 * 128, 2048))


def swizzle_wi(w):
    w5 = w.reshape(8, 16, 128, 8, 128)
    return np.ascontiguousarray(
        w5.transpose(0, 3, 2, 1, 4).reshape(8, 8, 128, 2048))


def make_in_maps(inputs, inputs_used, phases=None):
    x_full = np.asarray(inputs["L0_attn_x"], dtype=np.float32)
    base = {}
    base["c_ident_bf"] = np.eye(128, dtype=BF)
    if "l1f_wgT" in inputs_used:
        base["l1f_wgT"] = swizzle_gu(np.asarray(inputs["L1_ffn_w_gate"], np.float32))
        base["l1f_wuT"] = swizzle_gu(np.asarray(inputs["L1_ffn_w_up"], np.float32))
        base["l1f_wdT"] = np.ascontiguousarray(
            np.asarray(inputs["L1_ffn_w_down"], np.float32).T)
    if "l0a_winp" in inputs_used:
        pk = lambda v, w: np.ascontiguousarray(
            np.asarray(v, np.float32).reshape(16, 128, w).transpose(1, 0, 2))
        base["l0a_winp"] = pk(inputs["L0_attn_W_in"], 8)
        base["l0a_wdnp"] = pk(inputs["L0_attn_W_down"], 32)
        gm = np.asarray(inputs["L0_attn_ln_gamma"], np.float32)
        bt = np.asarray(inputs["L0_attn_ln_beta"], np.float32)
        base["l0a_gbp"] = np.ascontiguousarray(np.stack(
            [gm.reshape(16, 128).T, bt.reshape(16, 128).T]))
        base["l0a_wout"] = np.ascontiguousarray(np.asarray(inputs["L0_attn_W_out"], np.float32))
        base["l0a_wup"] = np.ascontiguousarray(np.asarray(inputs["L0_attn_W_up"], np.float32))
        base["l0a_dlg"] = np.asarray(inputs["L0_attn_decay_logit"], np.float32).reshape(8, 1)
        base["l0a_bin"] = np.asarray(inputs["L0_attn_b_in"], np.float32).reshape(8, 1)
        base["l0a_bdn"] = np.asarray(inputs["L0_attn_b_down"], np.float32).reshape(32, 1)
        base["l0a_rows"] = np.ascontiguousarray(np.stack([
            np.asarray(inputs["L0_attn_b_out"], np.float32),
            np.asarray(inputs["L0_attn_D_skip"], np.float32),
            np.asarray(inputs["L0_attn_b_up"], np.float32),
            gm, bt]))
        base["c_ones_bf"] = np.ones((1, 128), dtype=BF)
    if "l1a_wq" in inputs_used:
        base["l1a_wq"] = np.ascontiguousarray(np.asarray(inputs["L1_attn_Wq"], np.float32))
        base["l1a_wk"] = np.ascontiguousarray(np.asarray(inputs["L1_attn_Wk"], np.float32))
        base["l1a_wv"] = np.ascontiguousarray(np.asarray(inputs["L1_attn_Wv"], np.float32))
        base["l1a_wo"] = np.ascontiguousarray(np.asarray(inputs["L1_attn_Wo"], np.float32))
        pk16 = lambda v: np.ascontiguousarray(
            np.asarray(v, np.float32).reshape(16, 128).T)
        base["l1a_bqp"] = pk16(inputs["L1_attn_bq"])
        base["l1a_bkp"] = pk16(inputs["L1_attn_bk"])
        base["l1a_bv"] = np.asarray(inputs["L1_attn_bv"], np.float32).reshape(1, D)
        base["l1a_bo"] = np.asarray(inputs["L1_attn_bo"], np.float32).reshape(1, D)
        base["c_ones1a"] = np.ones((1, 128), dtype=BF)
    if "l0f_wi0" in inputs_used:
        wg = np.asarray(inputs["L0_ffn_w_gate"], np.float32)
        base["l0f_wg8p"] = np.ascontiguousarray(
            wg.reshape(16, 128, 8).transpose(1, 0, 2))
        oh = np.zeros((8, 8, 128), dtype=BF)
        for e in range(8):
            oh[e, e, :] = 1
        base["c_oh8"] = np.ascontiguousarray(oh.transpose(1, 0, 2).reshape(8, 8 * 128))
        base["l0f_wi0"] = swizzle_wi(np.asarray(inputs["L0_ffn_wi_0"], np.float32))
        base["l0f_wi1"] = swizzle_wi(np.asarray(inputs["L0_ffn_wi_1"], np.float32))
        base["l0f_wo"] = np.ascontiguousarray(np.asarray(inputs["L0_ffn_wo"], np.float32))
    in_maps = []
    for c in range(8):
        m = dict(base)
        m["x"] = np.ascontiguousarray(
            x_full[c * B_CORE:(c + 1) * B_CORE].reshape(NTOK, D))
        in_maps.append(m)
    return in_maps


def kernel(**inputs):
    nc, inputs_used = build_program()
    in_maps = make_in_maps(inputs, inputs_used)
    res = run_bass_kernel_spmd(nc, in_maps, core_ids=list(range(8)))
    outs = [r["out"].reshape(B_CORE, S, D) for r in res.results]
    return np.concatenate(outs, axis=0).astype(np.float32)


# version 02
# speedup vs baseline: 1.2992x; 1.2992x over previous
import numpy as np
import ml_dtypes
import concourse.bass as bass
import concourse.mybir as mybir
from concourse.bass_utils import run_bass_kernel_spmd

F32 = mybir.dt.float32
BF16 = mybir.dt.bfloat16
I32 = mybir.dt.int32
AF = mybir.ActivationFunctionType
ALU = mybir.AluOpType
AX = mybir.AxisListType
ET = mybir.EngineType

BF = ml_dtypes.bfloat16

D = 2048
S = 2048
B_CORE = 2
NTOK = B_CORE * S
EPS = 1e-5
RSQRT_MAGIC = 0x5f3759df

ALL_ENG = [ET.SP, ET.Pool, ET.PE, ET.DVE, ET.Activation]
L1F_NSUP = 8
USE_FORI = True
SKIP_PREP = False
DUMMY_PREP = False
DIAG_SERIAL_DMA = True
DEBUG_FEATS = False
DEBUG_L1A = False


class _PyLoop:
    def __init__(self, n):
        self.n = n

    def __enter__(self):
        return 0

    def __exit__(self, *a):
        return False


def loop_ctx(nc, n):
    if USE_FORI:
        return nc.Fori(0, n, engines=ALL_ENG)
    return _PyLoop(n)



class Ctx:
    def __init__(self, nc, es):
        self.nc = nc
        self.es = es
        self.sems = {}
        self.n = {}
        for name in ["lx", "lw", "sx", "pe", "dv", "ac", "g2",
                     "lwA", "lwB", "lwC", "lwD"]:
            self.sems[name] = nc.alloc_semaphore(name)
            self.n[name] = 0
        self._uq = 0

    def uniq(self):
        self._uq += 1
        return f"u{self._uq}_"

    def inc(self, instr, name, k=1):
        instr.then_inc(self.sems[name], k)
        self.n[name] += k
        return instr

    def linc(self, instr, name, k=1):
        instr.then_inc(self.sems[name], k)
        return instr

    def bump(self, name, k):
        self.n[name] += k

    def wait(self, eng, name, val=None):
        eng.wait_ge(self.sems[name], self.n[name] if val is None else val)

    def barrier(self):
        self.nc.all_engine_barrier()


class Site:

    def __init__(self, C, eng, sem_name, init_off, delta, name):
        self.eng = eng
        self.sem = C.sems[sem_name]
        self.delta = delta
        self.reg = eng.alloc_register(name)
        eng.reg_mov(self.reg, C.n[sem_name] + init_off)

    def wait(self):
        self.eng.wait_ge(self.sem, self.reg)
        if self.delta:
            self.eng.reg_add(self.reg, self.reg, self.delta)

    def adjust(self, k):
        if k:
            self.eng.reg_add(self.reg, self.reg, k)

    def free(self):
        self.eng.free_register(self.reg)


class Off:

    def __init__(self, eng, init, delta, name):
        self.eng = eng
        self.delta = delta
        self.reg = eng.alloc_register(name)
        eng.reg_mov(self.reg, init)

    def next(self):
        self.eng.reg_add(self.reg, self.reg, self.delta)

    def free(self):
        self.eng.free_register(self.reg)


def rap(tensor_ap, reg, pairs):
    t = tensor_ap.tensor if isinstance(tensor_ap, bass.AP) else tensor_ap
    return bass.AP(t, reg, pairs)


def newton_rsqrt(C, v_ap, out_ap, ia_ap, ib_ap, tmp_ap, tmp2_ap, iters=3):
    v = C.nc.vector
    v.tensor_single_scalar(ia_ap, v_ap.bitcast(I32), 1, ALU.arith_shift_right)
    v.drain()
    v.tensor_scalar(ib_ap, ia_ap, -1, RSQRT_MAGIC, ALU.mult, ALU.add)
    v.drain()
    cur, nxt = ib_ap, ia_ap
    for _ in range(iters):
        r = cur.bitcast(F32)
        v.tensor_mul(tmp_ap, r, r)
        v.drain()
        v.tensor_mul(tmp2_ap, tmp_ap, v_ap)
        v.drain()
        v.tensor_scalar(tmp_ap, tmp2_ap, -0.5, 1.5, ALU.mult, ALU.add)
        v.drain()
        v.tensor_mul(nxt.bitcast(F32), r, tmp_ap)
        v.drain()
        cur, nxt = nxt, cur
    v.tensor_copy(out_ap, cur.bitcast(F32))
    v.drain()



def emit_prep(C, jobs):
    nc = C.nc
    from contextlib import ExitStack
    with ExitStack() as ps:
        buf = [ps.enter_context(nc.sbuf_tensor(f"prep{i}", [128, 4096], BF16))
               for i in range(2)]
        k = 0
        for src, dst in jobs:
            R, Cc = src.shape
            assert R % 128 == 0, (R, Cc)
            for r0 in range(0, R, 128):
                for c0 in range(0, Cc, 4096):
                    w = min(4096, Cc - c0)
                    b = buf[k % 2]
                    if k >= 2:
                        C.wait(nc.gpsimd, "sx", C.n["sx"])
                    C.inc(nc.gpsimd.dma_start(b[:, 0:w], src[r0:r0 + 128, c0:c0 + w]),
                          "lw", 16)
                    C.wait(nc.sync, "lw")
                    C.inc(nc.sync.dma_start(dst[r0:r0 + 128, c0:c0 + w], b[:, 0:w]),
                          "sx", 16)
                    k += 1
        C.wait(nc.sync, "sx")



def emit_l1ffn(C, x_in, x_out, wgT, wuT, wdT, wsem="lw"):
    nc = C.nc
    uq = C.uniq()
    from contextlib import ExitStack
    with ExitStack() as ps:
        sb = lambda n, sh, dt: ps.enter_context(nc.sbuf_tensor(uq + n, sh, dt))
        xt = sb("f_xt", [128, 4 * D], F32)
        yT = sb("f_yT", [128, 16 * 512], BF16)
        gup = sb("f_gup", [128, 64 * 512], BF16)
        wg_b = [sb(f"f_wg{i}", [128, 16 * 128], BF16) for i in range(2)]
        wu_b = [sb(f"f_wu{i}", [128, 16 * 128], BF16) for i in range(2)]
        wd_b = [sb(f"f_wd{i}", [128, 512], BF16) for i in range(8)]
        sq = sb("f_sq", [128, D], F32)
        ybf4 = sb("f_ybf4", [128, 4 * D], BF16)
        gs = [sb(f"f_gs{i}", [128, 512], BF16) for i in range(2)]
        st4 = sb("f_st4", [128, 16], F32)
        sti = sb("f_sti", [128, 4], I32)
        sti2 = sb("f_sti2", [128, 4], I32)
        stt1 = sb("f_stt1", [128, 4], F32)
        stt2 = sb("f_stt2", [128, 4], F32)

        tp = ps.enter_context(nc.psum_tensor(uq + "f_tp", [128, 4096], BF16))
        p = [ps.enter_context(nc.psum_tensor(f"{uq}f_p{i}", [128, 512], F32))
             for i in range(4)]

        mu, var, rstd, nmr = (st4[:, 0:4], st4[:, 4:8], st4[:, 8:12], st4[:, 12:16])
        idn = C.idn_bf
        yT3 = yT[:].rearrange("p (k s) -> p k s", k=16)

        NSUP = L1F_NSUP
        DV, AC, PE_, LX, LW, SX = 72, 66, 386, 16, 6144, 16

        S1 = Site(C, nc.sync, "sx", 0, SX, uq + "fS1")
        S2 = Site(C, nc.sync, "dv", DV, DV, uq + "fS2")
        G1 = Site(C, nc.gpsimd, "pe", 0, 2, uq + "fG1")
        G2 = Site(C, nc.gpsimd, "pe", 129, 1, uq + "fG2")
        P1b = Site(C, nc.tensor, "dv", 0, DV, uq + "fP1b")
        P1 = Site(C, nc.tensor, "ac", 2, AC, uq + "fP1")
        P2 = Site(C, nc.tensor, "dv", 3, DV, uq + "fP2")
        P3a = Site(C, nc.tensor, "lwA", 32, 32, uq + "fP3a")
        P3b2 = Site(C, nc.tensor, "lwB", 32, 32, uq + "fP3b2")
        P3b = Site(C, nc.tensor, "dv", 4, DV, uq + "fP3b")
        P3c = Site(C, nc.tensor, "ac", 1, 1, uq + "fP3c")
        P3d = Site(C, nc.tensor, "dv", 3, 1, uq + "fP3d")
        P4a = Site(C, nc.tensor, "ac", AC, AC, uq + "fP4a")
        P4b = Site(C, nc.tensor, "dv", 68, DV, uq + "fP4b")
        P5a = Site(C, nc.tensor, "lwC", 16, 16, uq + "fP5a")
        P5b = Site(C, nc.tensor, "lwD", 16, 16, uq + "fP5b")
        P6 = Site(C, nc.tensor, "dv", 69, 1, uq + "fP6")
        V1 = Site(C, nc.vector, "lx", 16, LX, uq + "fV1")
        V2 = Site(C, nc.vector, "ac", 1, AC, uq + "fV2")
        V3 = Site(C, nc.vector, "pe", 1, PE_, uq + "fV3")
        V4 = Site(C, nc.vector, "pe", 2, PE_, uq + "fV4")
        V5a = Site(C, nc.vector, "pe", 4, 2, uq + "fV5a")
        V5b = Site(C, nc.vector, "ac", 3, 1, uq + "fV5b")
        V6 = Site(C, nc.vector, "sx", 0, SX, uq + "fV6")
        V7 = Site(C, nc.vector, "pe", 194, 64, uq + "fV7")
        A1 = Site(C, nc.scalar, "dv", 1, DV, uq + "fA1")
        A2 = Site(C, nc.scalar, "dv", 2, DV, uq + "fA2")
        A3 = Site(C, nc.scalar, "pe", 3, 2, uq + "fA3")
        A4 = Site(C, nc.scalar, "dv", 3, 1, uq + "fA4")
        sites = [S1, S2, G1, G2, P1b, P1, P2, P3a, P3b2, P3b, P3c, P3d, P4a, P4b, P5a, P5b, P6,
                 V1, V2, V3, V4, V5a, V5b, V6, V7, A1, A2, A3, A4]

        ox_ld = Off(nc.sync, 0, 512 * D, uq + "fox_ld")
        ox_st = Off(nc.sync, 0, 512 * D, uq + "fox_st")

        for _rep in range(1 if USE_FORI else NSUP):
         if _rep == 0:
          ctxmgr = loop_ctx(nc, NSUP)
         else:
          ctxmgr = _PyLoop(NSUP)
         with ctxmgr:
            S1.wait()
            C.linc(nc.sync.dma_start(
                xt[:].rearrange("p (t j) -> p t j", t=4),
                rap(x_in, ox_ld.reg, [[D, 128], [128 * D, 4], [1, D]])),
                "lx", 16)
            ox_ld.next()
            S2.wait()
            C.linc(nc.sync.dma_start(
                rap(x_out, ox_st.reg, [[D, 128], [128 * D, 4], [1, D]]),
                xt[:].rearrange("p (t j) -> p t j", t=4)),
                "sx", 16)
            ox_st.next()

            for hc in range(64):
                G1.wait()
                psem = "lwA" if hc % 2 == 0 else "lwB"
                C.linc(nc.gpsimd.dma_start(
                    wg_b[hc % 2][:],
                    wgT[hc * 128:(hc + 1) * 128, :]), psem, 16)
                C.linc(nc.gpsimd.dma_start(
                    wu_b[hc % 2][:],
                    wuT[hc * 128:(hc + 1) * 128, :]), psem, 16)
            G1.adjust(PE_ - 128)
            for dc in range(4):
                for hc in range(64):
                    G2.wait()
                    C.linc(nc.gpsimd.dma_start(
                        wd_b[(dc * 64 + hc) % 8][:],
                        bass.AP(wdT.tensor, hc * 128 * D + dc * 512,
                                [[D, 128], [1, 512]])),
                           "lwC" if (dc * 64 + hc) % 2 == 0 else "lwD", 16)
            G2.adjust(130)

            V1.wait()
            for t in range(4):
                nc.vector.reduce_sum(mu[:, t:t + 1], xt[:, t * D:(t + 1) * D],
                                     axis=AX.X, negate=True)
            nc.vector.drain()
            C.linc(nc.vector.tensor_scalar_mul(mu, mu, 1.0 / D), "dv", 1)
            V2.wait()
            nc.vector.tensor_scalar(var, var, 1.0 / D, EPS, ALU.mult, ALU.add)
            nc.vector.drain()
            newton_rsqrt(C, var, rstd, sti[:], sti2[:], stt1[:], stt2[:])
            C.linc(nc.vector.tensor_mul(nmr, mu, rstd), "dv", 1)
            for g, site in ((0, V3), (1, V4)):
                site.wait()
                instr = None
                for t in (2 * g, 2 * g + 1):
                    instr = nc.vector.tensor_copy(
                        yT3[:, :, t * 128:(t + 1) * 128],
                        tp[:, (t % 2) * 2048:(t % 2) * 2048 + 2048]
                        .rearrange("p (k m) -> p k m", k=16))
                C.linc(instr, "dv", 1)
            for hc in range(64):
                V5a.wait()
                V5b.wait()
                C.linc(nc.vector.tensor_tensor(
                    gup[:, hc * 512:(hc + 1) * 512], p[2 + hc % 2][:],
                    gs[hc % 2][:], ALU.mult), "dv", 1)
            V5a.adjust(PE_ - 128)
            V5b.adjust(AC - 64)
            V6.wait()
            for dc in range(4):
                V7.wait()
                instr = None
                for t in range(4):
                    sl = slice(t * D + dc * 512, t * D + (dc + 1) * 512)
                    instr = nc.vector.scalar_tensor_tensor(
                        xt[:, sl], p[t][:], 0.5, xt[:, sl], ALU.mult, ALU.add)
                C.linc(instr, "dv", 1)
            V7.adjust(130)

            A1.wait()
            instr = None
            for t in range(4):
                instr = nc.scalar.activation(sq[:], xt[:, t * D:(t + 1) * D],
                                             AF.Square, bias=mu[:, t:t + 1],
                                             accum_out=var[:, t:t + 1])
            C.linc(instr, "ac", 1)
            A2.wait()
            instr = None
            for t in range(4):
                instr = nc.scalar.activation(ybf4[:, t * D:(t + 1) * D],
                                             xt[:, t * D:(t + 1) * D],
                                             AF.Identity, scale=rstd[:, t:t + 1],
                                             bias=nmr[:, t:t + 1])
            C.linc(instr, "ac", 1)
            for hc in range(64):
                A3.wait()
                A4.wait()
                C.linc(nc.scalar.activation(gs[hc % 2][:], p[hc % 2][:], AF.Silu),
                       "ac", 1)
            A3.adjust(PE_ - 128)
            A4.adjust(DV - 64)

            P1b.wait()
            P1.wait()
            for g in range(2):
                if g == 1:
                    P2.wait()
                instr = None
                for t in (2 * g, 2 * g + 1):
                    for k in range(16):
                        instr = nc.tensor.transpose(
                            tp[:, (t % 2) * 2048 + k * 128:(t % 2) * 2048 + (k + 1) * 128],
                            ybf4[:, t * D + k * 128:t * D + (k + 1) * 128], idn[:])
                C.linc(instr, "pe", 1)
            P3b.wait()
            for hc in range(64):
                (P3a if hc % 2 == 0 else P3b2).wait()
                P3c.wait()
                P3d.wait()
                instr = None
                for k in range(16):
                    instr = nc.tensor.matmul(p[hc % 2][:],
                                             wg_b[hc % 2][:, k * 128:(k + 1) * 128],
                                             yT[:, k * 512:(k + 1) * 512],
                                             start=(k == 0), stop=(k == 15))
                C.linc(instr, "pe", 1)
                instr = None
                for k in range(16):
                    instr = nc.tensor.matmul(p[2 + hc % 2][:],
                                             wu_b[hc % 2][:, k * 128:(k + 1) * 128],
                                             yT[:, k * 512:(k + 1) * 512],
                                             start=(k == 0), stop=(k == 15))
                C.linc(instr, "pe", 1)
            P3c.adjust(AC - 64)
            P3d.adjust(DV - 64)
            P4a.wait()
            P4b.wait()
            for dc in range(4):
                if dc > 0:
                    P6.wait()
                for hc in range(64):
                    (P5a if (dc * 64 + hc) % 2 == 0 else P5b).wait()
                    instr = None
                    for t in range(4):
                        instr = nc.tensor.matmul(
                            p[t][:],
                            gup[:, hc * 512 + t * 128: hc * 512 + (t + 1) * 128],
                            wd_b[(dc * 64 + hc) % 8][:],
                            start=(hc == 0), stop=(hc == 63))
                    C.linc(instr, "pe", 1)
            P6.adjust(DV - 3)

        for name, tot in (("lx", LX), ("lwA", 1024), ("lwB", 1024),
                          ("lwC", 2048), ("lwD", 2048), ("sx", SX), ("pe", PE_),
                          ("dv", DV), ("ac", AC)):
            C.bump(name, tot * NSUP)

        C.wait(nc.sync, "sx")
        for s_ in sites:
            s_.free()
        ox_ld.free()
        ox_st.free()



def emit_moe(C, x_in, x_out, wg8, wi0s, wi1s, wos):
    nc = C.nc
    uq = C.uniq()
    from contextlib import ExitStack
    with ExitStack() as ps:
        sb = lambda n, sh, dt: ps.enter_context(nc.sbuf_tensor(uq + n, sh, dt))
        xt = sb("m_xt", [128, 4 * D], F32)
        yT = sb("m_yT", [128, 16 * 512], BF16)
        inter = [sb(f"m_it{e}", [128, 8 * 512], BF16) for e in range(8)]
        wi0b = [sb(f"m_w0{i}", [128, 2048], BF16) for i in range(2)]
        wi1b = [sb(f"m_w1{i}", [128, 2048], BF16) for i in range(2)]
        wob = [sb(f"m_wo{i}", [128, 8 * 512], BF16) for i in range(2)]
        sq = sb("m_sq", [128, D], F32)
        ybf4 = sb("m_ybf4", [128, 4 * D], BF16)
        ss = [sb(f"m_ss{i}", [128, 512], BF16) for i in range(2)]
        itmp = sb("m_itmp", [128, 512], F32)
        st4 = sb("m_st4", [128, 16], F32)
        sti = sb("m_sti", [128, 4], I32)
        sti2 = sb("m_sti2", [128, 4], I32)
        stt1 = sb("m_stt1", [128, 4], F32)
        stt2 = sb("m_stt2", [128, 4], F32)
        m1p4 = sb("m_m1p", [128, 4], F32)
        m1n4 = sb("m_m1n", [128, 4], F32)
        m2p4 = sb("m_m2p", [128, 4], F32)
        d214 = sb("m_d21", [128, 4], F32)
        den4 = sb("m_den", [128, 4], F32)
        rec4 = sb("m_rec", [128, 4], F32)
        msk = sb("m_msk", [128, 32], F32)
        lgm = sb("m_lgm", [128, 32], F32)
        e1s = sb("m_e1", [128, 32], F32)
        ge2 = sb("m_ge2", [128, 32], F32)
        wtmp = sb("m_wtmp", [128, 8], F32)
        wtk = sb("m_wtk", [128, 32], F32)
        wbf = sb("m_wbf", [128, 32], BF16)
        wT = sb("m_wT", [8, 512], BF16)
        wb8 = sb("m_wb8", [128, 8 * 512], F32)

        tp = ps.enter_context(nc.psum_tensor(uq + "m_tp", [128, 4096], BF16))
        p = [ps.enter_context(nc.psum_tensor(f"{uq}m_p{i}", [128, 512], F32))
             for i in range(4)]

        mu, var, rstd, nmr = (st4[:, 0:4], st4[:, 4:8], st4[:, 8:12], st4[:, 12:16])
        idn = C.idn_bf
        oh8 = C.oh8_bf
        yT3 = yT[:].rearrange("p (k s) -> p k s", k=16)

        NSUP = 8
        DV, AC, PE_, LX, SX = 92, 70, 175, 16, 16
        LAB = 32 * 32
        LCD = 16 * 16

        S1 = Site(C, nc.sync, "sx", 0, SX, uq + "mS1")
        S2 = Site(C, nc.sync, "dv", DV, DV, uq + "mS2")
        G1a = Site(C, nc.gpsimd, "pe", 13, 2, uq + "mG1a")
        G2a = Site(C, nc.gpsimd, "pe", 142, 1, uq + "mG2a")
        P1b = Site(C, nc.tensor, "dv", 0, DV, uq + "mP1b")
        P1 = Site(C, nc.tensor, "ac", 2, AC, uq + "mP1")
        P2 = Site(C, nc.tensor, "dv", 3, DV, uq + "mP2")
        PB1 = Site(C, nc.tensor, "dv", 4, DV, uq + "mPB1")
        PB2 = [Site(C, nc.tensor, "dv", v, DV, f"{uq}mPB2_{t}")
               for t, v in enumerate((9, 13, 14, 15))]
        PB4 = Site(C, nc.tensor, "dv", 16, DV, uq + "mPB4")
        PB5 = Site(C, nc.tensor, "dv", 17, 1, uq + "mPB5")
        PCa = Site(C, nc.tensor, "lwA", 32, 32, uq + "mPCa")
        PCb = Site(C, nc.tensor, "lwB", 32, 32, uq + "mPCb")
        PC0a = Site(C, nc.tensor, "ac", 6, AC, uq + "mPC0a")
        PC0b = Site(C, nc.tensor, "dv", 24, DV, uq + "mPC0b")
        PC2 = Site(C, nc.tensor, "ac", 5, 1, uq + "mPC2")
        PC3 = Site(C, nc.tensor, "dv", 23, 1, uq + "mPC3")
        PDa = Site(C, nc.tensor, "lwC", 16, 16, uq + "mPDa")
        PDb = Site(C, nc.tensor, "lwD", 16, 16, uq + "mPDb")
        PD2 = Site(C, nc.tensor, "dv", 89, 1, uq + "mPD2")
        PD3a = Site(C, nc.tensor, "ac", 70, AC, uq + "mPD3a")
        PD3b = Site(C, nc.tensor, "dv", 88, DV, uq + "mPD3b")
        V1 = Site(C, nc.vector, "lx", 16, LX, uq + "mV1")
        V2 = Site(C, nc.vector, "ac", 1, AC, uq + "mV2")
        V3 = Site(C, nc.vector, "pe", 1, PE_, uq + "mV3")
        V4 = Site(C, nc.vector, "pe", 2, PE_, uq + "mV4")
        VB1 = Site(C, nc.vector, "pe", 3, PE_, uq + "mVB1")
        VB2 = [Site(C, nc.vector, "ac", 3 + t, AC, f"{uq}mVB2_{t}") for t in range(4)]
        VB3 = [Site(C, nc.vector, "pe", 4 + t, PE_, f"{uq}mVB3_{t}") for t in range(4)]
        VB4 = Site(C, nc.vector, "pe", 8, 1, uq + "mVB4")
        VC1a = Site(C, nc.vector, "pe", 17, 2, uq + "mVC1a")
        VC1b = Site(C, nc.vector, "ac", 7, 1, uq + "mVC1b")
        VD1 = Site(C, nc.vector, "pe", 151, 8, uq + "mVD1")
        V6 = Site(C, nc.vector, "sx", 0, SX, uq + "mV6")
        A1 = Site(C, nc.scalar, "dv", 1, DV, uq + "mA1")
        A2 = Site(C, nc.scalar, "dv", 2, DV, uq + "mA2")
        AB1 = [Site(C, nc.scalar, "dv", 5 + t, DV, f"{uq}mAB1_{t}") for t in range(4)]
        AC1 = Site(C, nc.scalar, "pe", 16, 2, uq + "mAC1")
        AC2 = Site(C, nc.scalar, "dv", 23, 1, uq + "mAC2")
        sites = ([S1, S2, G1a, G2a, P1b, P1, P2, PB1, PB4, PB5, PCa, PCb,
                  PC0a, PC0b, PC2, PC3, PDa, PDb, PD2, PD3a, PD3b, V1, V2, V3,
                  V4, VB1, VB4, VC1a, VC1b, VD1, V6, A1, A2, AC1, AC2]
                 + PB2 + VB2 + VB3 + AB1)

        ox_ld = Off(nc.sync, 0, 512 * D, uq + "mox_ld")
        ox_st = Off(nc.sync, 0, 512 * D, uq + "mox_st")

        with nc.Fori(0, NSUP, engines=ALL_ENG):
            S1.wait()
            C.linc(nc.sync.dma_start(
                xt[:].rearrange("p (t j) -> p t j", t=4),
                rap(x_in, ox_ld.reg, [[D, 128], [128 * D, 4], [1, D]])), "lx", 16)
            ox_ld.next()
            S2.wait()
            C.linc(nc.sync.dma_start(
                rap(x_out, ox_st.reg, [[D, 128], [128 * D, 4], [1, D]]),
                xt[:].rearrange("p (t j) -> p t j", t=4)), "sx", 16)
            ox_st.next()

            for j in range(64):
                e, mc = j // 8, j % 8
                G1a.wait()
                sem = "lwA" if j % 2 == 0 else "lwB"
                C.linc(nc.gpsimd.dma_start(wi0b[j % 2][:], wi0s[e, mc]), sem, 16)
                C.linc(nc.gpsimd.dma_start(wi1b[j % 2][:], wi1s[e, mc]), sem, 16)
            G1a.adjust(PE_ - 128)
            for q in range(32):
                dcq, eq = q // 8, q % 8
                G2a.wait()
                sem = "lwC" if q % 2 == 0 else "lwD"
                C.linc(nc.gpsimd.dma_start(
                    wob[q % 2][:],
                    bass.AP(wos.tensor, eq * 1024 * 2048 + dcq * 512,
                            [[2048, 128], [128 * 2048, 8], [1, 512]])), sem, 16)
            G2a.adjust(PE_ - 32)

            V1.wait()
            for t in range(4):
                nc.vector.reduce_sum(mu[:, t:t + 1], xt[:, t * D:(t + 1) * D],
                                     axis=AX.X, negate=True)
            nc.vector.drain()
            C.linc(nc.vector.tensor_scalar_mul(mu, mu, 1.0 / D), "dv", 1)
            V2.wait()
            nc.vector.tensor_scalar(var, var, 1.0 / D, EPS, ALU.mult, ALU.add)
            nc.vector.drain()
            newton_rsqrt(C, var, rstd, sti[:], sti2[:], stt1[:], stt2[:])
            C.linc(nc.vector.tensor_mul(nmr, mu, rstd), "dv", 1)
            for g, site in ((0, V3), (1, V4)):
                site.wait()
                instr = None
                for t in (2 * g, 2 * g + 1):
                    instr = nc.vector.tensor_copy(
                        yT3[:, :, t * 128:(t + 1) * 128],
                        tp[:, (t % 2) * 2048:(t % 2) * 2048 + 2048]
                        .rearrange("p (k m) -> p k m", k=16))
                C.linc(instr, "dv", 1)
            VB1.wait()
            for t in range(4):
                sl = slice(8 * t, 8 * t + 8)
                pl = p[0][:, sl]
                nc.vector.reduce_max(m1p4[:, t:t + 1], pl, axis=AX.X)
                nc.vector.reduce_max(m1n4[:, t:t + 1], pl, axis=AX.X, negate=True)
                nc.vector.drain()
                nc.vector.tensor_single_scalar(msk[:, sl], pl, m1p4[:, t:t + 1],
                                               ALU.is_ge)
                nc.vector.drain()
                nc.vector.scalar_tensor_tensor(lgm[:, sl], msk[:, sl], -1e30, pl,
                                               ALU.mult, ALU.add)
                nc.vector.drain()
                C.linc(nc.vector.reduce_max(m2p4[:, t:t + 1], lgm[:, sl], axis=AX.X),
                       "dv", 1)
            for t in range(4):
                sl = slice(8 * t, 8 * t + 8)
                VB2[t].wait()
                nc.vector.tensor_single_scalar(ge2[:, sl], p[0][:, sl],
                                               m2p4[:, t:t + 1], ALU.is_ge)
                nc.vector.tensor_scalar_add(den4[:, t:t + 1], d214[:, t:t + 1], 1.0)
                nc.vector.drain()
                nc.vector.reciprocal(rec4[:, t:t + 1], den4[:, t:t + 1])
                nc.vector.drain()
                nc.vector.tensor_mul(wtmp[:], e1s[:, sl], ge2[:, sl])
                nc.vector.drain()
                nc.vector.tensor_scalar(wtk[:, sl], wtmp[:], rec4[:, t:t + 1], 0.5,
                                        ALU.mult, ALU.mult)
                nc.vector.drain()
                C.linc(nc.vector.tensor_copy(wbf[:, sl], wtk[:, sl]), "dv", 1)
            for t in range(4):
                VB3[t].wait()
                C.linc(nc.vector.tensor_copy(
                    wT[:, t * 128:(t + 1) * 128], tp[0:8, t * 128:(t + 1) * 128]),
                    "dv", 1)
            for e in range(8):
                VB4.wait()
                C.linc(nc.vector.tensor_copy(wb8[:, e * 512:(e + 1) * 512], p[1][:]),
                       "dv", 1)
            VB4.adjust(PE_ - 8)
            for j in range(64):
                e, mc = j // 8, j % 8
                VC1a.wait()
                VC1b.wait()
                ph1 = p[1 + 2 * (j % 2)][:]
                nc.vector.tensor_tensor(itmp[:], ph1, ss[j % 2][:], ALU.mult)
                nc.vector.drain()
                C.linc(nc.vector.tensor_tensor(
                    inter[e][:, mc * 512:(mc + 1) * 512], itmp[:],
                    wb8[:, e * 512:(e + 1) * 512], ALU.mult), "dv", 1)
            VC1a.adjust(PE_ - 128)
            VC1b.adjust(AC - 64)
            V6.wait()
            for dcq in range(4):
                VD1.wait()
                instr = None
                for t in range(4):
                    sl = slice(t * D + dcq * 512, t * D + (dcq + 1) * 512)
                    instr = nc.vector.scalar_tensor_tensor(
                        xt[:, sl], p[t][:], 1.0, xt[:, sl], ALU.mult, ALU.add)
                C.linc(instr, "dv", 1)
            VD1.adjust(PE_ - 32)

            A1.wait()
            instr = None
            for t in range(4):
                instr = nc.scalar.activation(sq[:], xt[:, t * D:(t + 1) * D],
                                             AF.Square, bias=mu[:, t:t + 1],
                                             accum_out=var[:, t:t + 1])
            C.linc(instr, "ac", 1)
            A2.wait()
            instr = None
            for t in range(4):
                instr = nc.scalar.activation(ybf4[:, t * D:(t + 1) * D],
                                             xt[:, t * D:(t + 1) * D],
                                             AF.Identity, scale=rstd[:, t:t + 1],
                                             bias=nmr[:, t:t + 1])
            C.linc(instr, "ac", 1)
            for t in range(4):
                sl = slice(8 * t, 8 * t + 8)
                AB1[t].wait()
                nc.scalar.activation(e1s[:, sl], p[0][:, sl], AF.Exp,
                                     bias=m1n4[:, t:t + 1])
                C.linc(nc.scalar.activation(d214[:, t:t + 1], m2p4[:, t:t + 1],
                                            AF.Exp, bias=m1n4[:, t:t + 1]),
                       "ac", 1)
            for j in range(64):
                AC1.wait()
                AC2.wait()
                ph0 = p[2 * (j % 2)][:]
                C.linc(nc.scalar.activation(ss[j % 2][:], ph0, AF.Silu), "ac", 1)
            AC1.adjust(PE_ - 128)
            AC2.adjust(DV - 64)

            P1b.wait()
            P1.wait()
            for g in range(2):
                if g == 1:
                    P2.wait()
                instr = None
                for t in (2 * g, 2 * g + 1):
                    for k in range(16):
                        instr = nc.tensor.transpose(
                            tp[:, (t % 2) * 2048 + k * 128:(t % 2) * 2048 + (k + 1) * 128],
                            ybf4[:, t * D + k * 128:t * D + (k + 1) * 128], idn[:])
                C.linc(instr, "pe", 1)
            PB1.wait()
            instr = None
            for t in range(4):
                for k in range(16):
                    instr = nc.tensor.matmul(
                        p[0][:, 8 * t:8 * t + 8],
                        yT[:, k * 512 + t * 128: k * 512 + (t + 1) * 128],
                        C.wg8_bf[:, k * 8:(k + 1) * 8],
                        start=(t == 0 and k == 0), stop=(t == 3 and k == 15))
            C.linc(instr, "pe", 1)
            for t in range(4):
                PB2[t].wait()
                C.linc(nc.tensor.transpose(
                    tp[0:8, t * 128:(t + 1) * 128],
                    wbf[:, 8 * t:8 * t + 8], idn[:]), "pe", 1)
            PB4.wait()
            for e in range(8):
                if e > 0:
                    PB5.wait()
                C.linc(nc.tensor.matmul(
                    p[1][:], oh8[:, e * 128:(e + 1) * 128], wT[:],
                    start=True, stop=True), "pe", 1)
            PB5.adjust(DV - 7)
            PC0a.wait()
            PC0b.wait()
            for j in range(64):
                e, mc = j // 8, j % 8
                (PCa if j % 2 == 0 else PCb).wait()
                PC2.wait()
                PC3.wait()
                ph0 = p[2 * (j % 2)][:]
                ph1 = p[1 + 2 * (j % 2)][:]
                instr = None
                for k in range(16):
                    instr = nc.tensor.matmul(ph0,
                                             wi0b[j % 2][:, k * 128:(k + 1) * 128],
                                             yT[:, k * 512:(k + 1) * 512],
                                             start=(k == 0), stop=(k == 15))
                C.linc(instr, "pe", 1)
                instr = None
                for k in range(16):
                    instr = nc.tensor.matmul(ph1,
                                             wi1b[j % 2][:, k * 128:(k + 1) * 128],
                                             yT[:, k * 512:(k + 1) * 512],
                                             start=(k == 0), stop=(k == 15))
                C.linc(instr, "pe", 1)
            PC2.adjust(AC - 64)
            PC3.adjust(DV - 64)
            PD3a.wait()
            PD3b.wait()
            for q in range(32):
                dcq, eq = q // 8, q % 8
                if eq == 0 and dcq > 0:
                    PD2.wait()
                (PDa if q % 2 == 0 else PDb).wait()
                instr = None
                for t in range(4):
                    for k in range(8):
                        instr = nc.tensor.matmul(
                            p[t][:],
                            inter[eq][:, k * 512 + t * 128: k * 512 + (t + 1) * 128],
                            wob[q % 2][:, k * 512:(k + 1) * 512],
                            start=(eq == 0 and k == 0), stop=(eq == 7 and k == 7))
                C.linc(instr, "pe", 1)
            PD2.adjust(DV - 3)

        for name, tot in (("lx", LX), ("sx", SX), ("pe", PE_), ("dv", DV),
                          ("ac", AC), ("lwA", LAB), ("lwB", LAB),
                          ("lwC", LCD), ("lwD", LCD)):
            C.bump(name, tot * NSUP)

        C.wait(nc.sync, "sx")
        for s_ in sites:
            s_.free()
        ox_ld.free()
        ox_st.free()



def ln_tile(C, x_ap, out_ap, sb1, rstd_out=None):
    nc = C.nc
    F = x_ap.shape[-1]
    nc.vector.reduce_sum(sb1["mu"], x_ap, axis=AX.X, negate=True)
    nc.vector.drain()
    C.inc(nc.vector.tensor_scalar_mul(sb1["mu"], sb1["mu"], 1.0 / F), "dv", 1)
    C.wait(nc.scalar, "dv")
    C.inc(nc.scalar.activation(sb1["sq"], x_ap, AF.Square, bias=sb1["mu"],
                               accum_out=sb1["var"]), "ac", 1)
    C.wait(nc.vector, "ac")
    nc.vector.tensor_scalar(sb1["var"], sb1["var"], 1.0 / F, EPS, ALU.mult, ALU.add)
    nc.vector.drain()
    newton_rsqrt(C, sb1["var"], sb1["rstd"], sb1["ia"], sb1["ib"],
                 sb1["t1"], sb1["t2"])
    C.inc(nc.vector.tensor_mul(sb1["nmr"], sb1["mu"], sb1["rstd"]), "dv", 1)
    C.wait(nc.scalar, "dv")
    C.inc(nc.scalar.activation(out_ap, x_ap, AF.Identity, scale=sb1["rstd"],
                               bias=sb1["nmr"]), "ac", 1)


def emit_l0a(C, x_in, x_out, W):
    nc = C.nc
    uq = C.uniq()
    from contextlib import ExitStack
    with ExitStack() as ps:
        sb = lambda n, sh, dt: ps.enter_context(nc.sbuf_tensor(uq + n, sh, dt))
        xt = sb("a_xt", [128, D], F32)
        y0f = sb("a_y0f", [128, D], F32)
        n2bf = sb("a_n2bf", [128, D], BF16)
        nT = sb("a_nT", [128, 16 * 128], BF16)
        zf = sb("a_zf", [128, D], F32)
        t4 = sb("a_t4", [128, D], F32)
        sq = sb("a_sq", [128, D], F32)
        u_sb = sb("a_u", [8, S], F32)
        decb = sb("a_decb", [8, S], F32)
        hs = sb("a_hs", [8, S], F32)
        hsb = sb("a_hsb", [8, S], BF16)
        hfbf = sb("a_hf", [32, 128], BF16)
        winf = sb("a_winf", [128, 128], F32)
        winb = sb("a_winb", [128, 128], BF16)
        winpb = sb("a_winpb", [128, 128], BF16)
        wdnf = sb("a_wdnf", [128, 512], F32)
        wdnb = sb("a_wdnb", [128, 512], BF16)
        wdnpb = sb("a_wdnpb", [128, 512], BF16)
        gb_f = sb("a_gb", [128, 32], F32)
        bb_bf = sb("a_bbf", [128, 16], BF16)
        woutb = sb("a_woutb", [8, D], BF16)
        wupb = sb("a_wupb", [32, D], BF16)
        dlg = sb("a_dlg", [8, 1], F32)
        dec = sb("a_dec", [8, 1], F32)
        binp = sb("a_binp", [8, 1], F32)
        bin1 = sb("a_bin1", [8, 1], F32)
        bdnp = sb("a_bdnp", [32, 1], F32)
        bdn1 = sb("a_bdn1", [32, 1], F32)
        r_bout = sb("a_rbo", [1, D], F32)
        r_dsk = sb("a_rds", [1, D], F32)
        r_bup = sb("a_rbu", [1, D], F32)
        r_t = sb("a_rt", [1, D], F32)
        r_bf = sb("a_rbf", [1, D], BF16)
        ones_bf = sb("a_ones", [1, 128], BF16)
        boutb = sb("a_boutb", [128, D], F32)
        gdb = sb("a_gdb", [128, D], BF16)
        bupb = sb("a_bupb", [128, D], F32)
        st1 = {k: sb(f"a_s_{k}", [128, 1], F32 if k not in ("ia", "ib") else I32)[:]
               for k in ("mu", "var", "rstd", "nmr", "t1", "t2", "ia", "ib")}
        st1["sq"] = sq[:]

        tp = ps.enter_context(nc.psum_tensor(uq + "a_tp", [128, 2048], BF16))
        pb = ps.enter_context(nc.psum_tensor(uq + "a_pb", [128, 2048], F32))
        p4 = ps.enter_context(nc.psum_tensor(uq + "a_p4", [128, 512], F32))

        g = nc.gpsimd
        C.inc(g.dma_start(winf[:], W["winp"][:].rearrange("p a b -> p (a b)")), "lw", 16)
        C.inc(g.dma_start(winb[:], W["winp"][:].rearrange("p a b -> p (a b)")), "lw", 16)
        C.inc(g.dma_start(wdnf[:], W["wdnp"][:].rearrange("p a b -> p (a b)")), "lw", 16)
        C.inc(g.dma_start(wdnb[:], W["wdnp"][:].rearrange("p a b -> p (a b)")), "lw", 16)
        C.inc(g.dma_start(gb_f[:, 0:16], W["gbp"][0]), "lw", 16)
        C.inc(g.dma_start(gb_f[:, 16:32], W["gbp"][1]), "lw", 16)
        C.inc(g.dma_start(bb_bf[:], W["gbp"][1]), "lw", 16)
        C.inc(g.dma_start(woutb[:], W["wout"][:]), "lw", 16)
        C.inc(g.dma_start(wupb[:], W["wup"][:]), "lw", 16)
        C.inc(g.dma_start(dlg[:], W["dlg"][:]), "lw", 16)
        C.inc(g.dma_start(binp[:], W["bin"][:]), "lw", 16)
        C.inc(g.dma_start(bdnp[:], W["bdn"][:]), "lw", 16)
        C.inc(g.dma_start(r_bout[:], W["rows"][0:1, :]), "lw", 16)
        C.inc(g.dma_start(r_dsk[:], W["rows"][1:2, :]), "lw", 16)
        C.inc(g.dma_start(r_bup[:], W["rows"][2:3, :]), "lw", 16)
        C.inc(g.dma_start(ones_bf[:], W["ones"][:]), "lw", 16)
        C.wait(nc.vector, "lw")
        C.wait(nc.scalar, "lw")
        C.wait(nc.tensor, "lw")
        g16 = gb_f[:, 0:16]
        nc.vector.tensor_tensor(
            winpb[:].rearrange("p (a b) -> p a b", a=16),
            winf[:].rearrange("p (a b) -> p a b", a=16),
            g16.unsqueeze(2).broadcast_to([128, 16, 8]), ALU.mult)
        nc.vector.tensor_tensor(
            wdnpb[:].rearrange("p (a b) -> p a b", a=16),
            wdnf[:].rearrange("p (a b) -> p a b", a=16),
            g16.unsqueeze(2).broadcast_to([128, 16, 32]), ALU.mult)
        C.inc(nc.scalar.activation(dec[:], dlg[:], AF.Sigmoid), "ac", 1)
        instr = None
        for k in range(16):
            instr = nc.tensor.matmul(p4[0:8, 0:1], winb[:, k * 8:(k + 1) * 8],
                                     bb_bf[:, k:k + 1], start=(k == 0), stop=(k == 15))
        C.inc(instr, "pe", 1)
        C.wait(nc.vector, "pe")
        C.inc(nc.vector.tensor_tensor(bin1[:], p4[0:8, 0:1], binp[:], ALU.add), "dv", 1)
        C.wait(nc.tensor, "dv")
        instr = None
        for k in range(16):
            instr = nc.tensor.matmul(p4[0:32, 1:2], wdnb[:, k * 32:(k + 1) * 32],
                                     bb_bf[:, k:k + 1], start=(k == 0), stop=(k == 15))
        C.inc(instr, "pe", 1)
        C.wait(nc.vector, "pe")
        C.inc(nc.vector.tensor_tensor(bdn1[:], p4[0:32, 1:2], bdnp[:], ALU.add), "dv", 1)
        bcasts = [("bout", boutb, None), ("gd", gdb, None), ("bup", bupb, None)]
        r_gam = sb("a_rgam", [1, D], F32)
        r_bet = sb("a_rbet", [1, D], F32)
        C.inc(g.dma_start(r_gam[:], W["rows"][3:4, :]), "lw", 16)
        C.inc(g.dma_start(r_bet[:], W["rows"][4:5, :]), "lw", 16)
        C.wait(nc.vector, "lw")
        for name, dstb, _ in bcasts:
            if name == "bout":
                nc.vector.tensor_mul(r_t[:], r_bet[:], r_dsk[:])
                nc.vector.drain()
                nc.vector.tensor_add(r_t[:], r_t[:], r_bout[:])
            elif name == "gd":
                nc.vector.tensor_mul(r_t[:], r_gam[:], r_dsk[:])
            else:
                nc.vector.tensor_copy(r_t[:], r_bup[:])
            nc.vector.drain()
            C.inc(nc.vector.tensor_copy(r_bf[:], r_t[:]), "dv", 1)
            C.wait(nc.tensor, "dv")
            instr = None
            for cc in range(4):
                instr = nc.tensor.matmul(pb[:, cc * 512:(cc + 1) * 512], ones_bf[:],
                                         r_bf[:, cc * 512:(cc + 1) * 512],
                                         start=True, stop=True)
            C.inc(instr, "pe", 1)
            C.wait(nc.vector, "pe")
            C.inc(nc.vector.tensor_copy(dstb[:], pb[:]), "dv", 1)
            C.wait(nc.tensor, "dv")

        for b in range(B_CORE):
            for t in range(16):
                row = b * S + t * 128
                C.wait(nc.sync, "dv")
                C.wait(nc.sync, "ac")
                C.inc(nc.sync.dma_start(xt[:], x_in[row:row + 128, :]), "lx", 16)
                C.wait(nc.vector, "lx")
                C.wait(nc.scalar, "lx")
                ln_tile(C, xt[:], y0f[:], st1)
                C.wait(nc.vector, "ac")
                ln_tile(C, y0f[:], n2bf[:], st1)
                C.wait(nc.tensor, "ac")
                C.wait(nc.tensor, "dv")
                instr = None
                for k in range(16):
                    instr = nc.tensor.transpose(tp[:, k * 128:(k + 1) * 128],
                                                n2bf[:, k * 128:(k + 1) * 128],
                                                C.idn_bf[:])
                C.inc(instr, "pe", 1)
                C.wait(nc.vector, "pe")
                C.inc(nc.vector.tensor_copy(nT[:], tp[:]), "dv", 1)
                C.wait(nc.tensor, "dv")
                instr = None
                for k in range(16):
                    instr = nc.tensor.matmul(p4[0:8, 0:128],
                                             winpb[:, k * 8:(k + 1) * 8],
                                             nT[:, k * 128:(k + 1) * 128],
                                             start=(k == 0), stop=(k == 15))
                C.inc(instr, "pe", 1)
                C.wait(nc.vector, "pe")
                C.inc(nc.vector.tensor_scalar(u_sb[:, t * 128:(t + 1) * 128],
                                              p4[0:8, 0:128], bin1[:], None,
                                              ALU.add), "dv", 1)
            C.wait(nc.vector, "dv")
            nc.vector.tensor_scalar(decb[:], u_sb[:], 0.0, dec[:], ALU.mult, ALU.add)
            nc.vector.drain()
            nc.vector.tensor_tensor_scan(hs[:], decb[:], u_sb[:], 0.0,
                                         ALU.mult, ALU.add)
            nc.vector.drain()
            C.inc(nc.vector.tensor_copy(hsb[:], hs[:]), "dv", 1)

            for t in range(16):
                row = b * S + t * 128
                C.wait(nc.sync, "dv")
                C.wait(nc.sync, "ac")
                C.inc(nc.sync.dma_start(xt[:], x_in[row:row + 128, :]), "lx", 16)
                C.wait(nc.vector, "lx")
                C.wait(nc.scalar, "lx")
                ln_tile(C, xt[:], y0f[:], st1)
                C.wait(nc.vector, "ac")
                ln_tile(C, y0f[:], n2bf[:], st1)
                C.wait(nc.tensor, "dv")
                C.wait(nc.tensor, "ac")
                instr = None
                for cc in range(4):
                    instr = nc.tensor.matmul(pb[:, cc * 512:(cc + 1) * 512],
                                             hsb[:, t * 128:(t + 1) * 128],
                                             woutb[:, cc * 512:(cc + 1) * 512],
                                             start=True, stop=True)
                C.inc(instr, "pe", 1)
                C.wait(nc.vector, "pe")
                nc.vector.tensor_tensor(t4[:], n2bf[:], gdb[:], ALU.mult)
                nc.vector.drain()
                nc.vector.tensor_add(t4[:], t4[:], y0f[:])
                nc.vector.drain()
                nc.vector.tensor_add(t4[:], t4[:], boutb[:])
                nc.vector.drain()
                C.inc(nc.vector.scalar_tensor_tensor(zf[:], pb[:], 1.0, t4[:],
                                                     ALU.mult, ALU.add), "dv", 1)
                C.wait(nc.scalar, "dv")
                C.wait(nc.vector, "dv")
                ln_tile(C, zf[:], n2bf[:], st1)
                C.wait(nc.tensor, "ac")
                C.wait(nc.tensor, "dv")
                instr = None
                for k in range(16):
                    instr = nc.tensor.transpose(tp[:, k * 128:(k + 1) * 128],
                                                n2bf[:, k * 128:(k + 1) * 128],
                                                C.idn_bf[:])
                C.inc(instr, "pe", 1)
                C.wait(nc.vector, "pe")
                C.inc(nc.vector.tensor_copy(nT[:], tp[:]), "dv", 1)
                C.wait(nc.tensor, "dv")
                instr = None
                for k in range(16):
                    instr = nc.tensor.matmul(p4[0:32, 0:128],
                                             wdnpb[:, k * 32:(k + 1) * 32],
                                             nT[:, k * 128:(k + 1) * 128],
                                             start=(k == 0), stop=(k == 15))
                C.inc(instr, "pe", 1)
                C.wait(nc.scalar, "pe")
                C.inc(nc.scalar.activation(hfbf[:], p4[0:32, 0:128], AF.Relu,
                                           bias=bdn1[:]), "ac", 1)
                C.wait(nc.tensor, "ac")
                instr = None
                for cc in range(4):
                    instr = nc.tensor.matmul(pb[:, cc * 512:(cc + 1) * 512],
                                             hfbf[:],
                                             wupb[:, cc * 512:(cc + 1) * 512],
                                             start=True, stop=True)
                C.inc(instr, "pe", 1)
                C.wait(nc.vector, "pe")
                nc.vector.tensor_add(t4[:], zf[:], bupb[:])
                nc.vector.drain()
                nc.vector.tensor_add(t4[:], t4[:], xt[:])
                nc.vector.drain()
                C.inc(nc.vector.scalar_tensor_tensor(xt[:], pb[:], 1.0, t4[:],
                                                     ALU.mult, ALU.add), "dv", 1)
                C.wait(nc.sync, "dv")
                C.inc(nc.sync.dma_start(x_out[row:row + 128, :], xt[:]), "sx", 16)
                C.wait(nc.sync, "sx")

        C.wait(nc.sync, "sx")



def emit_l1a(C, x_in, x_out, W, feats_scr):
    nc = C.nc
    uq = C.uniq()
    from contextlib import ExitStack
    with ExitStack() as ps:
        sb = lambda n, sh, dt: ps.enter_context(nc.sbuf_tensor(uq + n, sh, dt))
        xt = sb("q_xt", [128, D], F32)
        ybf = sb("q_ybf", [128, D], BF16)
        sq = sb("q_sq", [128, D], F32)
        st1 = {k: sb(f"q_s_{k}", [128, 1], F32 if k not in ("ia", "ib") else I32)[:]
               for k in ("mu", "var", "rstd", "nmr", "t1", "t2", "ia", "ib")}
        st1["sq"] = sq[:]
        bqkp = sb("q_bqkp", [128, 32], F32)
        r_bv = sb("q_rbv", [1, D], F32)
        r_bo = sb("q_rbo", [1, D], F32)
        r_bf = sb("q_rbf", [1, D], BF16)
        ones_bf = sb("q_ones", [1, 128], BF16)
        scl = sb("q_scl", [128, 1], F32)
        bvb = sb("q_bvb", [128, D], F32)
        bob = sb("q_bob", [128, D], F32)

        psc = [ps.enter_context(nc.psum_tensor(f"{uq}q_psc{i}", [128, 1024], F32))
               for i in range(2)]
        pf = ps.enter_context(nc.psum_tensor(uq + "q_pf", [96, 1024], F32))
        pnorm = ps.enter_context(nc.psum_tensor(uq + "q_pn", [128, 1024], BF16))
        prs = ps.enter_context(nc.psum_tensor(uq + "q_prs", [128, 512], BF16))

        g = nc.gpsimd
        C.inc(g.dma_start(bqkp[:, 0:16], W["bqp"][:]), "lw", 16)
        C.inc(g.dma_start(bqkp[:, 16:32], W["bkp"][:]), "lw", 16)
        C.inc(g.dma_start(r_bv[:], W["bv_row"][:]), "lw", 16)
        C.inc(g.dma_start(r_bo[:], W["bo_row"][:]), "lw", 16)
        C.inc(g.dma_start(ones_bf[:], W["ones"][:]), "lw", 16)
        if DEBUG_L1A:
            C.dbg_qT = nc.dram_tensor("dbg_qT", [128, 2 * 2048], BF16, kind="ExternalOutput").ap()
            C.dbg_kT = nc.dram_tensor("dbg_kT", [128, 2 * 2048], BF16, kind="ExternalOutput").ap()
            C.dbg_Vb = nc.dram_tensor("dbg_Vb", [128, 16 * 384], BF16, kind="ExternalOutput").ap()
            C.dbg_fm = nc.dram_tensor("dbg_fm", [64, 2048], BF16, kind="ExternalOutput").ap()
            C.dbg_rs = nc.dram_tensor("dbg_rs", [32, 2048], BF16, kind="ExternalOutput").ap()
            C.dbg_rsT = nc.dram_tensor("dbg_rsT", [128, 16], F32, kind="ExternalOutput").ap()
            C.dbg_rcT = nc.dram_tensor("dbg_rcT", [128, 16], BF16, kind="ExternalOutput").ap()
            C.dbg_ftg = nc.dram_tensor("dbg_ftg", [128, 16 * 256], BF16, kind="ExternalOutput").ap()
        C.wait(nc.vector, "lw")
        nc.vector.memset(scl[:], 1.0 / np.sqrt(32.0))
        nc.vector.drain()
        C.inc(nc.vector.tensor_copy(scl[:], scl[:]), "dv", 1)
        C.wait(nc.scalar, "dv")
        C.wait(nc.tensor, "dv")
        for row, dstb in ((r_bv, bvb), (r_bo, bob)):
            C.inc(nc.vector.tensor_copy(r_bf[:], row[:]), "dv", 1)
            C.wait(nc.tensor, "dv")
            instr = None
            for cc in range(2):
                instr = nc.tensor.matmul(psc[0][:, cc * 512:(cc + 1) * 512], ones_bf[:],
                                         r_bf[:, cc * 512:(cc + 1) * 512],
                                         start=True, stop=True)
                instr = nc.tensor.matmul(psc[1][:, cc * 512:(cc + 1) * 512], ones_bf[:],
                                         r_bf[:, 1024 + cc * 512:1024 + (cc + 1) * 512],
                                         start=True, stop=True)
            C.inc(instr, "pe", 1)
            C.wait(nc.vector, "pe")
            nc.vector.tensor_copy(dstb[:, 0:1024], psc[0][:])
            C.inc(nc.vector.tensor_copy(dstb[:, 1024:2048], psc[1][:]), "dv", 1)
            C.wait(nc.tensor, "dv")

        for b in range(B_CORE):
            with ExitStack() as bs:
                yT = bs.enter_context(nc.sbuf_tensor(f"{uq}q_yT_{b}", [128, 16 * 2048], BF16))
                qT = bs.enter_context(nc.sbuf_tensor(f"{uq}q_qT_{b}", [128, 2 * 2048], BF16))
                kT = bs.enter_context(nc.sbuf_tensor(f"{uq}q_kT_{b}", [128, 2 * 2048], BF16))
                Vb = bs.enter_context(nc.sbuf_tensor(f"{uq}q_Vb_{b}", [128, 16 * 384], BF16))
                pt = [bs.enter_context(nc.sbuf_tensor(f"{uq}q_pt{i}_{b}", [128, 1024], BF16))
                      for i in range(2)]
                fm = bs.enter_context(nc.sbuf_tensor(f"{uq}q_fm_{b}", [64, 2048], BF16))
                rs32 = bs.enter_context(nc.sbuf_tensor(f"{uq}q_rs32_{b}", [32, 2048], BF16))
                rsT = bs.enter_context(nc.sbuf_tensor(f"{uq}q_rsT_{b}", [128, 16], F32))
                rcT = bs.enter_context(nc.sbuf_tensor(f"{uq}q_rcT_{b}", [128, 16], F32))
                rcTb = bs.enter_context(nc.sbuf_tensor(f"{uq}q_rcTb_{b}", [128, 16], BF16))
                ftg = bs.enter_context(nc.sbuf_tensor(f"{uq}q_ftg_{b}", [128, 16 * 256], BF16))
                wq_b = bs.enter_context(nc.sbuf_tensor(f"{uq}q_wq_{b}", [128, 16 * 256], BF16))
                wk_b = bs.enter_context(nc.sbuf_tensor(f"{uq}q_wk_{b}", [128, 16 * 256], BF16))
                wv_b = bs.enter_context(nc.sbuf_tensor(f"{uq}q_wv_{b}", [128, 16 * 256], BF16))
                yT3 = yT[:].rearrange("p (k s) -> p k s", k=16)

                for t in range(16):
                    row = b * S + t * 128
                    C.wait(nc.sync, "ac")
                    C.inc(nc.sync.dma_start(xt[:], x_in[row:row + 128, :]), "lx", 16)
                    C.wait(nc.vector, "lx")
                    C.wait(nc.scalar, "lx")
                    ln_tile(C, xt[:], ybf[:], st1)
                    C.wait(nc.tensor, "ac")
                    C.wait(nc.tensor, "dv")
                    for grp in range(2):
                        instr = None
                        for k in range(8):
                            kk = grp * 8 + k
                            instr = nc.tensor.transpose(
                                pnorm[:, k * 128:(k + 1) * 128],
                                ybf[:, kk * 128:(kk + 1) * 128], C.idn_bf[:])
                        C.inc(instr, "pe", 1)
                        C.wait(nc.vector, "pe")
                        C.inc(nc.vector.tensor_copy(
                            yT3[:, grp * 8:(grp + 1) * 8, t * 128:(t + 1) * 128],
                            pnorm[:].rearrange("p (k m) -> p k m", k=8)), "dv", 1)
                        C.wait(nc.tensor, "dv")

                for gi in range(8):
                    col = gi * 256
                    C.wait(nc.gpsimd, "pe")
                    for wb_, wsrc in ((wq_b, W["wq"]), (wk_b, W["wk"]), (wv_b, W["wv"])):
                        C.inc(g.dma_start(
                            wb_[:],
                            bass.AP(wsrc.tensor, col,
                                    [[D, 128], [128 * D, 16], [1, 256]])), "lw", 16)
                    C.wait(nc.tensor, "lw")
                    for dst, wb_, bcol in ((qT, wq_b, 0), (kT, wk_b, 16)):
                        for mc in range(2):
                            for ns in range(4):
                                instr = None
                                for kc in range(16):
                                    instr = nc.tensor.matmul(
                                        psc[0][:, 0:512],
                                        wb_[:, kc * 256 + mc * 128: kc * 256 + (mc + 1) * 128],
                                        yT[:, kc * 2048 + ns * 512: kc * 2048 + (ns + 1) * 512],
                                        start=(kc == 0), stop=(kc == 15))
                                C.inc(instr, "pe", 1)
                                C.wait(nc.scalar, "pe")
                                C.inc(nc.scalar.activation(
                                    dst[:, mc * 2048 + ns * 512: mc * 2048 + (ns + 1) * 512],
                                    psc[0][:, 0:512], AF.Identity,
                                    bias=bqkp[:, bcol + 2 * gi + mc: bcol + 2 * gi + mc + 1]),
                                    "ac", 1)
                                C.wait(nc.tensor, "ac")
                    C.wait(nc.vector, "ac")
                    C.inc(nc.vector.memset(Vb[:], 1.0), "dv", 1)
                    C.wait(nc.tensor, "dv")
                    for t in range(16):
                        instr = None
                        for kc in range(16):
                            instr = nc.tensor.matmul(
                                psc[1][:, 0:256],
                                yT[:, kc * 2048 + t * 128: kc * 2048 + (t + 1) * 128],
                                wv_b[:, kc * 256:(kc + 1) * 256],
                                start=(kc == 0), stop=(kc == 15))
                        C.inc(instr, "pe", 1)
                        C.wait(nc.vector, "pe")
                        C.inc(nc.vector.tensor_tensor(
                            Vb[:].rearrange("p (t h m) -> p t h m", t=16, h=4)
                            [:, t, :, 0:64].rearrange("p h m -> p (h m)")
                            if False else
                            bass.AP(Vb, Vb[:].offset + t * 384,
                                    [[Vb[:].ap[0][0], 128], [96, 4], [1, 64]]),
                            psc[1][:, 0:256].rearrange("p (h m) -> p h m", h=4),
                            bvb[:, col:col + 256].rearrange("p (h m) -> p h m", h=4),
                            ALU.add), "dv", 1)
                        C.wait(nc.tensor, "dv")
                    if DEBUG_L1A and b == 0 and gi == 0:
                        C.wait(nc.sync, "dv")
                        C.wait(nc.sync, "ac")
                        C.inc(nc.sync.dma_start(C.dbg_qT[:], qT[:]), "sx", 16)
                        C.inc(nc.sync.dma_start(C.dbg_kT[:], kT[:]), "sx", 16)
                        C.inc(nc.sync.dma_start(C.dbg_Vb[:], Vb[:]), "sx", 16)
                        C.wait(nc.sync, "sx")
                    C.wait(nc.tensor, "dv")
                    for h in range(4):
                        mc = h // 2
                        kb = 64 * (h % 2)
                        for sh in range(2):
                            exp_ac = []
                            scores_pe = []
                            ac_base = C.n["ac"]

                            def emit_scores(tc):
                                C.wait(nc.tensor, "ac",
                                       exp_ac[tc - 2] if tc >= 2 else ac_base)
                                instr = None
                                for nn in range(2):
                                    instr = nc.tensor.matmul(
                                        psc[tc % 2][:, nn * 512:(nn + 1) * 512],
                                        kT[kb:kb + 64,
                                           mc * 2048 + tc * 128: mc * 2048 + (tc + 1) * 128],
                                        qT[kb:kb + 64,
                                           mc * 2048 + sh * 1024 + nn * 512:
                                           mc * 2048 + sh * 1024 + (nn + 1) * 512],
                                        start=True, stop=True)
                                C.inc(instr, "pe", 1)
                                scores_pe.append(C.n["pe"])

                            def emit_exp(tc):
                                C.wait(nc.scalar, "pe", scores_pe[tc])
                                C.inc(nc.scalar.activation(pt[tc % 2][:], psc[tc % 2][:],
                                                           AF.Exp, scale=scl[:]),
                                      "ac", 1)
                                exp_ac.append(C.n["ac"])

                            def emit_pv(tc):
                                C.wait(nc.tensor, "ac", exp_ac[tc])
                                instr = None
                                for nn in range(2):
                                    instr = nc.tensor.matmul(
                                        pf[:, nn * 512:(nn + 1) * 512],
                                        Vb[:, tc * 384 + h * 96: tc * 384 + (h + 1) * 96],
                                        pt[tc % 2][:, nn * 512:(nn + 1) * 512],
                                        start=(tc == 0), stop=(tc == 15))
                                C.inc(instr, "pe", 1)

                            for tc in range(16):
                                emit_scores(tc)
                                emit_exp(tc)
                                if tc >= 1:
                                    emit_pv(tc - 1)
                            emit_pv(15)
                            C.wait(nc.vector, "pe")
                            nc.vector.tensor_copy(fm[:, sh * 1024:(sh + 1) * 1024],
                                                  pf[0:64, :])
                            C.inc(nc.vector.tensor_copy(rs32[:, sh * 1024:(sh + 1) * 1024],
                                                        pf[64:96, :]), "dv", 1)
                            C.wait(nc.tensor, "dv")
                        for grp in range(2):
                            instr = None
                            for t in range(8):
                                tt = grp * 8 + t
                                instr = nc.tensor.transpose(
                                    pnorm[:, t * 128:t * 128 + 64],
                                    fm[:, tt * 128:(tt + 1) * 128],
                                    C.idn_bf[0:64, 0:64])
                            C.inc(instr, "pe", 1)
                            C.wait(nc.vector, "pe")
                            C.inc(nc.vector.tensor_copy(
                                bass.AP(ftg,
                                        ftg[:].offset + grp * 8 * 256 + h * 64,
                                        [[ftg[:].ap[0][0], 128], [256, 8], [1, 64]]),
                                pnorm[:].rearrange("p (t m) -> p t m", t=8)[:, :, 0:64]),
                                "dv", 1)
                            C.wait(nc.tensor, "dv")
                        if DEBUG_L1A and b == 0 and gi == 0 and h == 0:
                            C.wait(nc.sync, "dv")
                            C.inc(nc.sync.dma_start(C.dbg_fm[:], fm[:]), "sx", 16)
                            C.inc(nc.sync.dma_start(C.dbg_rs[:], rs32[:]), "sx", 16)
                            C.wait(nc.sync, "sx")
                        instr = None
                        for t in range(16):
                            instr = nc.tensor.transpose(
                                prs[:, t * 32:(t + 1) * 32],
                                rs32[:, t * 128:(t + 1) * 128],
                                C.idn_bf[0:32, 0:32])
                        C.inc(instr, "pe", 1)
                        C.wait(nc.vector, "pe")
                        nc.vector.tensor_copy(
                            rsT[:], prs[:].rearrange("p (t m) -> p t m", t=16)[:, :, 0:1]
                            .rearrange("p t m -> p (t m)"))
                        nc.vector.drain()
                        nc.vector.reciprocal(rcT[:], rsT[:])
                        nc.vector.drain()
                        C.inc(nc.vector.tensor_copy(rcTb[:], rcT[:]), "dv", 1)
                        nc.vector.drain()
                        if DEBUG_L1A and b == 0 and gi == 0 and h == 0:
                            C.wait(nc.sync, "dv")
                            C.inc(nc.sync.dma_start(C.dbg_rsT[:], rsT[:]), "sx", 16)
                            C.inc(nc.sync.dma_start(C.dbg_rcT[:], rcTb[:]), "sx", 16)
                            C.wait(nc.sync, "sx")
                        C.inc(nc.vector.tensor_tensor(
                            bass.AP(ftg, ftg[:].offset + h * 64,
                                    [[ftg[:].ap[0][0], 128], [256, 16], [1, 64]]),
                            bass.AP(ftg, ftg[:].offset + h * 64,
                                    [[ftg[:].ap[0][0], 128], [256, 16], [1, 64]]),
                            rcTb[:].unsqueeze(2).broadcast_to([128, 16, 64]),
                            ALU.mult), "dv", 1)
                        C.wait(nc.tensor, "dv")
                    if DEBUG_L1A and b == 0 and gi == 0:
                        C.wait(nc.sync, "dv")
                        C.inc(nc.sync.dma_start(C.dbg_ftg[:], ftg[:]), "sx", 16)
                        C.wait(nc.sync, "sx")
                    C.wait(nc.sync, "dv")
                    C.inc(nc.sync.dma_start(
                        bass.AP(feats_scr.tensor, b * S * D + col,
                                [[D, 128], [128 * D, 16], [1, 256]]),
                        ftg[:].rearrange("p (t m) -> p t m", t=16)), "sx", 16)
                    C.wait(nc.sync, "sx")
                    C.wait(nc.vector, "sx")

            with ExitStack() as bs:
                ftT = bs.enter_context(nc.sbuf_tensor(f"{uq}q_ftT_{b}", [128, 16 * 2048], BF16))
                wo_cc = bs.enter_context(nc.sbuf_tensor(f"{uq}q_wocc_{b}", [128, 16 * 512], BF16))
                ft_t = bs.enter_context(nc.sbuf_tensor(f"{uq}q_ftt_{b}", [128, 2048], BF16))
                xsl = bs.enter_context(nc.sbuf_tensor(f"{uq}q_xsl_{b}", [128, 512], F32))
                osl = bs.enter_context(nc.sbuf_tensor(f"{uq}q_osl_{b}", [128, 512], F32))
                fT3 = ftT[:].rearrange("p (k s) -> p k s", k=16)
                for t in range(16):
                    row = b * S + t * 128
                    C.wait(nc.sync, "dv")
                    C.inc(nc.sync.dma_start(ft_t[:], feats_scr[row:row + 128, :]),
                          "lx", 16)
                    C.wait(nc.tensor, "lx")
                    C.wait(nc.tensor, "dv")
                    for grp in range(2):
                        instr = None
                        for k in range(8):
                            kk = grp * 8 + k
                            instr = nc.tensor.transpose(
                                pnorm[:, k * 128:(k + 1) * 128],
                                ft_t[:, kk * 128:(kk + 1) * 128], C.idn_bf[:])
                        C.inc(instr, "pe", 1)
                        C.wait(nc.vector, "pe")
                        C.inc(nc.vector.tensor_copy(
                            fT3[:, grp * 8:(grp + 1) * 8, t * 128:(t + 1) * 128],
                            pnorm[:].rearrange("p (k m) -> p k m", k=8)), "dv", 1)
                        C.wait(nc.tensor, "dv")
                for cc in range(4):
                    C.wait(nc.gpsimd, "pe")
                    C.inc(g.dma_start(
                        wo_cc[:],
                        bass.AP(W["wo"].tensor, cc * 512,
                                [[D, 128], [128 * D, 16], [1, 512]])), "lw", 16)
                    C.wait(nc.tensor, "lw")
                    for t in range(16):
                        row = b * S + t * 128
                        C.wait(nc.sync, "dv")
                        C.inc(nc.sync.dma_start(
                            xsl[:], x_in[row:row + 128, cc * 512:(cc + 1) * 512]),
                            "lx", 16)
                        C.wait(nc.tensor, "lx")
                        instr = None
                        for kc in range(16):
                            instr = nc.tensor.matmul(
                                psc[0][:, 0:512],
                                fT3[:, kc, t * 128:(t + 1) * 128],
                                wo_cc[:, kc * 512:(kc + 1) * 512],
                                start=(kc == 0), stop=(kc == 15))
                        C.inc(instr, "pe", 1)
                        C.wait(nc.vector, "pe")
                        C.wait(nc.vector, "lx")
                        nc.vector.tensor_add(osl[:], xsl[:], bob[:, cc * 512:(cc + 1) * 512])
                        nc.vector.drain()
                        C.inc(nc.vector.scalar_tensor_tensor(
                            osl[:], psc[0][:, 0:512], 1.0, osl[:], ALU.mult, ALU.add),
                            "dv", 1)
                        C.wait(nc.sync, "dv")
                        C.inc(nc.sync.dma_start(
                            x_out[row:row + 128, cc * 512:(cc + 1) * 512], osl[:]),
                            "sx", 16)
                        C.wait(nc.sync, "sx")

        C.wait(nc.sync, "sx")



PHASES_ALL = ["l0f_a", "l0a", "l0f_b", "l1f_a", "l1a", "l1f_b"]


def build_program(phases=None):
    phases = phases or PHASES_ALL
    from contextlib import ExitStack
    es = ExitStack()
    nc = bass.Bass("TRN2", target_bir_lowering=False)

    x_ext = nc.dram_tensor("x", [NTOK, D], F32, kind="ExternalInput").ap()
    out = nc.dram_tensor("out", [NTOK, D], F32, kind="ExternalOutput").ap()

    need = set(phases)
    inputs_used = ["x", "c_ident_bf"]

    ident_bf = nc.dram_tensor("c_ident_bf", [128, 128], BF16, kind="ExternalInput").ap()

    if need & {"l1f_a", "l1f_b"}:
        if SKIP_PREP:
            wgT = nc.dram_tensor("l1f_wgTb", [64 * 128, D], BF16, kind="ExternalInput").ap()
            wuT = nc.dram_tensor("l1f_wuTb", [64 * 128, D], BF16, kind="ExternalInput").ap()
            wdT = nc.dram_tensor("l1f_wdTb", [8192, D], BF16, kind="ExternalInput").ap()
            inputs_used += ["l1f_wgTb", "l1f_wuTb", "l1f_wdTb"]
        else:
            wgT_f = nc.dram_tensor("l1f_wgT", [64 * 128, D], F32, kind="ExternalInput").ap()
            wuT_f = nc.dram_tensor("l1f_wuT", [64 * 128, D], F32, kind="ExternalInput").ap()
            wdT_f = nc.dram_tensor("l1f_wdT", [8192, D], F32, kind="ExternalInput").ap()
            wgT = nc.dram_tensor("s_wgT", [64 * 128, D], BF16, kind="Internal").ap()
            wuT = nc.dram_tensor("s_wuT", [64 * 128, D], BF16, kind="Internal").ap()
            wdT = nc.dram_tensor("s_wdT", [8192, D], BF16, kind="Internal").ap()
            inputs_used += ["l1f_wgT", "l1f_wuT", "l1f_wdT"]

    if need & {"l0f_a", "l0f_b"}:
        wg8p = nc.dram_tensor("l0f_wg8p", [128, 16, 8], F32, kind="ExternalInput").ap()
        oh8p = nc.dram_tensor("c_oh8", [8, 8 * 128], BF16, kind="ExternalInput").ap()
        wi0_f = nc.dram_tensor("l0f_wi0", [8, 8, 128, D], F32, kind="ExternalInput").ap()
        wi1_f = nc.dram_tensor("l0f_wi1", [8, 8, 128, D], F32, kind="ExternalInput").ap()
        wo_f = nc.dram_tensor("l0f_wo", [8, 1024, D], F32, kind="ExternalInput").ap()
        wi0s = nc.dram_tensor("s_wi0", [8, 8, 128, D], BF16, kind="Internal").ap()
        wi1s = nc.dram_tensor("s_wi1", [8, 8, 128, D], BF16, kind="Internal").ap()
        wos = nc.dram_tensor("s_wo", [8, 1024, D], BF16, kind="Internal").ap()
        inputs_used += ["l0f_wg8p", "c_oh8", "l0f_wi0", "l0f_wi1", "l0f_wo"]

    if "l0a" in need:
        l0a_W = {
            "winp": nc.dram_tensor("l0a_winp", [128, 16, 8], F32, kind="ExternalInput").ap(),
            "wdnp": nc.dram_tensor("l0a_wdnp", [128, 16, 32], F32, kind="ExternalInput").ap(),
            "gbp": nc.dram_tensor("l0a_gbp", [2, 128, 16], F32, kind="ExternalInput").ap(),
            "wout": nc.dram_tensor("l0a_wout", [8, D], F32, kind="ExternalInput").ap(),
            "wup": nc.dram_tensor("l0a_wup", [32, D], F32, kind="ExternalInput").ap(),
            "dlg": nc.dram_tensor("l0a_dlg", [8, 1], F32, kind="ExternalInput").ap(),
            "bin": nc.dram_tensor("l0a_bin", [8, 1], F32, kind="ExternalInput").ap(),
            "bdn": nc.dram_tensor("l0a_bdn", [32, 1], F32, kind="ExternalInput").ap(),
            "rows": nc.dram_tensor("l0a_rows", [5, D], F32, kind="ExternalInput").ap(),
            "ones": nc.dram_tensor("c_ones_bf", [1, 128], BF16, kind="ExternalInput").ap(),
        }
        inputs_used += ["l0a_winp", "l0a_wdnp", "l0a_gbp", "l0a_wout", "l0a_wup",
                        "l0a_dlg", "l0a_bin", "l0a_bdn", "l0a_rows", "c_ones_bf"]

    if "l1a" in need:
        l1a_W = {
            "wq": nc.dram_tensor("l1a_wq", [D, D], F32, kind="ExternalInput").ap(),
            "wk": nc.dram_tensor("l1a_wk", [D, D], F32, kind="ExternalInput").ap(),
            "wv": nc.dram_tensor("l1a_wv", [D, D], F32, kind="ExternalInput").ap(),
            "wo": nc.dram_tensor("l1a_wo", [D, D], F32, kind="ExternalInput").ap(),
            "bqp": nc.dram_tensor("l1a_bqp", [128, 16], F32, kind="ExternalInput").ap(),
            "bkp": nc.dram_tensor("l1a_bkp", [128, 16], F32, kind="ExternalInput").ap(),
            "bv_row": nc.dram_tensor("l1a_bv", [1, D], F32, kind="ExternalInput").ap(),
            "bo_row": nc.dram_tensor("l1a_bo", [1, D], F32, kind="ExternalInput").ap(),
            "ones": nc.dram_tensor("c_ones1a", [1, 128], BF16, kind="ExternalInput").ap(),
        }
        feats_scr = nc.dram_tensor("s_feats", [NTOK, D], BF16,
                                   kind="ExternalOutput" if DEBUG_FEATS else "Internal").ap()
        if DEBUG_L1A:
            pass
        inputs_used += ["l1a_wq", "l1a_wk", "l1a_wv", "l1a_wo", "l1a_bqp",
                        "l1a_bkp", "l1a_bv", "l1a_bo", "c_ones1a"]

    xA = nc.dram_tensor("s_xA", [NTOK, D], F32, kind="Internal").ap()
    xB = nc.dram_tensor("s_xB", [NTOK, D], F32, kind="Internal").ap()

    with es:
        C = Ctx(nc, es)
        C.idn_bf = es.enter_context(nc.sbuf_tensor("c_idn", [128, 128], BF16))
        C.inc(nc.sync.dma_start(C.idn_bf[:], ident_bf[:]), "lx", 16)
        if need & {"l0f_a", "l0f_b"}:
            C.oh8_bf = es.enter_context(nc.sbuf_tensor("c_oh8s", [8, 8 * 128], BF16))
            C.wg8_bf = es.enter_context(nc.sbuf_tensor("c_wg8s", [128, 128], BF16))
            C.inc(nc.sync.dma_start(C.oh8_bf[:], oh8p[:]), "lx", 16)
            C.inc(nc.gpsimd.dma_start(
                C.wg8_bf[:], wg8p[:].rearrange("p a b -> p (a b)")), "lw", 16)
            C.wait(nc.tensor, "lw")
        C.wait(nc.tensor, "lx")
        C.barrier()

        jobs = []
        if need & {"l1f_a", "l1f_b"} and not SKIP_PREP:
            jobs += [(wgT_f, wgT), (wuT_f, wuT), (wdT_f, wdT)]
        if need & {"l0f_a", "l0f_b"}:
            jobs += [(wi0_f.rearrange("a b p d -> (a b p) d"),
                      wi0s.rearrange("a b p d -> (a b p) d")),
                     (wi1_f.rearrange("a b p d -> (a b p) d"),
                      wi1s.rearrange("a b p d -> (a b p) d")),
                     (wo_f.rearrange("a m d -> (a m) d"),
                      wos.rearrange("a m d -> (a m) d"))]
        if DUMMY_PREP:
            dmy = nc.dram_tensor("s_dmy", [NTOK, D], BF16, kind="Internal").ap()
            jobs += [(x_ext, dmy)] * 3
        if jobs:
            emit_prep(C, jobs)
            C.barrier()

        cur = x_ext
        nxt = [xA, xB]
        ni = 0
        for ph in phases:
            dst = out if ph == phases[-1] else nxt[ni]
            if ph in ("l1f_a", "l1f_b"):
                emit_l1ffn(C, cur, dst, wgT, wuT, wdT,
                           wsem="lw" if ph == "l1f_a" else "g2")
            elif ph in ("l0f_a", "l0f_b"):
                emit_moe(C, cur, dst, None, wi0s, wi1s, wos)
            elif ph == "l0a":
                emit_l0a(C, cur, dst, l0a_W)
            elif ph == "l1a":
                emit_l1a(C, cur, dst, l1a_W, feats_scr)
            else:
                raise NotImplementedError(ph)
            C.barrier()
            cur = dst
            ni ^= 1

    return nc, inputs_used



def swizzle_gu(w):
    wT = w.T.reshape(16, 128, 64, 128)
    return np.ascontiguousarray(wT.transpose(2, 1, 0, 3).reshape(64 * 128, 2048))


def swizzle_wi(w):
    w5 = w.reshape(8, 16, 128, 8, 128)
    return np.ascontiguousarray(
        w5.transpose(0, 3, 2, 1, 4).reshape(8, 8, 128, 2048))


def make_in_maps(inputs, inputs_used, phases=None):
    x_full = np.asarray(inputs["L0_attn_x"], dtype=np.float32)
    base = {}
    base["c_ident_bf"] = np.eye(128, dtype=BF)
    if "l1f_wgT" in inputs_used:
        base["l1f_wgT"] = swizzle_gu(np.asarray(inputs["L1_ffn_w_gate"], np.float32))
        base["l1f_wuT"] = swizzle_gu(np.asarray(inputs["L1_ffn_w_up"], np.float32))
        base["l1f_wdT"] = np.ascontiguousarray(
            np.asarray(inputs["L1_ffn_w_down"], np.float32).T)
    if "l0a_winp" in inputs_used:
        pk = lambda v, w: np.ascontiguousarray(
            np.asarray(v, np.float32).reshape(16, 128, w).transpose(1, 0, 2))
        base["l0a_winp"] = pk(inputs["L0_attn_W_in"], 8)
        base["l0a_wdnp"] = pk(inputs["L0_attn_W_down"], 32)
        gm = np.asarray(inputs["L0_attn_ln_gamma"], np.float32)
        bt = np.asarray(inputs["L0_attn_ln_beta"], np.float32)
        base["l0a_gbp"] = np.ascontiguousarray(np.stack(
            [gm.reshape(16, 128).T, bt.reshape(16, 128).T]))
        base["l0a_wout"] = np.ascontiguousarray(np.asarray(inputs["L0_attn_W_out"], np.float32))
        base["l0a_wup"] = np.ascontiguousarray(np.asarray(inputs["L0_attn_W_up"], np.float32))
        base["l0a_dlg"] = np.asarray(inputs["L0_attn_decay_logit"], np.float32).reshape(8, 1)
        base["l0a_bin"] = np.asarray(inputs["L0_attn_b_in"], np.float32).reshape(8, 1)
        base["l0a_bdn"] = np.asarray(inputs["L0_attn_b_down"], np.float32).reshape(32, 1)
        base["l0a_rows"] = np.ascontiguousarray(np.stack([
            np.asarray(inputs["L0_attn_b_out"], np.float32),
            np.asarray(inputs["L0_attn_D_skip"], np.float32),
            np.asarray(inputs["L0_attn_b_up"], np.float32),
            gm, bt]))
        base["c_ones_bf"] = np.ones((1, 128), dtype=BF)
    if "l1a_wq" in inputs_used:
        base["l1a_wq"] = np.ascontiguousarray(np.asarray(inputs["L1_attn_Wq"], np.float32))
        base["l1a_wk"] = np.ascontiguousarray(np.asarray(inputs["L1_attn_Wk"], np.float32))
        base["l1a_wv"] = np.ascontiguousarray(np.asarray(inputs["L1_attn_Wv"], np.float32))
        base["l1a_wo"] = np.ascontiguousarray(np.asarray(inputs["L1_attn_Wo"], np.float32))
        pk16 = lambda v: np.ascontiguousarray(
            np.asarray(v, np.float32).reshape(16, 128).T)
        base["l1a_bqp"] = pk16(inputs["L1_attn_bq"])
        base["l1a_bkp"] = pk16(inputs["L1_attn_bk"])
        base["l1a_bv"] = np.asarray(inputs["L1_attn_bv"], np.float32).reshape(1, D)
        base["l1a_bo"] = np.asarray(inputs["L1_attn_bo"], np.float32).reshape(1, D)
        base["c_ones1a"] = np.ones((1, 128), dtype=BF)
    if "l0f_wi0" in inputs_used:
        wg = np.asarray(inputs["L0_ffn_w_gate"], np.float32)
        base["l0f_wg8p"] = np.ascontiguousarray(
            wg.reshape(16, 128, 8).transpose(1, 0, 2))
        oh = np.zeros((8, 8, 128), dtype=BF)
        for e in range(8):
            oh[e, e, :] = 1
        base["c_oh8"] = np.ascontiguousarray(oh.transpose(1, 0, 2).reshape(8, 8 * 128))
        base["l0f_wi0"] = swizzle_wi(np.asarray(inputs["L0_ffn_wi_0"], np.float32))
        base["l0f_wi1"] = swizzle_wi(np.asarray(inputs["L0_ffn_wi_1"], np.float32))
        base["l0f_wo"] = np.ascontiguousarray(np.asarray(inputs["L0_ffn_wo"], np.float32))
    in_maps = []
    for c in range(8):
        m = dict(base)
        m["x"] = np.ascontiguousarray(
            x_full[c * B_CORE:(c + 1) * B_CORE].reshape(NTOK, D))
        in_maps.append(m)
    return in_maps


def kernel(**inputs):
    nc, inputs_used = build_program()
    in_maps = make_in_maps(inputs, inputs_used)
    res = run_bass_kernel_spmd(nc, in_maps, core_ids=list(range(8)))
    outs = [r["out"].reshape(B_CORE, S, D) for r in res.results]
    return np.concatenate(outs, axis=0).astype(np.float32)


# version 04
# speedup vs baseline: 1.6778x; 1.2915x over previous
import numpy as np
import ml_dtypes
import concourse.bass as bass
import concourse.mybir as mybir
from concourse.bass_utils import run_bass_kernel_spmd

F32 = mybir.dt.float32
BF16 = mybir.dt.bfloat16
I32 = mybir.dt.int32
AF = mybir.ActivationFunctionType
ALU = mybir.AluOpType
AX = mybir.AxisListType
ET = mybir.EngineType

BF = ml_dtypes.bfloat16

D = 2048
S = 2048
B_CORE = 2
NTOK = B_CORE * S
EPS = 1e-5
RSQRT_MAGIC = 0x5f3759df

ALL_ENG = [ET.SP, ET.Pool, ET.PE, ET.DVE, ET.Activation]
L1F_NSUP = 8
USE_FORI = True
SKIP_PREP = False
DUMMY_PREP = False
DIAG_SERIAL_DMA = True
DEBUG_FEATS = False
DEBUG_L1A = False


class _PyLoop:
    def __init__(self, n):
        self.n = n

    def __enter__(self):
        return 0

    def __exit__(self, *a):
        return False


def loop_ctx(nc, n):
    if USE_FORI:
        return nc.Fori(0, n, engines=ALL_ENG)
    return _PyLoop(n)



class Ctx:
    def __init__(self, nc, es):
        self.nc = nc
        self.es = es
        self.sems = {}
        self.n = {}
        for name in ["lx", "lw", "sx", "pe", "dv", "ac", "g2",
                     "lwA", "lwB", "lwC", "lwD"]:
            self.sems[name] = nc.alloc_semaphore(name)
            self.n[name] = 0
        self._uq = 0

    def uniq(self):
        self._uq += 1
        return f"u{self._uq}_"

    def inc(self, instr, name, k=1):
        instr.then_inc(self.sems[name], k)
        self.n[name] += k
        return instr

    def linc(self, instr, name, k=1):
        instr.then_inc(self.sems[name], k)
        return instr

    def bump(self, name, k):
        self.n[name] += k

    def wait(self, eng, name, val=None):
        eng.wait_ge(self.sems[name], self.n[name] if val is None else val)

    def barrier(self):
        self.nc.all_engine_barrier()


class Site:

    def __init__(self, C, eng, sem_name, init_off, delta, name):
        self.eng = eng
        self.sem = C.sems[sem_name]
        self.delta = delta
        self.reg = eng.alloc_register(name)
        eng.reg_mov(self.reg, C.n[sem_name] + init_off)

    def wait(self):
        self.eng.wait_ge(self.sem, self.reg)
        if self.delta:
            self.eng.reg_add(self.reg, self.reg, self.delta)

    def adjust(self, k):
        if k:
            self.eng.reg_add(self.reg, self.reg, k)

    def free(self):
        self.eng.free_register(self.reg)


class Off:

    def __init__(self, eng, init, delta, name):
        self.eng = eng
        self.delta = delta
        self.reg = eng.alloc_register(name)
        eng.reg_mov(self.reg, init)

    def next(self):
        self.eng.reg_add(self.reg, self.reg, self.delta)

    def free(self):
        self.eng.free_register(self.reg)


def rap(tensor_ap, reg, pairs):
    t = tensor_ap.tensor if isinstance(tensor_ap, bass.AP) else tensor_ap
    return bass.AP(t, reg, pairs)


def newton_rsqrt(C, v_ap, out_ap, ia_ap, ib_ap, tmp_ap, tmp2_ap, iters=3):
    v = C.nc.vector
    v.tensor_single_scalar(ia_ap, v_ap.bitcast(I32), 1, ALU.arith_shift_right)
    v.drain()
    v.tensor_scalar(ib_ap, ia_ap, -1, RSQRT_MAGIC, ALU.mult, ALU.add)
    v.drain()
    cur, nxt = ib_ap, ia_ap
    for _ in range(iters):
        r = cur.bitcast(F32)
        v.tensor_mul(tmp_ap, r, r)
        v.drain()
        v.tensor_mul(tmp2_ap, tmp_ap, v_ap)
        v.drain()
        v.tensor_scalar(tmp_ap, tmp2_ap, -0.5, 1.5, ALU.mult, ALU.add)
        v.drain()
        v.tensor_mul(nxt.bitcast(F32), r, tmp_ap)
        v.drain()
        cur, nxt = nxt, cur
    v.tensor_copy(out_ap, cur.bitcast(F32))
    v.drain()



def emit_prep(C, jobs):
    nc = C.nc
    from contextlib import ExitStack
    with ExitStack() as ps:
        buf = [ps.enter_context(nc.sbuf_tensor(f"prep{i}", [128, 4096], BF16))
               for i in range(2)]
        k = 0
        for src, dst in jobs:
            R, Cc = src.shape
            assert R % 128 == 0, (R, Cc)
            for r0 in range(0, R, 128):
                for c0 in range(0, Cc, 4096):
                    w = min(4096, Cc - c0)
                    b = buf[k % 2]
                    if k >= 2:
                        C.wait(nc.gpsimd, "sx", C.n["sx"])
                    C.inc(nc.gpsimd.dma_start(b[:, 0:w], src[r0:r0 + 128, c0:c0 + w]),
                          "lw", 16)
                    C.wait(nc.sync, "lw")
                    C.inc(nc.sync.dma_start(dst[r0:r0 + 128, c0:c0 + w], b[:, 0:w]),
                          "sx", 16)
                    k += 1
        C.wait(nc.sync, "sx")



def emit_l1ffn(C, x_in, x_out, wgT, wuT, wdT, wsem="lw"):
    nc = C.nc
    uq = C.uniq()
    from contextlib import ExitStack
    with ExitStack() as ps:
        sb = lambda n, sh, dt: ps.enter_context(nc.sbuf_tensor(uq + n, sh, dt))
        xt = sb("f_xt", [128, 4 * D], F32)
        yT = sb("f_yT", [128, 16 * 512], BF16)
        gup = sb("f_gup", [128, 64 * 512], BF16)
        wg_b = [sb(f"f_wg{i}", [128, 16 * 128], BF16) for i in range(2)]
        wu_b = [sb(f"f_wu{i}", [128, 16 * 128], BF16) for i in range(2)]
        wd_b = [sb(f"f_wd{i}", [128, 8 * 512], BF16) for i in range(2)]
        sq = sb("f_sq", [128, D], F32)
        ybf4 = sb("f_ybf4", [128, 4 * D], BF16)
        gs = [sb(f"f_gs{i}", [128, 512], BF16) for i in range(2)]
        st4 = sb("f_st4", [128, 16], F32)
        sti = sb("f_sti", [128, 4], I32)
        sti2 = sb("f_sti2", [128, 4], I32)
        stt1 = sb("f_stt1", [128, 4], F32)
        stt2 = sb("f_stt2", [128, 4], F32)

        tp = ps.enter_context(nc.psum_tensor(uq + "f_tp", [128, 4096], BF16))
        p = [ps.enter_context(nc.psum_tensor(f"{uq}f_p{i}", [128, 512], F32))
             for i in range(4)]

        mu, var, rstd, nmr = (st4[:, 0:4], st4[:, 4:8], st4[:, 8:12], st4[:, 12:16])
        idn = C.idn_bf
        yT3 = yT[:].rearrange("p (k s) -> p k s", k=16)

        NSUP = L1F_NSUP
        DV, AC, PE_, LX, LW, SX = 72, 66, 386, 16, 6144, 16

        S1 = Site(C, nc.sync, "sx", 0, SX, uq + "fS1")
        S2 = Site(C, nc.sync, "dv", DV, DV, uq + "fS2")
        G1 = Site(C, nc.gpsimd, "pe", 0, 2, uq + "fG1")
        G2 = Site(C, nc.gpsimd, "pe", 122, 8, uq + "fG2")
        P1b = Site(C, nc.tensor, "dv", 0, DV, uq + "fP1b")
        P1 = Site(C, nc.tensor, "ac", 2, AC, uq + "fP1")
        P2 = Site(C, nc.tensor, "dv", 3, DV, uq + "fP2")
        P3a = Site(C, nc.tensor, "lwA", 32, 32, uq + "fP3a")
        P3b2 = Site(C, nc.tensor, "lwB", 32, 32, uq + "fP3b2")
        P3b = Site(C, nc.tensor, "dv", 4, DV, uq + "fP3b")
        P3c = Site(C, nc.tensor, "ac", 1, 1, uq + "fP3c")
        P3d = Site(C, nc.tensor, "dv", 3, 1, uq + "fP3d")
        P4a = Site(C, nc.tensor, "ac", AC, AC, uq + "fP4a")
        P4b = Site(C, nc.tensor, "dv", 68, DV, uq + "fP4b")
        P5a = Site(C, nc.tensor, "lwC", 16, 16, uq + "fP5a")
        P5b = Site(C, nc.tensor, "lwD", 16, 16, uq + "fP5b")
        P6 = Site(C, nc.tensor, "dv", 69, 1, uq + "fP6")
        V1 = Site(C, nc.vector, "lx", 16, LX, uq + "fV1")
        V2 = Site(C, nc.vector, "ac", 1, AC, uq + "fV2")
        V3 = Site(C, nc.vector, "pe", 1, PE_, uq + "fV3")
        V4 = Site(C, nc.vector, "pe", 2, PE_, uq + "fV4")
        V5a = Site(C, nc.vector, "pe", 4, 2, uq + "fV5a")
        V5b = Site(C, nc.vector, "ac", 3, 1, uq + "fV5b")
        V6 = Site(C, nc.vector, "sx", 0, SX, uq + "fV6")
        V7 = Site(C, nc.vector, "pe", 194, 64, uq + "fV7")
        A1 = Site(C, nc.scalar, "dv", 1, DV, uq + "fA1")
        A2 = Site(C, nc.scalar, "dv", 2, DV, uq + "fA2")
        A3 = Site(C, nc.scalar, "pe", 3, 2, uq + "fA3")
        A4 = Site(C, nc.scalar, "dv", 3, 1, uq + "fA4")
        sites = [S1, S2, G1, G2, P1b, P1, P2, P3a, P3b2, P3b, P3c, P3d, P4a, P4b, P5a, P5b, P6,
                 V1, V2, V3, V4, V5a, V5b, V6, V7, A1, A2, A3, A4]

        ox_ld = Off(nc.sync, 0, 512 * D, uq + "fox_ld")
        ox_st = Off(nc.sync, 0, 512 * D, uq + "fox_st")

        for _rep in range(1 if USE_FORI else NSUP):
         if _rep == 0:
          ctxmgr = loop_ctx(nc, NSUP)
         else:
          ctxmgr = _PyLoop(NSUP)
         with ctxmgr:
            S1.wait()
            C.linc(nc.sync.dma_start(
                xt[:].rearrange("p (t j) -> p t j", t=4),
                rap(x_in, ox_ld.reg, [[D, 128], [128 * D, 4], [1, D]])),
                "lx", 16)
            ox_ld.next()
            S2.wait()
            C.linc(nc.sync.dma_start(
                rap(x_out, ox_st.reg, [[D, 128], [128 * D, 4], [1, D]]),
                xt[:].rearrange("p (t j) -> p t j", t=4)),
                "sx", 16)
            ox_st.next()

            for hc in range(64):
                G1.wait()
                psem = "lwA" if hc % 2 == 0 else "lwB"
                C.linc(nc.gpsimd.dma_start(
                    wg_b[hc % 2][:],
                    wgT[hc * 128:(hc + 1) * 128, :]), psem, 16)
                C.linc(nc.gpsimd.dma_start(
                    wu_b[hc % 2][:],
                    wuT[hc * 128:(hc + 1) * 128, :]), psem, 16)
            G1.adjust(PE_ - 128)
            for dc in range(4):
                for hb in range(8):
                    q = dc * 8 + hb
                    G2.wait()
                    C.linc(nc.gpsimd.dma_start(
                        wd_b[q % 2][:].rearrange("p (h m) -> p h m", h=8),
                        bass.AP(wdT.tensor, hb * 8 * 128 * D + dc * 512,
                                [[D, 128], [128 * D, 8], [1, 512]])),
                           "lwC" if q % 2 == 0 else "lwD", 16)
            G2.adjust(130)

            V1.wait()
            for t in range(4):
                nc.vector.reduce_sum(mu[:, t:t + 1], xt[:, t * D:(t + 1) * D],
                                     axis=AX.X, negate=True)
            nc.vector.drain()
            C.linc(nc.vector.tensor_scalar_mul(mu, mu, 1.0 / D), "dv", 1)
            V2.wait()
            nc.vector.tensor_scalar(var, var, 1.0 / D, EPS, ALU.mult, ALU.add)
            nc.vector.drain()
            newton_rsqrt(C, var, rstd, sti[:], sti2[:], stt1[:], stt2[:])
            C.linc(nc.vector.tensor_mul(nmr, mu, rstd), "dv", 1)
            for g, site in ((0, V3), (1, V4)):
                site.wait()
                instr = None
                for t in (2 * g, 2 * g + 1):
                    instr = nc.vector.tensor_copy(
                        yT3[:, :, t * 128:(t + 1) * 128],
                        tp[:, (t % 2) * 2048:(t % 2) * 2048 + 2048]
                        .rearrange("p (k m) -> p k m", k=16))
                C.linc(instr, "dv", 1)
            for hc in range(64):
                V5a.wait()
                V5b.wait()
                C.linc(nc.vector.tensor_tensor(
                    gup[:, hc * 512:(hc + 1) * 512], p[2 + hc % 2][:],
                    gs[hc % 2][:], ALU.mult), "dv", 1)
            V5a.adjust(PE_ - 128)
            V5b.adjust(AC - 64)
            V6.wait()
            for dc in range(4):
                V7.wait()
                instr = None
                for t in range(4):
                    sl = slice(t * D + dc * 512, t * D + (dc + 1) * 512)
                    instr = nc.vector.scalar_tensor_tensor(
                        xt[:, sl], p[t][:], 0.5, xt[:, sl], ALU.mult, ALU.add)
                C.linc(instr, "dv", 1)
            V7.adjust(130)

            A1.wait()
            instr = None
            for t in range(4):
                instr = nc.scalar.activation(sq[:], xt[:, t * D:(t + 1) * D],
                                             AF.Square, bias=mu[:, t:t + 1],
                                             accum_out=var[:, t:t + 1])
            C.linc(instr, "ac", 1)
            A2.wait()
            instr = None
            for t in range(4):
                instr = nc.scalar.activation(ybf4[:, t * D:(t + 1) * D],
                                             xt[:, t * D:(t + 1) * D],
                                             AF.Identity, scale=rstd[:, t:t + 1],
                                             bias=nmr[:, t:t + 1])
            C.linc(instr, "ac", 1)
            for hc in range(64):
                A3.wait()
                A4.wait()
                C.linc(nc.scalar.activation(gs[hc % 2][:], p[hc % 2][:], AF.Silu),
                       "ac", 1)
            A3.adjust(PE_ - 128)
            A4.adjust(DV - 64)

            P1b.wait()
            P1.wait()
            for g in range(2):
                if g == 1:
                    P2.wait()
                instr = None
                for t in (2 * g, 2 * g + 1):
                    for k in range(16):
                        instr = nc.tensor.transpose(
                            tp[:, (t % 2) * 2048 + k * 128:(t % 2) * 2048 + (k + 1) * 128],
                            ybf4[:, t * D + k * 128:t * D + (k + 1) * 128], idn[:])
                C.linc(instr, "pe", 1)
            P3b.wait()
            for hc in range(64):
                (P3a if hc % 2 == 0 else P3b2).wait()
                P3c.wait()
                P3d.wait()
                instr = None
                for k in range(16):
                    instr = nc.tensor.matmul(p[hc % 2][:],
                                             wg_b[hc % 2][:, k * 128:(k + 1) * 128],
                                             yT[:, k * 512:(k + 1) * 512],
                                             start=(k == 0), stop=(k == 15))
                C.linc(instr, "pe", 1)
                instr = None
                for k in range(16):
                    instr = nc.tensor.matmul(p[2 + hc % 2][:],
                                             wu_b[hc % 2][:, k * 128:(k + 1) * 128],
                                             yT[:, k * 512:(k + 1) * 512],
                                             start=(k == 0), stop=(k == 15))
                C.linc(instr, "pe", 1)
            P3c.adjust(AC - 64)
            P3d.adjust(DV - 64)
            P4a.wait()
            P4b.wait()
            for dc in range(4):
                if dc > 0:
                    P6.wait()
                for hc in range(64):
                    q = dc * 8 + hc // 8
                    if hc % 8 == 0:
                        (P5a if q % 2 == 0 else P5b).wait()
                    instr = None
                    for t in range(4):
                        instr = nc.tensor.matmul(
                            p[t][:],
                            gup[:, hc * 512 + t * 128: hc * 512 + (t + 1) * 128],
                            wd_b[q % 2][:, (hc % 8) * 512:(hc % 8 + 1) * 512],
                            start=(hc == 0), stop=(hc == 63))
                    C.linc(instr, "pe", 1)
            P6.adjust(DV - 3)

        for name, tot in (("lx", LX), ("lwA", 1024), ("lwB", 1024),
                          ("lwC", 256), ("lwD", 256), ("sx", SX), ("pe", PE_),
                          ("dv", DV), ("ac", AC)):
            C.bump(name, tot * NSUP)

        C.wait(nc.sync, "sx")
        for s_ in sites:
            s_.free()
        ox_ld.free()
        ox_st.free()



def emit_moe(C, x_in, x_out, wi01, wos):
    nc = C.nc
    uq = C.uniq()
    from contextlib import ExitStack
    with ExitStack() as ps:
        sb = lambda n, sh, dt: ps.enter_context(nc.sbuf_tensor(uq + n, sh, dt))
        xt = sb("m_xt", [128, 4 * D], F32)
        yT = sb("m_yT", [128, 16 * 512], BF16)
        inter = [sb(f"m_it{e}", [128, 8 * 512], BF16) for e in range(8)]
        wib = [sb(f"m_wi{i}", [128, 4096], BF16) for i in range(2)]
        wob = [sb(f"m_wo{i}", [128, 8 * 512], BF16) for i in range(2)]
        sq = sb("m_sq", [128, D], F32)
        ybf4 = sb("m_ybf4", [128, 4 * D], BF16)
        ss = [sb(f"m_ss{i}", [128, 512], BF16) for i in range(2)]
        itmp = sb("m_itmp", [128, 512], F32)
        st4 = sb("m_st4", [128, 16], F32)
        sti = sb("m_sti", [128, 4], I32)
        sti2 = sb("m_sti2", [128, 4], I32)
        stt1 = sb("m_stt1", [128, 4], F32)
        stt2 = sb("m_stt2", [128, 4], F32)
        m1p4 = sb("m_m1p", [128, 4], F32)
        m1n4 = sb("m_m1n", [128, 4], F32)
        m2p4 = sb("m_m2p", [128, 4], F32)
        d214 = sb("m_d21", [128, 4], F32)
        den4 = sb("m_den", [128, 4], F32)
        rec4 = sb("m_rec", [128, 4], F32)
        msk = sb("m_msk", [128, 32], F32)
        lgm = sb("m_lgm", [128, 32], F32)
        e1s = sb("m_e1", [128, 32], F32)
        ge2 = sb("m_ge2", [128, 32], F32)
        wtmp = sb("m_wtmp", [128, 8], F32)
        wtk = sb("m_wtk", [128, 32], F32)
        wbf = sb("m_wbf", [128, 32], BF16)
        wT = sb("m_wT", [8, 512], BF16)
        wb8 = sb("m_wb8", [128, 8 * 512], F32)

        tp = ps.enter_context(nc.psum_tensor(uq + "m_tp", [128, 4096], BF16))
        p = [ps.enter_context(nc.psum_tensor(f"{uq}m_p{i}", [128, 512], F32))
             for i in range(4)]

        mu, var, rstd, nmr = (st4[:, 0:4], st4[:, 4:8], st4[:, 8:12], st4[:, 12:16])
        idn = C.idn_bf
        oh8 = C.oh8_bf
        yT3 = yT[:].rearrange("p (k s) -> p k s", k=16)

        NSUP = 8
        DV, AC, PE_, LX, SX = 92, 70, 175, 16, 16
        LAB = 16 * 32
        LCD = 16 * 16

        S1 = Site(C, nc.sync, "sx", 0, SX, uq + "mS1")
        S2 = Site(C, nc.sync, "dv", DV, DV, uq + "mS2")
        G1a = Site(C, nc.gpsimd, "pe", 13, 2, uq + "mG1a")
        G2a = Site(C, nc.gpsimd, "pe", 142, 1, uq + "mG2a")
        P1b = Site(C, nc.tensor, "dv", 0, DV, uq + "mP1b")
        P1 = Site(C, nc.tensor, "ac", 2, AC, uq + "mP1")
        P2 = Site(C, nc.tensor, "dv", 3, DV, uq + "mP2")
        PB1 = Site(C, nc.tensor, "dv", 4, DV, uq + "mPB1")
        PB2 = [Site(C, nc.tensor, "dv", v, DV, f"{uq}mPB2_{t}")
               for t, v in enumerate((9, 13, 14, 15))]
        PB4 = Site(C, nc.tensor, "dv", 16, DV, uq + "mPB4")
        PB5 = Site(C, nc.tensor, "dv", 17, 1, uq + "mPB5")
        PCa = Site(C, nc.tensor, "lwA", 16, 16, uq + "mPCa")
        PCb = Site(C, nc.tensor, "lwB", 16, 16, uq + "mPCb")
        PC0a = Site(C, nc.tensor, "ac", 6, AC, uq + "mPC0a")
        PC0b = Site(C, nc.tensor, "dv", 24, DV, uq + "mPC0b")
        PC2 = Site(C, nc.tensor, "ac", 5, 1, uq + "mPC2")
        PC3 = Site(C, nc.tensor, "dv", 23, 1, uq + "mPC3")
        PDa = Site(C, nc.tensor, "lwC", 16, 16, uq + "mPDa")
        PDb = Site(C, nc.tensor, "lwD", 16, 16, uq + "mPDb")
        PD2 = Site(C, nc.tensor, "dv", 89, 1, uq + "mPD2")
        PD3a = Site(C, nc.tensor, "ac", 70, AC, uq + "mPD3a")
        PD3b = Site(C, nc.tensor, "dv", 88, DV, uq + "mPD3b")
        V1 = Site(C, nc.vector, "lx", 16, LX, uq + "mV1")
        V2 = Site(C, nc.vector, "ac", 1, AC, uq + "mV2")
        V3 = Site(C, nc.vector, "pe", 1, PE_, uq + "mV3")
        V4 = Site(C, nc.vector, "pe", 2, PE_, uq + "mV4")
        VB1 = Site(C, nc.vector, "pe", 3, PE_, uq + "mVB1")
        VB2 = [Site(C, nc.vector, "ac", 3 + t, AC, f"{uq}mVB2_{t}") for t in range(4)]
        VB3 = [Site(C, nc.vector, "pe", 4 + t, PE_, f"{uq}mVB3_{t}") for t in range(4)]
        VB4 = Site(C, nc.vector, "pe", 8, 1, uq + "mVB4")
        VC1a = Site(C, nc.vector, "pe", 17, 2, uq + "mVC1a")
        VC1b = Site(C, nc.vector, "ac", 7, 1, uq + "mVC1b")
        VD1 = Site(C, nc.vector, "pe", 151, 8, uq + "mVD1")
        V6 = Site(C, nc.vector, "sx", 0, SX, uq + "mV6")
        A1 = Site(C, nc.scalar, "dv", 1, DV, uq + "mA1")
        A2 = Site(C, nc.scalar, "dv", 2, DV, uq + "mA2")
        AB1 = [Site(C, nc.scalar, "dv", 5 + t, DV, f"{uq}mAB1_{t}") for t in range(4)]
        AC1 = Site(C, nc.scalar, "pe", 16, 2, uq + "mAC1")
        AC2 = Site(C, nc.scalar, "dv", 23, 1, uq + "mAC2")
        sites = ([S1, S2, G1a, G2a, P1b, P1, P2, PB1, PB4, PB5, PCa, PCb,
                  PC0a, PC0b, PC2, PC3, PDa, PDb, PD2, PD3a, PD3b, V1, V2, V3,
                  V4, VB1, VB4, VC1a, VC1b, VD1, V6, A1, A2, AC1, AC2]
                 + PB2 + VB2 + VB3 + AB1)

        ox_ld = Off(nc.sync, 0, 512 * D, uq + "mox_ld")
        ox_st = Off(nc.sync, 0, 512 * D, uq + "mox_st")

        with nc.Fori(0, NSUP, engines=ALL_ENG):
            S1.wait()
            C.linc(nc.sync.dma_start(
                xt[:].rearrange("p (t j) -> p t j", t=4),
                rap(x_in, ox_ld.reg, [[D, 128], [128 * D, 4], [1, D]])), "lx", 16)
            ox_ld.next()
            S2.wait()
            C.linc(nc.sync.dma_start(
                rap(x_out, ox_st.reg, [[D, 128], [128 * D, 4], [1, D]]),
                xt[:].rearrange("p (t j) -> p t j", t=4)), "sx", 16)
            ox_st.next()

            for j in range(64):
                e, mc = j // 8, j % 8
                G1a.wait()
                sem = "lwA" if j % 2 == 0 else "lwB"
                C.linc(nc.gpsimd.dma_start(wib[j % 2][:], wi01[e, mc]), sem, 16)
            G1a.adjust(PE_ - 128)
            for q in range(32):
                dcq, eq = q // 8, q % 8
                G2a.wait()
                sem = "lwC" if q % 2 == 0 else "lwD"
                C.linc(nc.gpsimd.dma_start(
                    wob[q % 2][:],
                    bass.AP(wos.tensor, eq * 1024 * 2048 + dcq * 512,
                            [[2048, 128], [128 * 2048, 8], [1, 512]])), sem, 16)
            G2a.adjust(PE_ - 32)

            V1.wait()
            for t in range(4):
                nc.vector.reduce_sum(mu[:, t:t + 1], xt[:, t * D:(t + 1) * D],
                                     axis=AX.X, negate=True)
            nc.vector.drain()
            C.linc(nc.vector.tensor_scalar_mul(mu, mu, 1.0 / D), "dv", 1)
            V2.wait()
            nc.vector.tensor_scalar(var, var, 1.0 / D, EPS, ALU.mult, ALU.add)
            nc.vector.drain()
            newton_rsqrt(C, var, rstd, sti[:], sti2[:], stt1[:], stt2[:])
            C.linc(nc.vector.tensor_mul(nmr, mu, rstd), "dv", 1)
            for g, site in ((0, V3), (1, V4)):
                site.wait()
                instr = None
                for t in (2 * g, 2 * g + 1):
                    instr = nc.vector.tensor_copy(
                        yT3[:, :, t * 128:(t + 1) * 128],
                        tp[:, (t % 2) * 2048:(t % 2) * 2048 + 2048]
                        .rearrange("p (k m) -> p k m", k=16))
                C.linc(instr, "dv", 1)
            VB1.wait()
            for t in range(4):
                sl = slice(8 * t, 8 * t + 8)
                pl = p[0][:, sl]
                nc.vector.reduce_max(m1p4[:, t:t + 1], pl, axis=AX.X)
                nc.vector.reduce_max(m1n4[:, t:t + 1], pl, axis=AX.X, negate=True)
                nc.vector.drain()
                nc.vector.tensor_single_scalar(msk[:, sl], pl, m1p4[:, t:t + 1],
                                               ALU.is_ge)
                nc.vector.drain()
                nc.vector.scalar_tensor_tensor(lgm[:, sl], msk[:, sl], -1e30, pl,
                                               ALU.mult, ALU.add)
                nc.vector.drain()
                C.linc(nc.vector.reduce_max(m2p4[:, t:t + 1], lgm[:, sl], axis=AX.X),
                       "dv", 1)
            for t in range(4):
                sl = slice(8 * t, 8 * t + 8)
                VB2[t].wait()
                nc.vector.tensor_single_scalar(ge2[:, sl], p[0][:, sl],
                                               m2p4[:, t:t + 1], ALU.is_ge)
                nc.vector.tensor_scalar_add(den4[:, t:t + 1], d214[:, t:t + 1], 1.0)
                nc.vector.drain()
                nc.vector.reciprocal(rec4[:, t:t + 1], den4[:, t:t + 1])
                nc.vector.drain()
                nc.vector.tensor_mul(wtmp[:], e1s[:, sl], ge2[:, sl])
                nc.vector.drain()
                nc.vector.tensor_scalar(wtk[:, sl], wtmp[:], rec4[:, t:t + 1], 0.5,
                                        ALU.mult, ALU.mult)
                nc.vector.drain()
                C.linc(nc.vector.tensor_copy(wbf[:, sl], wtk[:, sl]), "dv", 1)
            for t in range(4):
                VB3[t].wait()
                C.linc(nc.vector.tensor_copy(
                    wT[:, t * 128:(t + 1) * 128], tp[0:8, t * 128:(t + 1) * 128]),
                    "dv", 1)
            for e in range(8):
                VB4.wait()
                C.linc(nc.vector.tensor_copy(wb8[:, e * 512:(e + 1) * 512], p[1][:]),
                       "dv", 1)
            VB4.adjust(PE_ - 8)
            for j in range(64):
                e, mc = j // 8, j % 8
                VC1a.wait()
                VC1b.wait()
                ph1 = p[1 + 2 * (j % 2)][:]
                nc.vector.tensor_tensor(itmp[:], ph1, ss[j % 2][:], ALU.mult)
                nc.vector.drain()
                C.linc(nc.vector.tensor_tensor(
                    inter[e][:, mc * 512:(mc + 1) * 512], itmp[:],
                    wb8[:, e * 512:(e + 1) * 512], ALU.mult), "dv", 1)
            VC1a.adjust(PE_ - 128)
            VC1b.adjust(AC - 64)
            V6.wait()
            for dcq in range(4):
                VD1.wait()
                instr = None
                for t in range(4):
                    sl = slice(t * D + dcq * 512, t * D + (dcq + 1) * 512)
                    instr = nc.vector.scalar_tensor_tensor(
                        xt[:, sl], p[t][:], 1.0, xt[:, sl], ALU.mult, ALU.add)
                C.linc(instr, "dv", 1)
            VD1.adjust(PE_ - 32)

            A1.wait()
            instr = None
            for t in range(4):
                instr = nc.scalar.activation(sq[:], xt[:, t * D:(t + 1) * D],
                                             AF.Square, bias=mu[:, t:t + 1],
                                             accum_out=var[:, t:t + 1])
            C.linc(instr, "ac", 1)
            A2.wait()
            instr = None
            for t in range(4):
                instr = nc.scalar.activation(ybf4[:, t * D:(t + 1) * D],
                                             xt[:, t * D:(t + 1) * D],
                                             AF.Identity, scale=rstd[:, t:t + 1],
                                             bias=nmr[:, t:t + 1])
            C.linc(instr, "ac", 1)
            for t in range(4):
                sl = slice(8 * t, 8 * t + 8)
                AB1[t].wait()
                nc.scalar.activation(e1s[:, sl], p[0][:, sl], AF.Exp,
                                     bias=m1n4[:, t:t + 1])
                C.linc(nc.scalar.activation(d214[:, t:t + 1], m2p4[:, t:t + 1],
                                            AF.Exp, bias=m1n4[:, t:t + 1]),
                       "ac", 1)
            for j in range(64):
                AC1.wait()
                AC2.wait()
                ph0 = p[2 * (j % 2)][:]
                C.linc(nc.scalar.activation(ss[j % 2][:], ph0, AF.Silu), "ac", 1)
            AC1.adjust(PE_ - 128)
            AC2.adjust(DV - 64)

            P1b.wait()
            P1.wait()
            for g in range(2):
                if g == 1:
                    P2.wait()
                instr = None
                for t in (2 * g, 2 * g + 1):
                    for k in range(16):
                        instr = nc.tensor.transpose(
                            tp[:, (t % 2) * 2048 + k * 128:(t % 2) * 2048 + (k + 1) * 128],
                            ybf4[:, t * D + k * 128:t * D + (k + 1) * 128], idn[:])
                C.linc(instr, "pe", 1)
            PB1.wait()
            instr = None
            for t in range(4):
                for k in range(16):
                    instr = nc.tensor.matmul(
                        p[0][:, 8 * t:8 * t + 8],
                        yT[:, k * 512 + t * 128: k * 512 + (t + 1) * 128],
                        C.wg8_bf[:, k * 8:(k + 1) * 8],
                        start=(t == 0 and k == 0), stop=(t == 3 and k == 15))
            C.linc(instr, "pe", 1)
            for t in range(4):
                PB2[t].wait()
                C.linc(nc.tensor.transpose(
                    tp[0:8, t * 128:(t + 1) * 128],
                    wbf[:, 8 * t:8 * t + 8], idn[:]), "pe", 1)
            PB4.wait()
            for e in range(8):
                if e > 0:
                    PB5.wait()
                C.linc(nc.tensor.matmul(
                    p[1][:], oh8[:, e * 128:(e + 1) * 128], wT[:],
                    start=True, stop=True), "pe", 1)
            PB5.adjust(DV - 7)
            PC0a.wait()
            PC0b.wait()
            for j in range(64):
                e, mc = j // 8, j % 8
                (PCa if j % 2 == 0 else PCb).wait()
                PC2.wait()
                PC3.wait()
                ph0 = p[2 * (j % 2)][:]
                ph1 = p[1 + 2 * (j % 2)][:]
                instr = None
                for k in range(16):
                    instr = nc.tensor.matmul(ph0,
                                             wib[j % 2][:, k * 128:(k + 1) * 128],
                                             yT[:, k * 512:(k + 1) * 512],
                                             start=(k == 0), stop=(k == 15))
                C.linc(instr, "pe", 1)
                instr = None
                for k in range(16):
                    instr = nc.tensor.matmul(ph1,
                                             wib[j % 2][:, 2048 + k * 128:2048 + (k + 1) * 128],
                                             yT[:, k * 512:(k + 1) * 512],
                                             start=(k == 0), stop=(k == 15))
                C.linc(instr, "pe", 1)
            PC2.adjust(AC - 64)
            PC3.adjust(DV - 64)
            PD3a.wait()
            PD3b.wait()
            for q in range(32):
                dcq, eq = q // 8, q % 8
                if eq == 0 and dcq > 0:
                    PD2.wait()
                (PDa if q % 2 == 0 else PDb).wait()
                instr = None
                for t in range(4):
                    for k in range(8):
                        instr = nc.tensor.matmul(
                            p[t][:],
                            inter[eq][:, k * 512 + t * 128: k * 512 + (t + 1) * 128],
                            wob[q % 2][:, k * 512:(k + 1) * 512],
                            start=(eq == 0 and k == 0), stop=(eq == 7 and k == 7))
                C.linc(instr, "pe", 1)
            PD2.adjust(DV - 3)

        for name, tot in (("lx", LX), ("sx", SX), ("pe", PE_), ("dv", DV),
                          ("ac", AC), ("lwA", LAB), ("lwB", LAB),
                          ("lwC", LCD), ("lwD", LCD)):
            C.bump(name, tot * NSUP)

        C.wait(nc.sync, "sx")
        for s_ in sites:
            s_.free()
        ox_ld.free()
        ox_st.free()



def ln_tile(C, x_ap, out_ap, sb1, rstd_out=None):
    nc = C.nc
    F = x_ap.shape[-1]
    nc.vector.reduce_sum(sb1["mu"], x_ap, axis=AX.X, negate=True)
    nc.vector.drain()
    C.inc(nc.vector.tensor_scalar_mul(sb1["mu"], sb1["mu"], 1.0 / F), "dv", 1)
    C.wait(nc.scalar, "dv")
    C.inc(nc.scalar.activation(sb1["sq"], x_ap, AF.Square, bias=sb1["mu"],
                               accum_out=sb1["var"]), "ac", 1)
    C.wait(nc.vector, "ac")
    nc.vector.tensor_scalar(sb1["var"], sb1["var"], 1.0 / F, EPS, ALU.mult, ALU.add)
    nc.vector.drain()
    newton_rsqrt(C, sb1["var"], sb1["rstd"], sb1["ia"], sb1["ib"],
                 sb1["t1"], sb1["t2"])
    C.inc(nc.vector.tensor_mul(sb1["nmr"], sb1["mu"], sb1["rstd"]), "dv", 1)
    C.wait(nc.scalar, "dv")
    C.inc(nc.scalar.activation(out_ap, x_ap, AF.Identity, scale=sb1["rstd"],
                               bias=sb1["nmr"]), "ac", 1)


def emit_l0a(C, x_in, x_out, W):
    nc = C.nc
    uq = C.uniq()
    from contextlib import ExitStack
    with ExitStack() as ps:
        sb = lambda n, sh, dt: ps.enter_context(nc.sbuf_tensor(uq + n, sh, dt))
        xt = sb("a_xt", [128, D], F32)
        y0f = sb("a_y0f", [128, D], F32)
        n2bf = sb("a_n2bf", [128, D], BF16)
        nT = sb("a_nT", [128, 16 * 128], BF16)
        zf = sb("a_zf", [128, D], F32)
        t4 = sb("a_t4", [128, D], F32)
        sq = sb("a_sq", [128, D], F32)
        u_sb = sb("a_u", [8, S], F32)
        decb = sb("a_decb", [8, S], F32)
        hs = sb("a_hs", [8, S], F32)
        hsb = sb("a_hsb", [8, S], BF16)
        hfbf = sb("a_hf", [32, 128], BF16)
        winf = sb("a_winf", [128, 128], F32)
        winb = sb("a_winb", [128, 128], BF16)
        winpb = sb("a_winpb", [128, 128], BF16)
        wdnf = sb("a_wdnf", [128, 512], F32)
        wdnb = sb("a_wdnb", [128, 512], BF16)
        wdnpb = sb("a_wdnpb", [128, 512], BF16)
        gb_f = sb("a_gb", [128, 32], F32)
        bb_bf = sb("a_bbf", [128, 16], BF16)
        woutb = sb("a_woutb", [8, D], BF16)
        wupb = sb("a_wupb", [32, D], BF16)
        dlg = sb("a_dlg", [8, 1], F32)
        dec = sb("a_dec", [8, 1], F32)
        binp = sb("a_binp", [8, 1], F32)
        bin1 = sb("a_bin1", [8, 1], F32)
        bdnp = sb("a_bdnp", [32, 1], F32)
        bdn1 = sb("a_bdn1", [32, 1], F32)
        r_bout = sb("a_rbo", [1, D], F32)
        r_dsk = sb("a_rds", [1, D], F32)
        r_bup = sb("a_rbu", [1, D], F32)
        r_t = sb("a_rt", [1, D], F32)
        r_bf = sb("a_rbf", [1, D], BF16)
        ones_bf = sb("a_ones", [1, 128], BF16)
        boutb = sb("a_boutb", [128, D], F32)
        gdb = sb("a_gdb", [128, D], BF16)
        bupb = sb("a_bupb", [128, D], F32)
        st1 = {k: sb(f"a_s_{k}", [128, 1], F32 if k not in ("ia", "ib") else I32)[:]
               for k in ("mu", "var", "rstd", "nmr", "t1", "t2", "ia", "ib")}
        st1["sq"] = sq[:]

        tp = ps.enter_context(nc.psum_tensor(uq + "a_tp", [128, 2048], BF16))
        pb = ps.enter_context(nc.psum_tensor(uq + "a_pb", [128, 2048], F32))
        p4 = ps.enter_context(nc.psum_tensor(uq + "a_p4", [128, 512], F32))

        g = nc.gpsimd
        C.inc(g.dma_start(winf[:], W["winp"][:].rearrange("p a b -> p (a b)")), "lw", 16)
        C.inc(g.dma_start(winb[:], W["winp"][:].rearrange("p a b -> p (a b)")), "lw", 16)
        C.inc(g.dma_start(wdnf[:], W["wdnp"][:].rearrange("p a b -> p (a b)")), "lw", 16)
        C.inc(g.dma_start(wdnb[:], W["wdnp"][:].rearrange("p a b -> p (a b)")), "lw", 16)
        C.inc(g.dma_start(gb_f[:, 0:16], W["gbp"][0]), "lw", 16)
        C.inc(g.dma_start(gb_f[:, 16:32], W["gbp"][1]), "lw", 16)
        C.inc(g.dma_start(bb_bf[:], W["gbp"][1]), "lw", 16)
        C.inc(g.dma_start(woutb[:], W["wout"][:]), "lw", 16)
        C.inc(g.dma_start(wupb[:], W["wup"][:]), "lw", 16)
        C.inc(g.dma_start(dlg[:], W["dlg"][:]), "lw", 16)
        C.inc(g.dma_start(binp[:], W["bin"][:]), "lw", 16)
        C.inc(g.dma_start(bdnp[:], W["bdn"][:]), "lw", 16)
        C.inc(g.dma_start(r_bout[:], W["rows"][0:1, :]), "lw", 16)
        C.inc(g.dma_start(r_dsk[:], W["rows"][1:2, :]), "lw", 16)
        C.inc(g.dma_start(r_bup[:], W["rows"][2:3, :]), "lw", 16)
        C.inc(g.dma_start(ones_bf[:], W["ones"][:]), "lw", 16)
        C.wait(nc.vector, "lw")
        C.wait(nc.scalar, "lw")
        C.wait(nc.tensor, "lw")
        g16 = gb_f[:, 0:16]
        nc.vector.tensor_tensor(
            winpb[:].rearrange("p (a b) -> p a b", a=16),
            winf[:].rearrange("p (a b) -> p a b", a=16),
            g16.unsqueeze(2).broadcast_to([128, 16, 8]), ALU.mult)
        nc.vector.tensor_tensor(
            wdnpb[:].rearrange("p (a b) -> p a b", a=16),
            wdnf[:].rearrange("p (a b) -> p a b", a=16),
            g16.unsqueeze(2).broadcast_to([128, 16, 32]), ALU.mult)
        C.inc(nc.scalar.activation(dec[:], dlg[:], AF.Sigmoid), "ac", 1)
        instr = None
        for k in range(16):
            instr = nc.tensor.matmul(p4[0:8, 0:1], winb[:, k * 8:(k + 1) * 8],
                                     bb_bf[:, k:k + 1], start=(k == 0), stop=(k == 15))
        C.inc(instr, "pe", 1)
        C.wait(nc.vector, "pe")
        C.inc(nc.vector.tensor_tensor(bin1[:], p4[0:8, 0:1], binp[:], ALU.add), "dv", 1)
        C.wait(nc.tensor, "dv")
        instr = None
        for k in range(16):
            instr = nc.tensor.matmul(p4[0:32, 1:2], wdnb[:, k * 32:(k + 1) * 32],
                                     bb_bf[:, k:k + 1], start=(k == 0), stop=(k == 15))
        C.inc(instr, "pe", 1)
        C.wait(nc.vector, "pe")
        C.inc(nc.vector.tensor_tensor(bdn1[:], p4[0:32, 1:2], bdnp[:], ALU.add), "dv", 1)
        bcasts = [("bout", boutb, None), ("gd", gdb, None), ("bup", bupb, None)]
        r_gam = sb("a_rgam", [1, D], F32)
        r_bet = sb("a_rbet", [1, D], F32)
        C.inc(g.dma_start(r_gam[:], W["rows"][3:4, :]), "lw", 16)
        C.inc(g.dma_start(r_bet[:], W["rows"][4:5, :]), "lw", 16)
        C.wait(nc.vector, "lw")
        for name, dstb, _ in bcasts:
            if name == "bout":
                nc.vector.tensor_mul(r_t[:], r_bet[:], r_dsk[:])
                nc.vector.drain()
                nc.vector.tensor_add(r_t[:], r_t[:], r_bout[:])
            elif name == "gd":
                nc.vector.tensor_mul(r_t[:], r_gam[:], r_dsk[:])
            else:
                nc.vector.tensor_copy(r_t[:], r_bup[:])
            nc.vector.drain()
            C.inc(nc.vector.tensor_copy(r_bf[:], r_t[:]), "dv", 1)
            C.wait(nc.tensor, "dv")
            instr = None
            for cc in range(4):
                instr = nc.tensor.matmul(pb[:, cc * 512:(cc + 1) * 512], ones_bf[:],
                                         r_bf[:, cc * 512:(cc + 1) * 512],
                                         start=True, stop=True)
            C.inc(instr, "pe", 1)
            C.wait(nc.vector, "pe")
            C.inc(nc.vector.tensor_copy(dstb[:], pb[:]), "dv", 1)
            C.wait(nc.tensor, "dv")

        for b in range(B_CORE):
            for t in range(16):
                row = b * S + t * 128
                C.wait(nc.sync, "dv")
                C.wait(nc.sync, "ac")
                C.inc(nc.sync.dma_start(xt[:], x_in[row:row + 128, :]), "lx", 16)
                C.wait(nc.vector, "lx")
                C.wait(nc.scalar, "lx")
                ln_tile(C, xt[:], y0f[:], st1)
                C.wait(nc.vector, "ac")
                ln_tile(C, y0f[:], n2bf[:], st1)
                C.wait(nc.tensor, "ac")
                C.wait(nc.tensor, "dv")
                instr = None
                for k in range(16):
                    instr = nc.tensor.transpose(tp[:, k * 128:(k + 1) * 128],
                                                n2bf[:, k * 128:(k + 1) * 128],
                                                C.idn_bf[:])
                C.inc(instr, "pe", 1)
                C.wait(nc.vector, "pe")
                C.inc(nc.vector.tensor_copy(nT[:], tp[:]), "dv", 1)
                C.wait(nc.tensor, "dv")
                instr = None
                for k in range(16):
                    instr = nc.tensor.matmul(p4[0:8, 0:128],
                                             winpb[:, k * 8:(k + 1) * 8],
                                             nT[:, k * 128:(k + 1) * 128],
                                             start=(k == 0), stop=(k == 15))
                C.inc(instr, "pe", 1)
                C.wait(nc.vector, "pe")
                C.inc(nc.vector.tensor_scalar(u_sb[:, t * 128:(t + 1) * 128],
                                              p4[0:8, 0:128], bin1[:], None,
                                              ALU.add), "dv", 1)
            C.wait(nc.vector, "dv")
            nc.vector.tensor_scalar(decb[:], u_sb[:], 0.0, dec[:], ALU.mult, ALU.add)
            nc.vector.drain()
            nc.vector.tensor_tensor_scan(hs[:], decb[:], u_sb[:], 0.0,
                                         ALU.mult, ALU.add)
            nc.vector.drain()
            C.inc(nc.vector.tensor_copy(hsb[:], hs[:]), "dv", 1)

            for t in range(16):
                row = b * S + t * 128
                C.wait(nc.sync, "dv")
                C.wait(nc.sync, "ac")
                C.inc(nc.sync.dma_start(xt[:], x_in[row:row + 128, :]), "lx", 16)
                C.wait(nc.vector, "lx")
                C.wait(nc.scalar, "lx")
                ln_tile(C, xt[:], y0f[:], st1)
                C.wait(nc.vector, "ac")
                ln_tile(C, y0f[:], n2bf[:], st1)
                C.wait(nc.tensor, "dv")
                C.wait(nc.tensor, "ac")
                instr = None
                for cc in range(4):
                    instr = nc.tensor.matmul(pb[:, cc * 512:(cc + 1) * 512],
                                             hsb[:, t * 128:(t + 1) * 128],
                                             woutb[:, cc * 512:(cc + 1) * 512],
                                             start=True, stop=True)
                C.inc(instr, "pe", 1)
                C.wait(nc.vector, "pe")
                nc.vector.tensor_tensor(t4[:], n2bf[:], gdb[:], ALU.mult)
                nc.vector.drain()
                nc.vector.tensor_add(t4[:], t4[:], y0f[:])
                nc.vector.drain()
                nc.vector.tensor_add(t4[:], t4[:], boutb[:])
                nc.vector.drain()
                C.inc(nc.vector.scalar_tensor_tensor(zf[:], pb[:], 1.0, t4[:],
                                                     ALU.mult, ALU.add), "dv", 1)
                C.wait(nc.scalar, "dv")
                C.wait(nc.vector, "dv")
                ln_tile(C, zf[:], n2bf[:], st1)
                C.wait(nc.tensor, "ac")
                C.wait(nc.tensor, "dv")
                instr = None
                for k in range(16):
                    instr = nc.tensor.transpose(tp[:, k * 128:(k + 1) * 128],
                                                n2bf[:, k * 128:(k + 1) * 128],
                                                C.idn_bf[:])
                C.inc(instr, "pe", 1)
                C.wait(nc.vector, "pe")
                C.inc(nc.vector.tensor_copy(nT[:], tp[:]), "dv", 1)
                C.wait(nc.tensor, "dv")
                instr = None
                for k in range(16):
                    instr = nc.tensor.matmul(p4[0:32, 0:128],
                                             wdnpb[:, k * 32:(k + 1) * 32],
                                             nT[:, k * 128:(k + 1) * 128],
                                             start=(k == 0), stop=(k == 15))
                C.inc(instr, "pe", 1)
                C.wait(nc.scalar, "pe")
                C.inc(nc.scalar.activation(hfbf[:], p4[0:32, 0:128], AF.Relu,
                                           bias=bdn1[:]), "ac", 1)
                C.wait(nc.tensor, "ac")
                instr = None
                for cc in range(4):
                    instr = nc.tensor.matmul(pb[:, cc * 512:(cc + 1) * 512],
                                             hfbf[:],
                                             wupb[:, cc * 512:(cc + 1) * 512],
                                             start=True, stop=True)
                C.inc(instr, "pe", 1)
                C.wait(nc.vector, "pe")
                nc.vector.tensor_add(t4[:], zf[:], bupb[:])
                nc.vector.drain()
                nc.vector.tensor_add(t4[:], t4[:], xt[:])
                nc.vector.drain()
                C.inc(nc.vector.scalar_tensor_tensor(xt[:], pb[:], 1.0, t4[:],
                                                     ALU.mult, ALU.add), "dv", 1)
                C.wait(nc.sync, "dv")
                C.inc(nc.sync.dma_start(x_out[row:row + 128, :], xt[:]), "sx", 16)
                C.wait(nc.sync, "sx")

        C.wait(nc.sync, "sx")



def emit_l1a(C, x_in, x_out, W, feats_scr):
    nc = C.nc
    uq = C.uniq()
    from contextlib import ExitStack
    with ExitStack() as ps:
        sb = lambda n, sh, dt: ps.enter_context(nc.sbuf_tensor(uq + n, sh, dt))
        xt = sb("q_xt", [128, D], F32)
        ybf = sb("q_ybf", [128, D], BF16)
        sq = sb("q_sq", [128, D], F32)
        st1 = {k: sb(f"q_s_{k}", [128, 1], F32 if k not in ("ia", "ib") else I32)[:]
               for k in ("mu", "var", "rstd", "nmr", "t1", "t2", "ia", "ib")}
        st1["sq"] = sq[:]
        bqkp = sb("q_bqkp", [128, 32], F32)
        r_bv = sb("q_rbv", [1, D], F32)
        r_bo = sb("q_rbo", [1, D], F32)
        r_bf = sb("q_rbf", [1, D], BF16)
        ones_bf = sb("q_ones", [1, 128], BF16)
        scl = sb("q_scl", [128, 1], F32)
        bvb = sb("q_bvb", [128, D], F32)
        bob = sb("q_bob", [128, D], F32)

        psc = [ps.enter_context(nc.psum_tensor(f"{uq}q_psc{i}", [128, 1024], F32))
               for i in range(2)]
        pf = ps.enter_context(nc.psum_tensor(uq + "q_pf", [96, 1024], F32))
        pnorm = ps.enter_context(nc.psum_tensor(uq + "q_pn", [128, 1024], BF16))
        prs = ps.enter_context(nc.psum_tensor(uq + "q_prs", [128, 512], BF16))

        g = nc.gpsimd
        C.inc(g.dma_start(bqkp[:, 0:16], W["bqp"][:]), "lw", 16)
        C.inc(g.dma_start(bqkp[:, 16:32], W["bkp"][:]), "lw", 16)
        C.inc(g.dma_start(r_bv[:], W["bv_row"][:]), "lw", 16)
        C.inc(g.dma_start(r_bo[:], W["bo_row"][:]), "lw", 16)
        C.inc(g.dma_start(ones_bf[:], W["ones"][:]), "lw", 16)
        if DEBUG_L1A:
            C.dbg_qT = nc.dram_tensor("dbg_qT", [128, 2 * 2048], BF16, kind="ExternalOutput").ap()
            C.dbg_kT = nc.dram_tensor("dbg_kT", [128, 2 * 2048], BF16, kind="ExternalOutput").ap()
            C.dbg_Vb = nc.dram_tensor("dbg_Vb", [128, 16 * 384], BF16, kind="ExternalOutput").ap()
            C.dbg_fm = nc.dram_tensor("dbg_fm", [64, 2048], BF16, kind="ExternalOutput").ap()
            C.dbg_rs = nc.dram_tensor("dbg_rs", [32, 2048], BF16, kind="ExternalOutput").ap()
            C.dbg_rsT = nc.dram_tensor("dbg_rsT", [128, 16], F32, kind="ExternalOutput").ap()
            C.dbg_rcT = nc.dram_tensor("dbg_rcT", [128, 16], BF16, kind="ExternalOutput").ap()
            C.dbg_ftg = nc.dram_tensor("dbg_ftg", [128, 16 * 256], BF16, kind="ExternalOutput").ap()
        C.wait(nc.vector, "lw")
        nc.vector.memset(scl[:], 1.0 / np.sqrt(32.0))
        nc.vector.drain()
        C.inc(nc.vector.tensor_copy(scl[:], scl[:]), "dv", 1)
        C.wait(nc.scalar, "dv")
        C.wait(nc.tensor, "dv")
        for row, dstb in ((r_bv, bvb), (r_bo, bob)):
            C.inc(nc.vector.tensor_copy(r_bf[:], row[:]), "dv", 1)
            C.wait(nc.tensor, "dv")
            instr = None
            for cc in range(2):
                instr = nc.tensor.matmul(psc[0][:, cc * 512:(cc + 1) * 512], ones_bf[:],
                                         r_bf[:, cc * 512:(cc + 1) * 512],
                                         start=True, stop=True)
                instr = nc.tensor.matmul(psc[1][:, cc * 512:(cc + 1) * 512], ones_bf[:],
                                         r_bf[:, 1024 + cc * 512:1024 + (cc + 1) * 512],
                                         start=True, stop=True)
            C.inc(instr, "pe", 1)
            C.wait(nc.vector, "pe")
            nc.vector.tensor_copy(dstb[:, 0:1024], psc[0][:])
            C.inc(nc.vector.tensor_copy(dstb[:, 1024:2048], psc[1][:]), "dv", 1)
            C.wait(nc.tensor, "dv")

        for b in range(B_CORE):
            with ExitStack() as bs:
                yT = bs.enter_context(nc.sbuf_tensor(f"{uq}q_yT_{b}", [128, 16 * 2048], BF16))
                qT = bs.enter_context(nc.sbuf_tensor(f"{uq}q_qT_{b}", [128, 2 * 2048], BF16))
                kT = bs.enter_context(nc.sbuf_tensor(f"{uq}q_kT_{b}", [128, 2 * 2048], BF16))
                Vb = bs.enter_context(nc.sbuf_tensor(f"{uq}q_Vb_{b}", [128, 16 * 384], BF16))
                pt = [bs.enter_context(nc.sbuf_tensor(f"{uq}q_pt{i}_{b}", [128, 1024], BF16))
                      for i in range(2)]
                fm = bs.enter_context(nc.sbuf_tensor(f"{uq}q_fm_{b}", [64, 2048], BF16))
                rs32 = bs.enter_context(nc.sbuf_tensor(f"{uq}q_rs32_{b}", [32, 2048], BF16))
                rsT = bs.enter_context(nc.sbuf_tensor(f"{uq}q_rsT_{b}", [128, 16], F32))
                rcT = bs.enter_context(nc.sbuf_tensor(f"{uq}q_rcT_{b}", [128, 16], F32))
                rcTb = bs.enter_context(nc.sbuf_tensor(f"{uq}q_rcTb_{b}", [128, 16], BF16))
                ftg = bs.enter_context(nc.sbuf_tensor(f"{uq}q_ftg_{b}", [128, 16 * 256], BF16))
                wq_b = bs.enter_context(nc.sbuf_tensor(f"{uq}q_wq_{b}", [128, 16 * 256], BF16))
                wk_b = bs.enter_context(nc.sbuf_tensor(f"{uq}q_wk_{b}", [128, 16 * 256], BF16))
                wv_b = bs.enter_context(nc.sbuf_tensor(f"{uq}q_wv_{b}", [128, 16 * 256], BF16))
                yT3 = yT[:].rearrange("p (k s) -> p k s", k=16)

                for t in range(16):
                    row = b * S + t * 128
                    C.wait(nc.sync, "ac")
                    C.inc(nc.sync.dma_start(xt[:], x_in[row:row + 128, :]), "lx", 16)
                    C.wait(nc.vector, "lx")
                    C.wait(nc.scalar, "lx")
                    ln_tile(C, xt[:], ybf[:], st1)
                    C.wait(nc.tensor, "ac")
                    C.wait(nc.tensor, "dv")
                    for grp in range(2):
                        instr = None
                        for k in range(8):
                            kk = grp * 8 + k
                            instr = nc.tensor.transpose(
                                pnorm[:, k * 128:(k + 1) * 128],
                                ybf[:, kk * 128:(kk + 1) * 128], C.idn_bf[:])
                        C.inc(instr, "pe", 1)
                        C.wait(nc.vector, "pe")
                        C.inc(nc.vector.tensor_copy(
                            yT3[:, grp * 8:(grp + 1) * 8, t * 128:(t + 1) * 128],
                            pnorm[:].rearrange("p (k m) -> p k m", k=8)), "dv", 1)
                        C.wait(nc.tensor, "dv")

                for gi in range(8):
                    col = gi * 256
                    C.wait(nc.gpsimd, "pe")
                    for wb_, wsrc in ((wq_b, W["wq"]), (wk_b, W["wk"]), (wv_b, W["wv"])):
                        C.inc(g.dma_start(
                            wb_[:],
                            bass.AP(wsrc.tensor, col,
                                    [[D, 128], [128 * D, 16], [1, 256]])), "lw", 16)
                    C.wait(nc.tensor, "lw")
                    for dst, wb_, bcol in ((qT, wq_b, 0), (kT, wk_b, 16)):
                        for mc in range(2):
                            for ns in range(4):
                                instr = None
                                for kc in range(16):
                                    instr = nc.tensor.matmul(
                                        psc[0][:, 0:512],
                                        wb_[:, kc * 256 + mc * 128: kc * 256 + (mc + 1) * 128],
                                        yT[:, kc * 2048 + ns * 512: kc * 2048 + (ns + 1) * 512],
                                        start=(kc == 0), stop=(kc == 15))
                                C.inc(instr, "pe", 1)
                                C.wait(nc.scalar, "pe")
                                C.inc(nc.scalar.activation(
                                    dst[:, mc * 2048 + ns * 512: mc * 2048 + (ns + 1) * 512],
                                    psc[0][:, 0:512], AF.Identity,
                                    bias=bqkp[:, bcol + 2 * gi + mc: bcol + 2 * gi + mc + 1]),
                                    "ac", 1)
                                C.wait(nc.tensor, "ac")
                    C.wait(nc.vector, "ac")
                    C.inc(nc.vector.memset(Vb[:], 1.0), "dv", 1)
                    C.wait(nc.tensor, "dv")
                    for t in range(16):
                        instr = None
                        for kc in range(16):
                            instr = nc.tensor.matmul(
                                psc[1][:, 0:256],
                                yT[:, kc * 2048 + t * 128: kc * 2048 + (t + 1) * 128],
                                wv_b[:, kc * 256:(kc + 1) * 256],
                                start=(kc == 0), stop=(kc == 15))
                        C.inc(instr, "pe", 1)
                        C.wait(nc.vector, "pe")
                        C.inc(nc.vector.tensor_tensor(
                            Vb[:].rearrange("p (t h m) -> p t h m", t=16, h=4)
                            [:, t, :, 0:64].rearrange("p h m -> p (h m)")
                            if False else
                            bass.AP(Vb, Vb[:].offset + t * 384,
                                    [[Vb[:].ap[0][0], 128], [96, 4], [1, 64]]),
                            psc[1][:, 0:256].rearrange("p (h m) -> p h m", h=4),
                            bvb[:, col:col + 256].rearrange("p (h m) -> p h m", h=4),
                            ALU.add), "dv", 1)
                        C.wait(nc.tensor, "dv")
                    if DEBUG_L1A and b == 0 and gi == 0:
                        C.wait(nc.sync, "dv")
                        C.wait(nc.sync, "ac")
                        C.inc(nc.sync.dma_start(C.dbg_qT[:], qT[:]), "sx", 16)
                        C.inc(nc.sync.dma_start(C.dbg_kT[:], kT[:]), "sx", 16)
                        C.inc(nc.sync.dma_start(C.dbg_Vb[:], Vb[:]), "sx", 16)
                        C.wait(nc.sync, "sx")
                    C.wait(nc.tensor, "dv")
                    for h in range(4):
                        mc = h // 2
                        kb = 64 * (h % 2)
                        for sh in range(2):
                            exp_ac = []
                            scores_pe = []
                            ac_base = C.n["ac"]

                            def emit_scores(tc):
                                C.wait(nc.tensor, "ac",
                                       exp_ac[tc - 2] if tc >= 2 else ac_base)
                                instr = None
                                for nn in range(2):
                                    instr = nc.tensor.matmul(
                                        psc[tc % 2][:, nn * 512:(nn + 1) * 512],
                                        kT[kb:kb + 64,
                                           mc * 2048 + tc * 128: mc * 2048 + (tc + 1) * 128],
                                        qT[kb:kb + 64,
                                           mc * 2048 + sh * 1024 + nn * 512:
                                           mc * 2048 + sh * 1024 + (nn + 1) * 512],
                                        start=True, stop=True)
                                C.inc(instr, "pe", 1)
                                scores_pe.append(C.n["pe"])

                            def emit_exp(tc):
                                C.wait(nc.scalar, "pe", scores_pe[tc])
                                C.inc(nc.scalar.activation(pt[tc % 2][:], psc[tc % 2][:],
                                                           AF.Exp, scale=scl[:]),
                                      "ac", 1)
                                exp_ac.append(C.n["ac"])

                            def emit_pv(tc):
                                C.wait(nc.tensor, "ac", exp_ac[tc])
                                instr = None
                                for nn in range(2):
                                    instr = nc.tensor.matmul(
                                        pf[:, nn * 512:(nn + 1) * 512],
                                        Vb[:, tc * 384 + h * 96: tc * 384 + (h + 1) * 96],
                                        pt[tc % 2][:, nn * 512:(nn + 1) * 512],
                                        start=(tc == 0), stop=(tc == 15))
                                C.inc(instr, "pe", 1)

                            for tc in range(16):
                                emit_scores(tc)
                                emit_exp(tc)
                                if tc >= 1:
                                    emit_pv(tc - 1)
                            emit_pv(15)
                            C.wait(nc.vector, "pe")
                            nc.vector.tensor_copy(fm[:, sh * 1024:(sh + 1) * 1024],
                                                  pf[0:64, :])
                            C.inc(nc.vector.tensor_copy(rs32[:, sh * 1024:(sh + 1) * 1024],
                                                        pf[64:96, :]), "dv", 1)
                            C.wait(nc.tensor, "dv")
                        for grp in range(2):
                            instr = None
                            for t in range(8):
                                tt = grp * 8 + t
                                instr = nc.tensor.transpose(
                                    pnorm[:, t * 128:t * 128 + 64],
                                    fm[:, tt * 128:(tt + 1) * 128],
                                    C.idn_bf[0:64, 0:64])
                            C.inc(instr, "pe", 1)
                            C.wait(nc.vector, "pe")
                            C.inc(nc.vector.tensor_copy(
                                bass.AP(ftg,
                                        ftg[:].offset + grp * 8 * 256 + h * 64,
                                        [[ftg[:].ap[0][0], 128], [256, 8], [1, 64]]),
                                pnorm[:].rearrange("p (t m) -> p t m", t=8)[:, :, 0:64]),
                                "dv", 1)
                            C.wait(nc.tensor, "dv")
                        if DEBUG_L1A and b == 0 and gi == 0 and h == 0:
                            C.wait(nc.sync, "dv")
                            C.inc(nc.sync.dma_start(C.dbg_fm[:], fm[:]), "sx", 16)
                            C.inc(nc.sync.dma_start(C.dbg_rs[:], rs32[:]), "sx", 16)
                            C.wait(nc.sync, "sx")
                        instr = None
                        for t in range(16):
                            instr = nc.tensor.transpose(
                                prs[:, t * 32:(t + 1) * 32],
                                rs32[:, t * 128:(t + 1) * 128],
                                C.idn_bf[0:32, 0:32])
                        C.inc(instr, "pe", 1)
                        C.wait(nc.vector, "pe")
                        nc.vector.tensor_copy(
                            rsT[:], prs[:].rearrange("p (t m) -> p t m", t=16)[:, :, 0:1]
                            .rearrange("p t m -> p (t m)"))
                        nc.vector.drain()
                        nc.vector.reciprocal(rcT[:], rsT[:])
                        nc.vector.drain()
                        C.inc(nc.vector.tensor_copy(rcTb[:], rcT[:]), "dv", 1)
                        nc.vector.drain()
                        if DEBUG_L1A and b == 0 and gi == 0 and h == 0:
                            C.wait(nc.sync, "dv")
                            C.inc(nc.sync.dma_start(C.dbg_rsT[:], rsT[:]), "sx", 16)
                            C.inc(nc.sync.dma_start(C.dbg_rcT[:], rcTb[:]), "sx", 16)
                            C.wait(nc.sync, "sx")
                        C.inc(nc.vector.tensor_tensor(
                            bass.AP(ftg, ftg[:].offset + h * 64,
                                    [[ftg[:].ap[0][0], 128], [256, 16], [1, 64]]),
                            bass.AP(ftg, ftg[:].offset + h * 64,
                                    [[ftg[:].ap[0][0], 128], [256, 16], [1, 64]]),
                            rcTb[:].unsqueeze(2).broadcast_to([128, 16, 64]),
                            ALU.mult), "dv", 1)
                        C.wait(nc.tensor, "dv")
                    if DEBUG_L1A and b == 0 and gi == 0:
                        C.wait(nc.sync, "dv")
                        C.inc(nc.sync.dma_start(C.dbg_ftg[:], ftg[:]), "sx", 16)
                        C.wait(nc.sync, "sx")
                    C.wait(nc.sync, "dv")
                    C.inc(nc.sync.dma_start(
                        bass.AP(feats_scr.tensor, b * S * D + col,
                                [[D, 128], [128 * D, 16], [1, 256]]),
                        ftg[:].rearrange("p (t m) -> p t m", t=16)), "sx", 16)
                    C.wait(nc.sync, "sx")
                    C.wait(nc.vector, "sx")

            with ExitStack() as bs:
                ftT = bs.enter_context(nc.sbuf_tensor(f"{uq}q_ftT_{b}", [128, 16 * 2048], BF16))
                wo_cc = bs.enter_context(nc.sbuf_tensor(f"{uq}q_wocc_{b}", [128, 16 * 512], BF16))
                ft_t = bs.enter_context(nc.sbuf_tensor(f"{uq}q_ftt_{b}", [128, 2048], BF16))
                xsl = bs.enter_context(nc.sbuf_tensor(f"{uq}q_xsl_{b}", [128, 512], F32))
                osl = bs.enter_context(nc.sbuf_tensor(f"{uq}q_osl_{b}", [128, 512], F32))
                fT3 = ftT[:].rearrange("p (k s) -> p k s", k=16)
                for t in range(16):
                    row = b * S + t * 128
                    C.wait(nc.sync, "dv")
                    C.inc(nc.sync.dma_start(ft_t[:], feats_scr[row:row + 128, :]),
                          "lx", 16)
                    C.wait(nc.tensor, "lx")
                    C.wait(nc.tensor, "dv")
                    for grp in range(2):
                        instr = None
                        for k in range(8):
                            kk = grp * 8 + k
                            instr = nc.tensor.transpose(
                                pnorm[:, k * 128:(k + 1) * 128],
                                ft_t[:, kk * 128:(kk + 1) * 128], C.idn_bf[:])
                        C.inc(instr, "pe", 1)
                        C.wait(nc.vector, "pe")
                        C.inc(nc.vector.tensor_copy(
                            fT3[:, grp * 8:(grp + 1) * 8, t * 128:(t + 1) * 128],
                            pnorm[:].rearrange("p (k m) -> p k m", k=8)), "dv", 1)
                        C.wait(nc.tensor, "dv")
                for cc in range(4):
                    C.wait(nc.gpsimd, "pe")
                    C.inc(g.dma_start(
                        wo_cc[:],
                        bass.AP(W["wo"].tensor, cc * 512,
                                [[D, 128], [128 * D, 16], [1, 512]])), "lw", 16)
                    C.wait(nc.tensor, "lw")
                    for t in range(16):
                        row = b * S + t * 128
                        C.wait(nc.sync, "dv")
                        C.inc(nc.sync.dma_start(
                            xsl[:], x_in[row:row + 128, cc * 512:(cc + 1) * 512]),
                            "lx", 16)
                        C.wait(nc.tensor, "lx")
                        instr = None
                        for kc in range(16):
                            instr = nc.tensor.matmul(
                                psc[0][:, 0:512],
                                fT3[:, kc, t * 128:(t + 1) * 128],
                                wo_cc[:, kc * 512:(kc + 1) * 512],
                                start=(kc == 0), stop=(kc == 15))
                        C.inc(instr, "pe", 1)
                        C.wait(nc.vector, "pe")
                        C.wait(nc.vector, "lx")
                        nc.vector.tensor_add(osl[:], xsl[:], bob[:, cc * 512:(cc + 1) * 512])
                        nc.vector.drain()
                        C.inc(nc.vector.scalar_tensor_tensor(
                            osl[:], psc[0][:, 0:512], 1.0, osl[:], ALU.mult, ALU.add),
                            "dv", 1)
                        C.wait(nc.sync, "dv")
                        C.inc(nc.sync.dma_start(
                            x_out[row:row + 128, cc * 512:(cc + 1) * 512], osl[:]),
                            "sx", 16)
                        C.wait(nc.sync, "sx")

        C.wait(nc.sync, "sx")



PHASES_ALL = ["l0f_a", "l0a", "l0f_b", "l1f_a", "l1a", "l1f_b"]


def build_program(phases=None):
    phases = phases or PHASES_ALL
    from contextlib import ExitStack
    es = ExitStack()
    nc = bass.Bass("TRN2", target_bir_lowering=False)

    x_ext = nc.dram_tensor("x", [NTOK, D], F32, kind="ExternalInput").ap()
    out = nc.dram_tensor("out", [NTOK, D], F32, kind="ExternalOutput").ap()

    need = set(phases)
    inputs_used = ["x", "c_ident_bf"]

    ident_bf = nc.dram_tensor("c_ident_bf", [128, 128], BF16, kind="ExternalInput").ap()

    if need & {"l1f_a", "l1f_b"}:
        if SKIP_PREP:
            wgT = nc.dram_tensor("l1f_wgTb", [64 * 128, D], BF16, kind="ExternalInput").ap()
            wuT = nc.dram_tensor("l1f_wuTb", [64 * 128, D], BF16, kind="ExternalInput").ap()
            wdT = nc.dram_tensor("l1f_wdTb", [8192, D], BF16, kind="ExternalInput").ap()
            inputs_used += ["l1f_wgTb", "l1f_wuTb", "l1f_wdTb"]
        else:
            wgT_f = nc.dram_tensor("l1f_wgT", [64 * 128, D], F32, kind="ExternalInput").ap()
            wuT_f = nc.dram_tensor("l1f_wuT", [64 * 128, D], F32, kind="ExternalInput").ap()
            wdT_f = nc.dram_tensor("l1f_wdT", [8192, D], F32, kind="ExternalInput").ap()
            wgT = nc.dram_tensor("s_wgT", [64 * 128, D], BF16, kind="Internal").ap()
            wuT = nc.dram_tensor("s_wuT", [64 * 128, D], BF16, kind="Internal").ap()
            wdT = nc.dram_tensor("s_wdT", [8192, D], BF16, kind="Internal").ap()
            inputs_used += ["l1f_wgT", "l1f_wuT", "l1f_wdT"]

    if need & {"l0f_a", "l0f_b"}:
        wg8p = nc.dram_tensor("l0f_wg8p", [128, 16, 8], F32, kind="ExternalInput").ap()
        oh8p = nc.dram_tensor("c_oh8", [8, 8 * 128], BF16, kind="ExternalInput").ap()
        wi01_f = nc.dram_tensor("l0f_wi01", [8, 8, 128, 2 * D], F32, kind="ExternalInput").ap()
        wo_f = nc.dram_tensor("l0f_wo", [8, 1024, D], F32, kind="ExternalInput").ap()
        wi01 = nc.dram_tensor("s_wi01", [8, 8, 128, 2 * D], BF16, kind="Internal").ap()
        wos = nc.dram_tensor("s_wo", [8, 1024, D], BF16, kind="Internal").ap()
        inputs_used += ["l0f_wg8p", "c_oh8", "l0f_wi01", "l0f_wo"]

    if "l0a" in need:
        l0a_W = {
            "winp": nc.dram_tensor("l0a_winp", [128, 16, 8], F32, kind="ExternalInput").ap(),
            "wdnp": nc.dram_tensor("l0a_wdnp", [128, 16, 32], F32, kind="ExternalInput").ap(),
            "gbp": nc.dram_tensor("l0a_gbp", [2, 128, 16], F32, kind="ExternalInput").ap(),
            "wout": nc.dram_tensor("l0a_wout", [8, D], F32, kind="ExternalInput").ap(),
            "wup": nc.dram_tensor("l0a_wup", [32, D], F32, kind="ExternalInput").ap(),
            "dlg": nc.dram_tensor("l0a_dlg", [8, 1], F32, kind="ExternalInput").ap(),
            "bin": nc.dram_tensor("l0a_bin", [8, 1], F32, kind="ExternalInput").ap(),
            "bdn": nc.dram_tensor("l0a_bdn", [32, 1], F32, kind="ExternalInput").ap(),
            "rows": nc.dram_tensor("l0a_rows", [5, D], F32, kind="ExternalInput").ap(),
            "ones": nc.dram_tensor("c_ones_bf", [1, 128], BF16, kind="ExternalInput").ap(),
        }
        inputs_used += ["l0a_winp", "l0a_wdnp", "l0a_gbp", "l0a_wout", "l0a_wup",
                        "l0a_dlg", "l0a_bin", "l0a_bdn", "l0a_rows", "c_ones_bf"]

    if "l1a" in need:
        l1a_W = {
            "wq": nc.dram_tensor("l1a_wq", [D, D], F32, kind="ExternalInput").ap(),
            "wk": nc.dram_tensor("l1a_wk", [D, D], F32, kind="ExternalInput").ap(),
            "wv": nc.dram_tensor("l1a_wv", [D, D], F32, kind="ExternalInput").ap(),
            "wo": nc.dram_tensor("l1a_wo", [D, D], F32, kind="ExternalInput").ap(),
            "bqp": nc.dram_tensor("l1a_bqp", [128, 16], F32, kind="ExternalInput").ap(),
            "bkp": nc.dram_tensor("l1a_bkp", [128, 16], F32, kind="ExternalInput").ap(),
            "bv_row": nc.dram_tensor("l1a_bv", [1, D], F32, kind="ExternalInput").ap(),
            "bo_row": nc.dram_tensor("l1a_bo", [1, D], F32, kind="ExternalInput").ap(),
            "ones": nc.dram_tensor("c_ones1a", [1, 128], BF16, kind="ExternalInput").ap(),
        }
        feats_scr = nc.dram_tensor("s_feats", [NTOK, D], BF16,
                                   kind="ExternalOutput" if DEBUG_FEATS else "Internal").ap()
        if DEBUG_L1A:
            pass
        inputs_used += ["l1a_wq", "l1a_wk", "l1a_wv", "l1a_wo", "l1a_bqp",
                        "l1a_bkp", "l1a_bv", "l1a_bo", "c_ones1a"]

    xA = nc.dram_tensor("s_xA", [NTOK, D], F32, kind="Internal").ap()
    xB = nc.dram_tensor("s_xB", [NTOK, D], F32, kind="Internal").ap()

    with es:
        C = Ctx(nc, es)
        C.idn_bf = es.enter_context(nc.sbuf_tensor("c_idn", [128, 128], BF16))
        C.inc(nc.sync.dma_start(C.idn_bf[:], ident_bf[:]), "lx", 16)
        if need & {"l0f_a", "l0f_b"}:
            C.oh8_bf = es.enter_context(nc.sbuf_tensor("c_oh8s", [8, 8 * 128], BF16))
            C.wg8_bf = es.enter_context(nc.sbuf_tensor("c_wg8s", [128, 128], BF16))
            C.inc(nc.sync.dma_start(C.oh8_bf[:], oh8p[:]), "lx", 16)
            C.inc(nc.gpsimd.dma_start(
                C.wg8_bf[:], wg8p[:].rearrange("p a b -> p (a b)")), "lw", 16)
            C.wait(nc.tensor, "lw")
        C.wait(nc.tensor, "lx")
        C.barrier()

        jobs = []
        if need & {"l1f_a", "l1f_b"} and not SKIP_PREP:
            jobs += [(wgT_f, wgT), (wuT_f, wuT), (wdT_f, wdT)]
        if need & {"l0f_a", "l0f_b"}:
            jobs += [(wi01_f.rearrange("a b p d -> (a b p) d"),
                      wi01.rearrange("a b p d -> (a b p) d")),
                     (wo_f.rearrange("a m d -> (a m) d"),
                      wos.rearrange("a m d -> (a m) d"))]
        if DUMMY_PREP:
            dmy = nc.dram_tensor("s_dmy", [NTOK, D], BF16, kind="Internal").ap()
            jobs += [(x_ext, dmy)] * 3
        if jobs:
            emit_prep(C, jobs)
            C.barrier()

        cur = x_ext
        nxt = [xA, xB]
        ni = 0
        for ph in phases:
            dst = out if ph == phases[-1] else nxt[ni]
            if ph in ("l1f_a", "l1f_b"):
                emit_l1ffn(C, cur, dst, wgT, wuT, wdT,
                           wsem="lw" if ph == "l1f_a" else "g2")
            elif ph in ("l0f_a", "l0f_b"):
                emit_moe(C, cur, dst, wi01, wos)
            elif ph == "l0a":
                emit_l0a(C, cur, dst, l0a_W)
            elif ph == "l1a":
                emit_l1a(C, cur, dst, l1a_W, feats_scr)
            else:
                raise NotImplementedError(ph)
            C.barrier()
            cur = dst
            ni ^= 1

    return nc, inputs_used



def swizzle_gu(w):
    wT = w.T.reshape(16, 128, 64, 128)
    return np.ascontiguousarray(wT.transpose(2, 1, 0, 3).reshape(64 * 128, 2048))


def swizzle_wi(w):
    w5 = w.reshape(8, 16, 128, 8, 128)
    return np.ascontiguousarray(
        w5.transpose(0, 3, 2, 1, 4).reshape(8, 8, 128, 2048))


def make_in_maps(inputs, inputs_used, phases=None):
    x_full = np.asarray(inputs["L0_attn_x"], dtype=np.float32)
    base = {}
    base["c_ident_bf"] = np.eye(128, dtype=BF)
    if "l1f_wgT" in inputs_used:
        base["l1f_wgT"] = swizzle_gu(np.asarray(inputs["L1_ffn_w_gate"], np.float32))
        base["l1f_wuT"] = swizzle_gu(np.asarray(inputs["L1_ffn_w_up"], np.float32))
        base["l1f_wdT"] = np.ascontiguousarray(
            np.asarray(inputs["L1_ffn_w_down"], np.float32).T)
    if "l0a_winp" in inputs_used:
        pk = lambda v, w: np.ascontiguousarray(
            np.asarray(v, np.float32).reshape(16, 128, w).transpose(1, 0, 2))
        base["l0a_winp"] = pk(inputs["L0_attn_W_in"], 8)
        base["l0a_wdnp"] = pk(inputs["L0_attn_W_down"], 32)
        gm = np.asarray(inputs["L0_attn_ln_gamma"], np.float32)
        bt = np.asarray(inputs["L0_attn_ln_beta"], np.float32)
        base["l0a_gbp"] = np.ascontiguousarray(np.stack(
            [gm.reshape(16, 128).T, bt.reshape(16, 128).T]))
        base["l0a_wout"] = np.ascontiguousarray(np.asarray(inputs["L0_attn_W_out"], np.float32))
        base["l0a_wup"] = np.ascontiguousarray(np.asarray(inputs["L0_attn_W_up"], np.float32))
        base["l0a_dlg"] = np.asarray(inputs["L0_attn_decay_logit"], np.float32).reshape(8, 1)
        base["l0a_bin"] = np.asarray(inputs["L0_attn_b_in"], np.float32).reshape(8, 1)
        base["l0a_bdn"] = np.asarray(inputs["L0_attn_b_down"], np.float32).reshape(32, 1)
        base["l0a_rows"] = np.ascontiguousarray(np.stack([
            np.asarray(inputs["L0_attn_b_out"], np.float32),
            np.asarray(inputs["L0_attn_D_skip"], np.float32),
            np.asarray(inputs["L0_attn_b_up"], np.float32),
            gm, bt]))
        base["c_ones_bf"] = np.ones((1, 128), dtype=BF)
    if "l1a_wq" in inputs_used:
        base["l1a_wq"] = np.ascontiguousarray(np.asarray(inputs["L1_attn_Wq"], np.float32))
        base["l1a_wk"] = np.ascontiguousarray(np.asarray(inputs["L1_attn_Wk"], np.float32))
        base["l1a_wv"] = np.ascontiguousarray(np.asarray(inputs["L1_attn_Wv"], np.float32))
        base["l1a_wo"] = np.ascontiguousarray(np.asarray(inputs["L1_attn_Wo"], np.float32))
        pk16 = lambda v: np.ascontiguousarray(
            np.asarray(v, np.float32).reshape(16, 128).T)
        base["l1a_bqp"] = pk16(inputs["L1_attn_bq"])
        base["l1a_bkp"] = pk16(inputs["L1_attn_bk"])
        base["l1a_bv"] = np.asarray(inputs["L1_attn_bv"], np.float32).reshape(1, D)
        base["l1a_bo"] = np.asarray(inputs["L1_attn_bo"], np.float32).reshape(1, D)
        base["c_ones1a"] = np.ones((1, 128), dtype=BF)
    if "l0f_wi01" in inputs_used:
        wg = np.asarray(inputs["L0_ffn_w_gate"], np.float32)
        base["l0f_wg8p"] = np.ascontiguousarray(
            wg.reshape(16, 128, 8).transpose(1, 0, 2))
        oh = np.zeros((8, 8, 128), dtype=BF)
        for e in range(8):
            oh[e, e, :] = 1
        base["c_oh8"] = np.ascontiguousarray(oh.transpose(1, 0, 2).reshape(8, 8 * 128))
        base["l0f_wi01"] = np.ascontiguousarray(np.concatenate(
            [swizzle_wi(np.asarray(inputs["L0_ffn_wi_0"], np.float32)),
             swizzle_wi(np.asarray(inputs["L0_ffn_wi_1"], np.float32))], axis=3))
        base["l0f_wo"] = np.ascontiguousarray(np.asarray(inputs["L0_ffn_wo"], np.float32))
    in_maps = []
    for c in range(8):
        m = dict(base)
        m["x"] = np.ascontiguousarray(
            x_full[c * B_CORE:(c + 1) * B_CORE].reshape(NTOK, D))
        in_maps.append(m)
    return in_maps


def kernel(**inputs):
    nc, inputs_used = build_program()
    in_maps = make_in_maps(inputs, inputs_used)
    res = run_bass_kernel_spmd(nc, in_maps, core_ids=list(range(8)))
    outs = [r["out"].reshape(B_CORE, S, D) for r in res.results]
    return np.concatenate(outs, axis=0).astype(np.float32)


# version 05 (final)
# speedup vs baseline: 1.8213x; 1.0855x over previous
import numpy as np
import ml_dtypes
import concourse.bass as bass
import concourse.mybir as mybir
from concourse.bass_utils import run_bass_kernel_spmd

F32 = mybir.dt.float32
BF16 = mybir.dt.bfloat16
I32 = mybir.dt.int32
AF = mybir.ActivationFunctionType
ALU = mybir.AluOpType
AX = mybir.AxisListType
ET = mybir.EngineType

BF = ml_dtypes.bfloat16

D = 2048
S = 2048
B_CORE = 2
NTOK = B_CORE * S
EPS = 1e-5
RSQRT_MAGIC = 0x5f3759df

ALL_ENG = [ET.SP, ET.Pool, ET.PE, ET.DVE, ET.Activation]
L1F_NSUP = 8
USE_FORI = True
SKIP_PREP = False
DUMMY_PREP = False
DIAG_SERIAL_DMA = True
DEBUG_FEATS = False
DEBUG_L1A = False


class _PyLoop:
    def __init__(self, n):
        self.n = n

    def __enter__(self):
        return 0

    def __exit__(self, *a):
        return False


def loop_ctx(nc, n):
    if USE_FORI:
        return nc.Fori(0, n, engines=ALL_ENG)
    return _PyLoop(n)



class Ctx:
    def __init__(self, nc, es):
        self.nc = nc
        self.es = es
        self.sems = {}
        self.n = {}
        for name in ["lx", "lw", "sx", "pe", "dv", "ac", "g2",
                     "lwA", "lwB", "lwC", "lwD"]:
            self.sems[name] = nc.alloc_semaphore(name)
            self.n[name] = 0
        self._uq = 0

    def uniq(self):
        self._uq += 1
        return f"u{self._uq}_"

    def inc(self, instr, name, k=1):
        instr.then_inc(self.sems[name], k)
        self.n[name] += k
        return instr

    def linc(self, instr, name, k=1):
        instr.then_inc(self.sems[name], k)
        return instr

    def bump(self, name, k):
        self.n[name] += k

    def wait(self, eng, name, val=None):
        eng.wait_ge(self.sems[name], self.n[name] if val is None else val)

    def barrier(self):
        self.nc.all_engine_barrier()


class Site:

    def __init__(self, C, eng, sem_name, init_off, delta, name):
        self.eng = eng
        self.sem = C.sems[sem_name]
        self.delta = delta
        self.reg = eng.alloc_register(name)
        eng.reg_mov(self.reg, C.n[sem_name] + init_off)

    def wait(self):
        self.eng.wait_ge(self.sem, self.reg)
        if self.delta:
            self.eng.reg_add(self.reg, self.reg, self.delta)

    def adjust(self, k):
        if k:
            self.eng.reg_add(self.reg, self.reg, k)

    def free(self):
        self.eng.free_register(self.reg)


class Off:

    def __init__(self, eng, init, delta, name):
        self.eng = eng
        self.delta = delta
        self.reg = eng.alloc_register(name)
        eng.reg_mov(self.reg, init)

    def next(self):
        self.eng.reg_add(self.reg, self.reg, self.delta)

    def free(self):
        self.eng.free_register(self.reg)


def rap(tensor_ap, reg, pairs):
    t = tensor_ap.tensor if isinstance(tensor_ap, bass.AP) else tensor_ap
    return bass.AP(t, reg, pairs)


def newton_rsqrt(C, v_ap, out_ap, ia_ap, ib_ap, tmp_ap, tmp2_ap, iters=3):
    v = C.nc.vector
    v.tensor_single_scalar(ia_ap, v_ap.bitcast(I32), 1, ALU.arith_shift_right)
    v.drain()
    v.tensor_scalar(ib_ap, ia_ap, -1, RSQRT_MAGIC, ALU.mult, ALU.add)
    v.drain()
    cur, nxt = ib_ap, ia_ap
    for _ in range(iters):
        r = cur.bitcast(F32)
        v.tensor_mul(tmp_ap, r, r)
        v.drain()
        v.tensor_mul(tmp2_ap, tmp_ap, v_ap)
        v.drain()
        v.tensor_scalar(tmp_ap, tmp2_ap, -0.5, 1.5, ALU.mult, ALU.add)
        v.drain()
        v.tensor_mul(nxt.bitcast(F32), r, tmp_ap)
        v.drain()
        cur, nxt = nxt, cur
    v.tensor_copy(out_ap, cur.bitcast(F32))
    v.drain()



def emit_prep(C, jobs):
    nc = C.nc
    from contextlib import ExitStack
    with ExitStack() as ps:
        buf = [ps.enter_context(nc.sbuf_tensor(f"prep{C.uniq()}", [128, 4096], BF16))
               for i in range(2)]
        tiles = []
        for src, dst in jobs:
            R, Cc = src.shape
            assert R % 128 == 0, (R, Cc)
            for r0 in range(0, R, 128):
                for c0 in range(0, Cc, 4096):
                    tiles.append((src, dst, r0, c0, min(4096, Cc - c0)))
        nld = {0: 0, 1: 0}
        nst = {0: 0, 1: 0}
        for k, (src, dst, r0, c0, w) in enumerate(tiles):
            par = k % 2
            lsem = "lwA" if par == 0 else "lwB"
            ssem = "lwC" if par == 0 else "lwD"
            if k >= 2:
                C.wait(nc.gpsimd, ssem, C.n[ssem])
            C.inc(nc.gpsimd.dma_start(buf[par][:, 0:w], src[r0:r0 + 128, c0:c0 + w]),
                  lsem, 16)
            nld[par] += 1
            C.wait(nc.sync, lsem, C.n[lsem])
            C.inc(nc.sync.dma_start(dst[r0:r0 + 128, c0:c0 + w], buf[par][:, 0:w]),
                  ssem, 16)
            nst[par] += 1
        C.wait(nc.sync, "lwC")
        C.wait(nc.sync, "lwD")


def emit_l1ffn(C, x_in, x_out, wgT, wuT, wdT, wsem="lw"):
    nc = C.nc
    uq = C.uniq()
    from contextlib import ExitStack
    with ExitStack() as ps:
        sb = lambda n, sh, dt: ps.enter_context(nc.sbuf_tensor(uq + n, sh, dt))
        xt = sb("f_xt", [128, 4 * D], F32)
        yT = sb("f_yT", [128, 16 * 512], BF16)
        gup = sb("f_gup", [128, 64 * 512], BF16)
        wg_b = [sb(f"f_wg{i}", [128, 16 * 128], BF16) for i in range(2)]
        wu_b = [sb(f"f_wu{i}", [128, 16 * 128], BF16) for i in range(2)]
        wd_b = [sb(f"f_wd{i}", [128, 8 * 512], BF16) for i in range(2)]
        sq = sb("f_sq", [128, D], F32)
        ybf4 = sb("f_ybf4", [128, 4 * D], BF16)
        gs = [sb(f"f_gs{i}", [128, 512], BF16) for i in range(2)]
        st4 = sb("f_st4", [128, 16], F32)
        sti = sb("f_sti", [128, 4], I32)
        sti2 = sb("f_sti2", [128, 4], I32)
        stt1 = sb("f_stt1", [128, 4], F32)
        stt2 = sb("f_stt2", [128, 4], F32)

        tp = ps.enter_context(nc.psum_tensor(uq + "f_tp", [128, 4096], BF16))
        p = [ps.enter_context(nc.psum_tensor(f"{uq}f_p{i}", [128, 512], F32))
             for i in range(4)]

        mu, var, rstd, nmr = (st4[:, 0:4], st4[:, 4:8], st4[:, 8:12], st4[:, 12:16])
        idn = C.idn_bf
        yT3 = yT[:].rearrange("p (k s) -> p k s", k=16)

        NSUP = L1F_NSUP
        DV, AC, PE_, LX, LW, SX = 72, 66, 386, 16, 6144, 16

        S1 = Site(C, nc.sync, "sx", 0, SX, uq + "fS1")
        S2 = Site(C, nc.sync, "dv", DV, DV, uq + "fS2")
        G1 = Site(C, nc.gpsimd, "pe", 0, 2, uq + "fG1")
        G2 = Site(C, nc.gpsimd, "pe", 122, 8, uq + "fG2")
        P1b = Site(C, nc.tensor, "dv", 0, DV, uq + "fP1b")
        P1 = Site(C, nc.tensor, "ac", 2, AC, uq + "fP1")
        P2 = Site(C, nc.tensor, "dv", 3, DV, uq + "fP2")
        P3a = Site(C, nc.tensor, "lwA", 32, 32, uq + "fP3a")
        P3b2 = Site(C, nc.tensor, "lwB", 32, 32, uq + "fP3b2")
        P3b = Site(C, nc.tensor, "dv", 4, DV, uq + "fP3b")
        P3c = Site(C, nc.tensor, "ac", 1, 1, uq + "fP3c")
        P3d = Site(C, nc.tensor, "dv", 3, 1, uq + "fP3d")
        P4a = Site(C, nc.tensor, "ac", AC, AC, uq + "fP4a")
        P4b = Site(C, nc.tensor, "dv", 68, DV, uq + "fP4b")
        P5a = Site(C, nc.tensor, "lwC", 16, 16, uq + "fP5a")
        P5b = Site(C, nc.tensor, "lwD", 16, 16, uq + "fP5b")
        P6 = Site(C, nc.tensor, "dv", 69, 1, uq + "fP6")
        V1 = Site(C, nc.vector, "lx", 16, LX, uq + "fV1")
        V2 = Site(C, nc.vector, "ac", 1, AC, uq + "fV2")
        V3 = Site(C, nc.vector, "pe", 1, PE_, uq + "fV3")
        V4 = Site(C, nc.vector, "pe", 2, PE_, uq + "fV4")
        V5a = Site(C, nc.vector, "pe", 4, 2, uq + "fV5a")
        V5b = Site(C, nc.vector, "ac", 3, 1, uq + "fV5b")
        V6 = Site(C, nc.vector, "sx", 0, SX, uq + "fV6")
        V7 = Site(C, nc.vector, "pe", 194, 64, uq + "fV7")
        A1 = Site(C, nc.scalar, "dv", 1, DV, uq + "fA1")
        A2 = Site(C, nc.scalar, "dv", 2, DV, uq + "fA2")
        A3 = Site(C, nc.scalar, "pe", 3, 2, uq + "fA3")
        A4 = Site(C, nc.scalar, "dv", 3, 1, uq + "fA4")
        sites = [S1, S2, G1, G2, P1b, P1, P2, P3a, P3b2, P3b, P3c, P3d, P4a, P4b, P5a, P5b, P6,
                 V1, V2, V3, V4, V5a, V5b, V6, V7, A1, A2, A3, A4]

        ox_ld = Off(nc.sync, 0, 512 * D, uq + "fox_ld")
        ox_st = Off(nc.sync, 0, 512 * D, uq + "fox_st")

        for _rep in range(1 if USE_FORI else NSUP):
         if _rep == 0:
          ctxmgr = loop_ctx(nc, NSUP)
         else:
          ctxmgr = _PyLoop(NSUP)
         with ctxmgr:
            S1.wait()
            C.linc(nc.sync.dma_start(
                xt[:].rearrange("p (t j) -> p t j", t=4),
                rap(x_in, ox_ld.reg, [[D, 128], [128 * D, 4], [1, D]])),
                "lx", 16)
            ox_ld.next()
            S2.wait()
            C.linc(nc.sync.dma_start(
                rap(x_out, ox_st.reg, [[D, 128], [128 * D, 4], [1, D]]),
                xt[:].rearrange("p (t j) -> p t j", t=4)),
                "sx", 16)
            ox_st.next()

            for hc in range(64):
                G1.wait()
                psem = "lwA" if hc % 2 == 0 else "lwB"
                C.linc(nc.gpsimd.dma_start(
                    wg_b[hc % 2][:],
                    wgT[hc * 128:(hc + 1) * 128, :]), psem, 16)
                C.linc(nc.gpsimd.dma_start(
                    wu_b[hc % 2][:],
                    wuT[hc * 128:(hc + 1) * 128, :]), psem, 16)
            G1.adjust(PE_ - 128)
            for dc in range(4):
                for hb in range(8):
                    q = dc * 8 + hb
                    G2.wait()
                    C.linc(nc.gpsimd.dma_start(
                        wd_b[q % 2][:].rearrange("p (h m) -> p h m", h=8),
                        bass.AP(wdT.tensor, hb * 8 * 128 * D + dc * 512,
                                [[D, 128], [128 * D, 8], [1, 512]])),
                           "lwC" if q % 2 == 0 else "lwD", 16)
            G2.adjust(130)

            V1.wait()
            for t in range(4):
                nc.vector.reduce_sum(mu[:, t:t + 1], xt[:, t * D:(t + 1) * D],
                                     axis=AX.X, negate=True)
            nc.vector.drain()
            C.linc(nc.vector.tensor_scalar_mul(mu, mu, 1.0 / D), "dv", 1)
            V2.wait()
            nc.vector.tensor_scalar(var, var, 1.0 / D, EPS, ALU.mult, ALU.add)
            nc.vector.drain()
            newton_rsqrt(C, var, rstd, sti[:], sti2[:], stt1[:], stt2[:])
            C.linc(nc.vector.tensor_mul(nmr, mu, rstd), "dv", 1)
            for g, site in ((0, V3), (1, V4)):
                site.wait()
                instr = None
                for t in (2 * g, 2 * g + 1):
                    instr = nc.vector.tensor_copy(
                        yT3[:, :, t * 128:(t + 1) * 128],
                        tp[:, (t % 2) * 2048:(t % 2) * 2048 + 2048]
                        .rearrange("p (k m) -> p k m", k=16))
                C.linc(instr, "dv", 1)
            for hc in range(64):
                V5a.wait()
                V5b.wait()
                C.linc(nc.vector.tensor_tensor(
                    gup[:, hc * 512:(hc + 1) * 512], p[2 + hc % 2][:],
                    gs[hc % 2][:], ALU.mult), "dv", 1)
            V5a.adjust(PE_ - 128)
            V5b.adjust(AC - 64)
            V6.wait()
            for dc in range(4):
                V7.wait()
                instr = None
                for t in range(4):
                    sl = slice(t * D + dc * 512, t * D + (dc + 1) * 512)
                    instr = nc.vector.scalar_tensor_tensor(
                        xt[:, sl], p[t][:], 0.5, xt[:, sl], ALU.mult, ALU.add)
                C.linc(instr, "dv", 1)
            V7.adjust(130)

            A1.wait()
            instr = None
            for t in range(4):
                instr = nc.scalar.activation(sq[:], xt[:, t * D:(t + 1) * D],
                                             AF.Square, bias=mu[:, t:t + 1],
                                             accum_out=var[:, t:t + 1])
            C.linc(instr, "ac", 1)
            A2.wait()
            instr = None
            for t in range(4):
                instr = nc.scalar.activation(ybf4[:, t * D:(t + 1) * D],
                                             xt[:, t * D:(t + 1) * D],
                                             AF.Identity, scale=rstd[:, t:t + 1],
                                             bias=nmr[:, t:t + 1])
            C.linc(instr, "ac", 1)
            for hc in range(64):
                A3.wait()
                A4.wait()
                C.linc(nc.scalar.activation(gs[hc % 2][:], p[hc % 2][:], AF.Silu),
                       "ac", 1)
            A3.adjust(PE_ - 128)
            A4.adjust(DV - 64)

            P1b.wait()
            P1.wait()
            for g in range(2):
                if g == 1:
                    P2.wait()
                instr = None
                for t in (2 * g, 2 * g + 1):
                    for k in range(16):
                        instr = nc.tensor.transpose(
                            tp[:, (t % 2) * 2048 + k * 128:(t % 2) * 2048 + (k + 1) * 128],
                            ybf4[:, t * D + k * 128:t * D + (k + 1) * 128], idn[:])
                C.linc(instr, "pe", 1)
            P3b.wait()
            for hc in range(64):
                (P3a if hc % 2 == 0 else P3b2).wait()
                P3c.wait()
                P3d.wait()
                instr = None
                for k in range(16):
                    instr = nc.tensor.matmul(p[hc % 2][:],
                                             wg_b[hc % 2][:, k * 128:(k + 1) * 128],
                                             yT[:, k * 512:(k + 1) * 512],
                                             start=(k == 0), stop=(k == 15))
                C.linc(instr, "pe", 1)
                instr = None
                for k in range(16):
                    instr = nc.tensor.matmul(p[2 + hc % 2][:],
                                             wu_b[hc % 2][:, k * 128:(k + 1) * 128],
                                             yT[:, k * 512:(k + 1) * 512],
                                             start=(k == 0), stop=(k == 15))
                C.linc(instr, "pe", 1)
            P3c.adjust(AC - 64)
            P3d.adjust(DV - 64)
            P4a.wait()
            P4b.wait()
            for dc in range(4):
                if dc > 0:
                    P6.wait()
                for hc in range(64):
                    q = dc * 8 + hc // 8
                    if hc % 8 == 0:
                        (P5a if q % 2 == 0 else P5b).wait()
                    instr = None
                    for t in range(4):
                        instr = nc.tensor.matmul(
                            p[t][:],
                            gup[:, hc * 512 + t * 128: hc * 512 + (t + 1) * 128],
                            wd_b[q % 2][:, (hc % 8) * 512:(hc % 8 + 1) * 512],
                            start=(hc == 0), stop=(hc == 63))
                    C.linc(instr, "pe", 1)
            P6.adjust(DV - 3)

        for name, tot in (("lx", LX), ("lwA", 1024), ("lwB", 1024),
                          ("lwC", 256), ("lwD", 256), ("sx", SX), ("pe", PE_),
                          ("dv", DV), ("ac", AC)):
            C.bump(name, tot * NSUP)

        C.wait(nc.sync, "sx")
        for s_ in sites:
            s_.free()
        ox_ld.free()
        ox_st.free()



def emit_moe(C, x_in, x_out, wi01, wos):
    nc = C.nc
    uq = C.uniq()
    from contextlib import ExitStack
    with ExitStack() as ps:
        sb = lambda n, sh, dt: ps.enter_context(nc.sbuf_tensor(uq + n, sh, dt))
        xt = sb("m_xt", [128, 4 * D], F32)
        yT = sb("m_yT", [128, 16 * 512], BF16)
        inter = [sb(f"m_it{e}", [128, 8 * 512], BF16) for e in range(8)]
        wib = [sb(f"m_wi{i}", [128, 4096], BF16) for i in range(2)]
        wob = [sb(f"m_wo{i}", [128, 8 * 512], BF16) for i in range(2)]
        sq = sb("m_sq", [128, D], F32)
        ybf4 = sb("m_ybf4", [128, 4 * D], BF16)
        ss = [sb(f"m_ss{i}", [128, 512], BF16) for i in range(2)]
        itmp = sb("m_itmp", [128, 512], F32)
        st4 = sb("m_st4", [128, 16], F32)
        sti = sb("m_sti", [128, 4], I32)
        sti2 = sb("m_sti2", [128, 4], I32)
        stt1 = sb("m_stt1", [128, 4], F32)
        stt2 = sb("m_stt2", [128, 4], F32)
        m1p4 = sb("m_m1p", [128, 4], F32)
        m1n4 = sb("m_m1n", [128, 4], F32)
        m2p4 = sb("m_m2p", [128, 4], F32)
        d214 = sb("m_d21", [128, 4], F32)
        den4 = sb("m_den", [128, 4], F32)
        rec4 = sb("m_rec", [128, 4], F32)
        msk = sb("m_msk", [128, 32], F32)
        lgm = sb("m_lgm", [128, 32], F32)
        e1s = sb("m_e1", [128, 32], F32)
        ge2 = sb("m_ge2", [128, 32], F32)
        wtmp = sb("m_wtmp", [128, 8], F32)
        wtk = sb("m_wtk", [128, 32], F32)
        wbf = sb("m_wbf", [128, 32], BF16)
        wT = sb("m_wT", [8, 512], BF16)
        wb8 = sb("m_wb8", [128, 8 * 512], F32)

        tp = ps.enter_context(nc.psum_tensor(uq + "m_tp", [128, 4096], BF16))
        p = [ps.enter_context(nc.psum_tensor(f"{uq}m_p{i}", [128, 512], F32))
             for i in range(4)]

        mu, var, rstd, nmr = (st4[:, 0:4], st4[:, 4:8], st4[:, 8:12], st4[:, 12:16])
        idn = C.idn_bf
        oh8 = C.oh8_bf
        yT3 = yT[:].rearrange("p (k s) -> p k s", k=16)

        NSUP = 8
        DV, AC, PE_, LX, SX = 92, 70, 175, 16, 16
        LAB = 16 * 32
        LCD = 16 * 16

        S1 = Site(C, nc.sync, "sx", 0, SX, uq + "mS1")
        S2 = Site(C, nc.sync, "dv", DV, DV, uq + "mS2")
        G1a = Site(C, nc.gpsimd, "pe", 13, 2, uq + "mG1a")
        G2a = Site(C, nc.gpsimd, "pe", 142, 1, uq + "mG2a")
        P1b = Site(C, nc.tensor, "dv", 0, DV, uq + "mP1b")
        P1 = Site(C, nc.tensor, "ac", 2, AC, uq + "mP1")
        P2 = Site(C, nc.tensor, "dv", 3, DV, uq + "mP2")
        PB1 = Site(C, nc.tensor, "dv", 4, DV, uq + "mPB1")
        PB2 = [Site(C, nc.tensor, "dv", v, DV, f"{uq}mPB2_{t}")
               for t, v in enumerate((9, 13, 14, 15))]
        PB4 = Site(C, nc.tensor, "dv", 16, DV, uq + "mPB4")
        PB5 = Site(C, nc.tensor, "dv", 17, 1, uq + "mPB5")
        PCa = Site(C, nc.tensor, "lwA", 16, 16, uq + "mPCa")
        PCb = Site(C, nc.tensor, "lwB", 16, 16, uq + "mPCb")
        PC0a = Site(C, nc.tensor, "ac", 6, AC, uq + "mPC0a")
        PC0b = Site(C, nc.tensor, "dv", 24, DV, uq + "mPC0b")
        PC2 = Site(C, nc.tensor, "ac", 5, 1, uq + "mPC2")
        PC3 = Site(C, nc.tensor, "dv", 23, 1, uq + "mPC3")
        PDa = Site(C, nc.tensor, "lwC", 16, 16, uq + "mPDa")
        PDb = Site(C, nc.tensor, "lwD", 16, 16, uq + "mPDb")
        PD2 = Site(C, nc.tensor, "dv", 89, 1, uq + "mPD2")
        PD3a = Site(C, nc.tensor, "ac", 70, AC, uq + "mPD3a")
        PD3b = Site(C, nc.tensor, "dv", 88, DV, uq + "mPD3b")
        V1 = Site(C, nc.vector, "lx", 16, LX, uq + "mV1")
        V2 = Site(C, nc.vector, "ac", 1, AC, uq + "mV2")
        V3 = Site(C, nc.vector, "pe", 1, PE_, uq + "mV3")
        V4 = Site(C, nc.vector, "pe", 2, PE_, uq + "mV4")
        VB1 = Site(C, nc.vector, "pe", 3, PE_, uq + "mVB1")
        VB2 = [Site(C, nc.vector, "ac", 3 + t, AC, f"{uq}mVB2_{t}") for t in range(4)]
        VB3 = [Site(C, nc.vector, "pe", 4 + t, PE_, f"{uq}mVB3_{t}") for t in range(4)]
        VB4 = Site(C, nc.vector, "pe", 8, 1, uq + "mVB4")
        VC1a = Site(C, nc.vector, "pe", 17, 2, uq + "mVC1a")
        VC1b = Site(C, nc.vector, "ac", 7, 1, uq + "mVC1b")
        VD1 = Site(C, nc.vector, "pe", 151, 8, uq + "mVD1")
        V6 = Site(C, nc.vector, "sx", 0, SX, uq + "mV6")
        A1 = Site(C, nc.scalar, "dv", 1, DV, uq + "mA1")
        A2 = Site(C, nc.scalar, "dv", 2, DV, uq + "mA2")
        AB1 = [Site(C, nc.scalar, "dv", 5 + t, DV, f"{uq}mAB1_{t}") for t in range(4)]
        AC1 = Site(C, nc.scalar, "pe", 16, 2, uq + "mAC1")
        AC2 = Site(C, nc.scalar, "dv", 23, 1, uq + "mAC2")
        sites = ([S1, S2, G1a, G2a, P1b, P1, P2, PB1, PB4, PB5, PCa, PCb,
                  PC0a, PC0b, PC2, PC3, PDa, PDb, PD2, PD3a, PD3b, V1, V2, V3,
                  V4, VB1, VB4, VC1a, VC1b, VD1, V6, A1, A2, AC1, AC2]
                 + PB2 + VB2 + VB3 + AB1)

        ox_ld = Off(nc.sync, 0, 512 * D, uq + "mox_ld")
        ox_st = Off(nc.sync, 0, 512 * D, uq + "mox_st")

        with nc.Fori(0, NSUP, engines=ALL_ENG):
            S1.wait()
            C.linc(nc.sync.dma_start(
                xt[:].rearrange("p (t j) -> p t j", t=4),
                rap(x_in, ox_ld.reg, [[D, 128], [128 * D, 4], [1, D]])), "lx", 16)
            ox_ld.next()
            S2.wait()
            C.linc(nc.sync.dma_start(
                rap(x_out, ox_st.reg, [[D, 128], [128 * D, 4], [1, D]]),
                xt[:].rearrange("p (t j) -> p t j", t=4)), "sx", 16)
            ox_st.next()

            for j in range(64):
                e, mc = j // 8, j % 8
                G1a.wait()
                sem = "lwA" if j % 2 == 0 else "lwB"
                C.linc(nc.gpsimd.dma_start(wib[j % 2][:], wi01[e, mc]), sem, 16)
            G1a.adjust(PE_ - 128)
            for q in range(32):
                dcq, eq = q // 8, q % 8
                G2a.wait()
                sem = "lwC" if q % 2 == 0 else "lwD"
                C.linc(nc.gpsimd.dma_start(
                    wob[q % 2][:],
                    bass.AP(wos.tensor, eq * 1024 * 2048 + dcq * 512,
                            [[2048, 128], [128 * 2048, 8], [1, 512]])), sem, 16)
            G2a.adjust(PE_ - 32)

            V1.wait()
            for t in range(4):
                nc.vector.reduce_sum(mu[:, t:t + 1], xt[:, t * D:(t + 1) * D],
                                     axis=AX.X, negate=True)
            nc.vector.drain()
            C.linc(nc.vector.tensor_scalar_mul(mu, mu, 1.0 / D), "dv", 1)
            V2.wait()
            nc.vector.tensor_scalar(var, var, 1.0 / D, EPS, ALU.mult, ALU.add)
            nc.vector.drain()
            newton_rsqrt(C, var, rstd, sti[:], sti2[:], stt1[:], stt2[:])
            C.linc(nc.vector.tensor_mul(nmr, mu, rstd), "dv", 1)
            for g, site in ((0, V3), (1, V4)):
                site.wait()
                instr = None
                for t in (2 * g, 2 * g + 1):
                    instr = nc.vector.tensor_copy(
                        yT3[:, :, t * 128:(t + 1) * 128],
                        tp[:, (t % 2) * 2048:(t % 2) * 2048 + 2048]
                        .rearrange("p (k m) -> p k m", k=16))
                C.linc(instr, "dv", 1)
            VB1.wait()
            for t in range(4):
                sl = slice(8 * t, 8 * t + 8)
                pl = p[0][:, sl]
                nc.vector.reduce_max(m1p4[:, t:t + 1], pl, axis=AX.X)
                nc.vector.reduce_max(m1n4[:, t:t + 1], pl, axis=AX.X, negate=True)
                nc.vector.drain()
                nc.vector.tensor_single_scalar(msk[:, sl], pl, m1p4[:, t:t + 1],
                                               ALU.is_ge)
                nc.vector.drain()
                nc.vector.scalar_tensor_tensor(lgm[:, sl], msk[:, sl], -1e30, pl,
                                               ALU.mult, ALU.add)
                nc.vector.drain()
                C.linc(nc.vector.reduce_max(m2p4[:, t:t + 1], lgm[:, sl], axis=AX.X),
                       "dv", 1)
            for t in range(4):
                sl = slice(8 * t, 8 * t + 8)
                VB2[t].wait()
                nc.vector.tensor_single_scalar(ge2[:, sl], p[0][:, sl],
                                               m2p4[:, t:t + 1], ALU.is_ge)
                nc.vector.tensor_scalar_add(den4[:, t:t + 1], d214[:, t:t + 1], 1.0)
                nc.vector.drain()
                nc.vector.reciprocal(rec4[:, t:t + 1], den4[:, t:t + 1])
                nc.vector.drain()
                nc.vector.tensor_mul(wtmp[:], e1s[:, sl], ge2[:, sl])
                nc.vector.drain()
                nc.vector.tensor_scalar(wtk[:, sl], wtmp[:], rec4[:, t:t + 1], 0.5,
                                        ALU.mult, ALU.mult)
                nc.vector.drain()
                C.linc(nc.vector.tensor_copy(wbf[:, sl], wtk[:, sl]), "dv", 1)
            for t in range(4):
                VB3[t].wait()
                C.linc(nc.vector.tensor_copy(
                    wT[:, t * 128:(t + 1) * 128], tp[0:8, t * 128:(t + 1) * 128]),
                    "dv", 1)
            for e in range(8):
                VB4.wait()
                C.linc(nc.vector.tensor_copy(wb8[:, e * 512:(e + 1) * 512], p[1][:]),
                       "dv", 1)
            VB4.adjust(PE_ - 8)
            for j in range(64):
                e, mc = j // 8, j % 8
                VC1a.wait()
                VC1b.wait()
                ph1 = p[1 + 2 * (j % 2)][:]
                nc.vector.tensor_tensor(itmp[:], ph1, ss[j % 2][:], ALU.mult)
                nc.vector.drain()
                C.linc(nc.vector.tensor_tensor(
                    inter[e][:, mc * 512:(mc + 1) * 512], itmp[:],
                    wb8[:, e * 512:(e + 1) * 512], ALU.mult), "dv", 1)
            VC1a.adjust(PE_ - 128)
            VC1b.adjust(AC - 64)
            V6.wait()
            for dcq in range(4):
                VD1.wait()
                instr = None
                for t in range(4):
                    sl = slice(t * D + dcq * 512, t * D + (dcq + 1) * 512)
                    instr = nc.vector.scalar_tensor_tensor(
                        xt[:, sl], p[t][:], 1.0, xt[:, sl], ALU.mult, ALU.add)
                C.linc(instr, "dv", 1)
            VD1.adjust(PE_ - 32)

            A1.wait()
            instr = None
            for t in range(4):
                instr = nc.scalar.activation(sq[:], xt[:, t * D:(t + 1) * D],
                                             AF.Square, bias=mu[:, t:t + 1],
                                             accum_out=var[:, t:t + 1])
            C.linc(instr, "ac", 1)
            A2.wait()
            instr = None
            for t in range(4):
                instr = nc.scalar.activation(ybf4[:, t * D:(t + 1) * D],
                                             xt[:, t * D:(t + 1) * D],
                                             AF.Identity, scale=rstd[:, t:t + 1],
                                             bias=nmr[:, t:t + 1])
            C.linc(instr, "ac", 1)
            for t in range(4):
                sl = slice(8 * t, 8 * t + 8)
                AB1[t].wait()
                nc.scalar.activation(e1s[:, sl], p[0][:, sl], AF.Exp,
                                     bias=m1n4[:, t:t + 1])
                C.linc(nc.scalar.activation(d214[:, t:t + 1], m2p4[:, t:t + 1],
                                            AF.Exp, bias=m1n4[:, t:t + 1]),
                       "ac", 1)
            for j in range(64):
                AC1.wait()
                AC2.wait()
                ph0 = p[2 * (j % 2)][:]
                C.linc(nc.scalar.activation(ss[j % 2][:], ph0, AF.Silu), "ac", 1)
            AC1.adjust(PE_ - 128)
            AC2.adjust(DV - 64)

            P1b.wait()
            P1.wait()
            for g in range(2):
                if g == 1:
                    P2.wait()
                instr = None
                for t in (2 * g, 2 * g + 1):
                    for k in range(16):
                        instr = nc.tensor.transpose(
                            tp[:, (t % 2) * 2048 + k * 128:(t % 2) * 2048 + (k + 1) * 128],
                            ybf4[:, t * D + k * 128:t * D + (k + 1) * 128], idn[:])
                C.linc(instr, "pe", 1)
            PB1.wait()
            instr = None
            for t in range(4):
                for k in range(16):
                    instr = nc.tensor.matmul(
                        p[0][:, 8 * t:8 * t + 8],
                        yT[:, k * 512 + t * 128: k * 512 + (t + 1) * 128],
                        C.wg8_bf[:, k * 8:(k + 1) * 8],
                        start=(t == 0 and k == 0), stop=(t == 3 and k == 15))
            C.linc(instr, "pe", 1)
            for t in range(4):
                PB2[t].wait()
                C.linc(nc.tensor.transpose(
                    tp[0:8, t * 128:(t + 1) * 128],
                    wbf[:, 8 * t:8 * t + 8], idn[:]), "pe", 1)
            PB4.wait()
            for e in range(8):
                if e > 0:
                    PB5.wait()
                C.linc(nc.tensor.matmul(
                    p[1][:], oh8[:, e * 128:(e + 1) * 128], wT[:],
                    start=True, stop=True), "pe", 1)
            PB5.adjust(DV - 7)
            PC0a.wait()
            PC0b.wait()
            for j in range(64):
                e, mc = j // 8, j % 8
                (PCa if j % 2 == 0 else PCb).wait()
                PC2.wait()
                PC3.wait()
                ph0 = p[2 * (j % 2)][:]
                ph1 = p[1 + 2 * (j % 2)][:]
                instr = None
                for k in range(16):
                    instr = nc.tensor.matmul(ph0,
                                             wib[j % 2][:, k * 128:(k + 1) * 128],
                                             yT[:, k * 512:(k + 1) * 512],
                                             start=(k == 0), stop=(k == 15))
                C.linc(instr, "pe", 1)
                instr = None
                for k in range(16):
                    instr = nc.tensor.matmul(ph1,
                                             wib[j % 2][:, 2048 + k * 128:2048 + (k + 1) * 128],
                                             yT[:, k * 512:(k + 1) * 512],
                                             start=(k == 0), stop=(k == 15))
                C.linc(instr, "pe", 1)
            PC2.adjust(AC - 64)
            PC3.adjust(DV - 64)
            PD3a.wait()
            PD3b.wait()
            for q in range(32):
                dcq, eq = q // 8, q % 8
                if eq == 0 and dcq > 0:
                    PD2.wait()
                (PDa if q % 2 == 0 else PDb).wait()
                instr = None
                for t in range(4):
                    for k in range(8):
                        instr = nc.tensor.matmul(
                            p[t][:],
                            inter[eq][:, k * 512 + t * 128: k * 512 + (t + 1) * 128],
                            wob[q % 2][:, k * 512:(k + 1) * 512],
                            start=(eq == 0 and k == 0), stop=(eq == 7 and k == 7))
                C.linc(instr, "pe", 1)
            PD2.adjust(DV - 3)

        for name, tot in (("lx", LX), ("sx", SX), ("pe", PE_), ("dv", DV),
                          ("ac", AC), ("lwA", LAB), ("lwB", LAB),
                          ("lwC", LCD), ("lwD", LCD)):
            C.bump(name, tot * NSUP)

        C.wait(nc.sync, "sx")
        for s_ in sites:
            s_.free()
        ox_ld.free()
        ox_st.free()



def ln_tile(C, x_ap, out_ap, sb1, rstd_out=None):
    nc = C.nc
    F = x_ap.shape[-1]
    nc.vector.reduce_sum(sb1["mu"], x_ap, axis=AX.X, negate=True)
    nc.vector.drain()
    C.inc(nc.vector.tensor_scalar_mul(sb1["mu"], sb1["mu"], 1.0 / F), "dv", 1)
    C.wait(nc.scalar, "dv")
    C.inc(nc.scalar.activation(sb1["sq"], x_ap, AF.Square, bias=sb1["mu"],
                               accum_out=sb1["var"]), "ac", 1)
    C.wait(nc.vector, "ac")
    nc.vector.tensor_scalar(sb1["var"], sb1["var"], 1.0 / F, EPS, ALU.mult, ALU.add)
    nc.vector.drain()
    newton_rsqrt(C, sb1["var"], sb1["rstd"], sb1["ia"], sb1["ib"],
                 sb1["t1"], sb1["t2"])
    C.inc(nc.vector.tensor_mul(sb1["nmr"], sb1["mu"], sb1["rstd"]), "dv", 1)
    C.wait(nc.scalar, "dv")
    C.inc(nc.scalar.activation(out_ap, x_ap, AF.Identity, scale=sb1["rstd"],
                               bias=sb1["nmr"]), "ac", 1)


def emit_l0a(C, x_in, x_out, W):
    nc = C.nc
    uq = C.uniq()
    from contextlib import ExitStack
    with ExitStack() as ps:
        sb = lambda n, sh, dt: ps.enter_context(nc.sbuf_tensor(uq + n, sh, dt))
        xt = sb("a_xt", [128, D], F32)
        y0f = sb("a_y0f", [128, D], F32)
        n2bf = sb("a_n2bf", [128, D], BF16)
        nT = sb("a_nT", [128, 16 * 128], BF16)
        zf = sb("a_zf", [128, D], F32)
        t4 = sb("a_t4", [128, D], F32)
        sq = sb("a_sq", [128, D], F32)
        u_sb = sb("a_u", [8, S], F32)
        decb = sb("a_decb", [8, S], F32)
        hs = sb("a_hs", [8, S], F32)
        hsb = sb("a_hsb", [8, S], BF16)
        hfbf = sb("a_hf", [32, 128], BF16)
        winf = sb("a_winf", [128, 128], F32)
        winb = sb("a_winb", [128, 128], BF16)
        winpb = sb("a_winpb", [128, 128], BF16)
        wdnf = sb("a_wdnf", [128, 512], F32)
        wdnb = sb("a_wdnb", [128, 512], BF16)
        wdnpb = sb("a_wdnpb", [128, 512], BF16)
        gb_f = sb("a_gb", [128, 32], F32)
        bb_bf = sb("a_bbf", [128, 16], BF16)
        woutb = sb("a_woutb", [8, D], BF16)
        wupb = sb("a_wupb", [32, D], BF16)
        dlg = sb("a_dlg", [8, 1], F32)
        dec = sb("a_dec", [8, 1], F32)
        binp = sb("a_binp", [8, 1], F32)
        bin1 = sb("a_bin1", [8, 1], F32)
        bdnp = sb("a_bdnp", [32, 1], F32)
        bdn1 = sb("a_bdn1", [32, 1], F32)
        r_bout = sb("a_rbo", [1, D], F32)
        r_dsk = sb("a_rds", [1, D], F32)
        r_bup = sb("a_rbu", [1, D], F32)
        r_t = sb("a_rt", [1, D], F32)
        r_bf = sb("a_rbf", [1, D], BF16)
        ones_bf = sb("a_ones", [1, 128], BF16)
        boutb = sb("a_boutb", [128, D], F32)
        gdb = sb("a_gdb", [128, D], BF16)
        bupb = sb("a_bupb", [128, D], F32)
        st1 = {k: sb(f"a_s_{k}", [128, 1], F32 if k not in ("ia", "ib") else I32)[:]
               for k in ("mu", "var", "rstd", "nmr", "t1", "t2", "ia", "ib")}
        st1["sq"] = sq[:]

        tp = ps.enter_context(nc.psum_tensor(uq + "a_tp", [128, 2048], BF16))
        pb = ps.enter_context(nc.psum_tensor(uq + "a_pb", [128, 2048], F32))
        p4 = ps.enter_context(nc.psum_tensor(uq + "a_p4", [128, 512], F32))

        g = nc.gpsimd
        C.inc(g.dma_start(winf[:], W["winp"][:].rearrange("p a b -> p (a b)")), "lw", 16)
        C.inc(g.dma_start(winb[:], W["winp"][:].rearrange("p a b -> p (a b)")), "lw", 16)
        C.inc(g.dma_start(wdnf[:], W["wdnp"][:].rearrange("p a b -> p (a b)")), "lw", 16)
        C.inc(g.dma_start(wdnb[:], W["wdnp"][:].rearrange("p a b -> p (a b)")), "lw", 16)
        C.inc(g.dma_start(gb_f[:, 0:16], W["gbp"][0]), "lw", 16)
        C.inc(g.dma_start(gb_f[:, 16:32], W["gbp"][1]), "lw", 16)
        C.inc(g.dma_start(bb_bf[:], W["gbp"][1]), "lw", 16)
        C.inc(g.dma_start(woutb[:], W["wout"][:]), "lw", 16)
        C.inc(g.dma_start(wupb[:], W["wup"][:]), "lw", 16)
        C.inc(g.dma_start(dlg[:], W["dlg"][:]), "lw", 16)
        C.inc(g.dma_start(binp[:], W["bin"][:]), "lw", 16)
        C.inc(g.dma_start(bdnp[:], W["bdn"][:]), "lw", 16)
        C.inc(g.dma_start(r_bout[:], W["rows"][0:1, :]), "lw", 16)
        C.inc(g.dma_start(r_dsk[:], W["rows"][1:2, :]), "lw", 16)
        C.inc(g.dma_start(r_bup[:], W["rows"][2:3, :]), "lw", 16)
        C.inc(g.dma_start(ones_bf[:], W["ones"][:]), "lw", 16)
        C.wait(nc.vector, "lw")
        C.wait(nc.scalar, "lw")
        C.wait(nc.tensor, "lw")
        g16 = gb_f[:, 0:16]
        nc.vector.tensor_tensor(
            winpb[:].rearrange("p (a b) -> p a b", a=16),
            winf[:].rearrange("p (a b) -> p a b", a=16),
            g16.unsqueeze(2).broadcast_to([128, 16, 8]), ALU.mult)
        nc.vector.tensor_tensor(
            wdnpb[:].rearrange("p (a b) -> p a b", a=16),
            wdnf[:].rearrange("p (a b) -> p a b", a=16),
            g16.unsqueeze(2).broadcast_to([128, 16, 32]), ALU.mult)
        C.inc(nc.scalar.activation(dec[:], dlg[:], AF.Sigmoid), "ac", 1)
        instr = None
        for k in range(16):
            instr = nc.tensor.matmul(p4[0:8, 0:1], winb[:, k * 8:(k + 1) * 8],
                                     bb_bf[:, k:k + 1], start=(k == 0), stop=(k == 15))
        C.inc(instr, "pe", 1)
        C.wait(nc.vector, "pe")
        C.inc(nc.vector.tensor_tensor(bin1[:], p4[0:8, 0:1], binp[:], ALU.add), "dv", 1)
        C.wait(nc.tensor, "dv")
        instr = None
        for k in range(16):
            instr = nc.tensor.matmul(p4[0:32, 1:2], wdnb[:, k * 32:(k + 1) * 32],
                                     bb_bf[:, k:k + 1], start=(k == 0), stop=(k == 15))
        C.inc(instr, "pe", 1)
        C.wait(nc.vector, "pe")
        C.inc(nc.vector.tensor_tensor(bdn1[:], p4[0:32, 1:2], bdnp[:], ALU.add), "dv", 1)
        bcasts = [("bout", boutb, None), ("gd", gdb, None), ("bup", bupb, None)]
        r_gam = sb("a_rgam", [1, D], F32)
        r_bet = sb("a_rbet", [1, D], F32)
        C.inc(g.dma_start(r_gam[:], W["rows"][3:4, :]), "lw", 16)
        C.inc(g.dma_start(r_bet[:], W["rows"][4:5, :]), "lw", 16)
        C.wait(nc.vector, "lw")
        for name, dstb, _ in bcasts:
            if name == "bout":
                nc.vector.tensor_mul(r_t[:], r_bet[:], r_dsk[:])
                nc.vector.drain()
                nc.vector.tensor_add(r_t[:], r_t[:], r_bout[:])
            elif name == "gd":
                nc.vector.tensor_mul(r_t[:], r_gam[:], r_dsk[:])
            else:
                nc.vector.tensor_copy(r_t[:], r_bup[:])
            nc.vector.drain()
            C.inc(nc.vector.tensor_copy(r_bf[:], r_t[:]), "dv", 1)
            C.wait(nc.tensor, "dv")
            instr = None
            for cc in range(4):
                instr = nc.tensor.matmul(pb[:, cc * 512:(cc + 1) * 512], ones_bf[:],
                                         r_bf[:, cc * 512:(cc + 1) * 512],
                                         start=True, stop=True)
            C.inc(instr, "pe", 1)
            C.wait(nc.vector, "pe")
            C.inc(nc.vector.tensor_copy(dstb[:], pb[:]), "dv", 1)
            C.wait(nc.tensor, "dv")

        for b in range(B_CORE):
            for t in range(16):
                row = b * S + t * 128
                C.wait(nc.sync, "dv")
                C.wait(nc.sync, "ac")
                C.inc(nc.sync.dma_start(xt[:], x_in[row:row + 128, :]), "lx", 16)
                C.wait(nc.vector, "lx")
                C.wait(nc.scalar, "lx")
                ln_tile(C, xt[:], y0f[:], st1)
                C.wait(nc.vector, "ac")
                ln_tile(C, y0f[:], n2bf[:], st1)
                C.wait(nc.tensor, "ac")
                C.wait(nc.tensor, "dv")
                instr = None
                for k in range(16):
                    instr = nc.tensor.transpose(tp[:, k * 128:(k + 1) * 128],
                                                n2bf[:, k * 128:(k + 1) * 128],
                                                C.idn_bf[:])
                C.inc(instr, "pe", 1)
                C.wait(nc.vector, "pe")
                C.inc(nc.vector.tensor_copy(nT[:], tp[:]), "dv", 1)
                C.wait(nc.tensor, "dv")
                instr = None
                for k in range(16):
                    instr = nc.tensor.matmul(p4[0:8, 0:128],
                                             winpb[:, k * 8:(k + 1) * 8],
                                             nT[:, k * 128:(k + 1) * 128],
                                             start=(k == 0), stop=(k == 15))
                C.inc(instr, "pe", 1)
                C.wait(nc.vector, "pe")
                C.inc(nc.vector.tensor_scalar(u_sb[:, t * 128:(t + 1) * 128],
                                              p4[0:8, 0:128], bin1[:], None,
                                              ALU.add), "dv", 1)
            C.wait(nc.vector, "dv")
            nc.vector.tensor_scalar(decb[:], u_sb[:], 0.0, dec[:], ALU.mult, ALU.add)
            nc.vector.drain()
            nc.vector.tensor_tensor_scan(hs[:], decb[:], u_sb[:], 0.0,
                                         ALU.mult, ALU.add)
            nc.vector.drain()
            C.inc(nc.vector.tensor_copy(hsb[:], hs[:]), "dv", 1)

            for t in range(16):
                row = b * S + t * 128
                C.wait(nc.sync, "dv")
                C.wait(nc.sync, "ac")
                C.inc(nc.sync.dma_start(xt[:], x_in[row:row + 128, :]), "lx", 16)
                C.wait(nc.vector, "lx")
                C.wait(nc.scalar, "lx")
                ln_tile(C, xt[:], y0f[:], st1)
                C.wait(nc.vector, "ac")
                ln_tile(C, y0f[:], n2bf[:], st1)
                C.wait(nc.tensor, "dv")
                C.wait(nc.tensor, "ac")
                instr = None
                for cc in range(4):
                    instr = nc.tensor.matmul(pb[:, cc * 512:(cc + 1) * 512],
                                             hsb[:, t * 128:(t + 1) * 128],
                                             woutb[:, cc * 512:(cc + 1) * 512],
                                             start=True, stop=True)
                C.inc(instr, "pe", 1)
                C.wait(nc.vector, "pe")
                nc.vector.tensor_tensor(t4[:], n2bf[:], gdb[:], ALU.mult)
                nc.vector.drain()
                nc.vector.tensor_add(t4[:], t4[:], y0f[:])
                nc.vector.drain()
                nc.vector.tensor_add(t4[:], t4[:], boutb[:])
                nc.vector.drain()
                C.inc(nc.vector.scalar_tensor_tensor(zf[:], pb[:], 1.0, t4[:],
                                                     ALU.mult, ALU.add), "dv", 1)
                C.wait(nc.scalar, "dv")
                C.wait(nc.vector, "dv")
                ln_tile(C, zf[:], n2bf[:], st1)
                C.wait(nc.tensor, "ac")
                C.wait(nc.tensor, "dv")
                instr = None
                for k in range(16):
                    instr = nc.tensor.transpose(tp[:, k * 128:(k + 1) * 128],
                                                n2bf[:, k * 128:(k + 1) * 128],
                                                C.idn_bf[:])
                C.inc(instr, "pe", 1)
                C.wait(nc.vector, "pe")
                C.inc(nc.vector.tensor_copy(nT[:], tp[:]), "dv", 1)
                C.wait(nc.tensor, "dv")
                instr = None
                for k in range(16):
                    instr = nc.tensor.matmul(p4[0:32, 0:128],
                                             wdnpb[:, k * 32:(k + 1) * 32],
                                             nT[:, k * 128:(k + 1) * 128],
                                             start=(k == 0), stop=(k == 15))
                C.inc(instr, "pe", 1)
                C.wait(nc.scalar, "pe")
                C.inc(nc.scalar.activation(hfbf[:], p4[0:32, 0:128], AF.Relu,
                                           bias=bdn1[:]), "ac", 1)
                C.wait(nc.tensor, "ac")
                instr = None
                for cc in range(4):
                    instr = nc.tensor.matmul(pb[:, cc * 512:(cc + 1) * 512],
                                             hfbf[:],
                                             wupb[:, cc * 512:(cc + 1) * 512],
                                             start=True, stop=True)
                C.inc(instr, "pe", 1)
                C.wait(nc.vector, "pe")
                nc.vector.tensor_add(t4[:], zf[:], bupb[:])
                nc.vector.drain()
                nc.vector.tensor_add(t4[:], t4[:], xt[:])
                nc.vector.drain()
                C.inc(nc.vector.scalar_tensor_tensor(xt[:], pb[:], 1.0, t4[:],
                                                     ALU.mult, ALU.add), "dv", 1)
                C.wait(nc.sync, "dv")
                C.inc(nc.sync.dma_start(x_out[row:row + 128, :], xt[:]), "sx", 16)
                C.wait(nc.sync, "sx")

        C.wait(nc.sync, "sx")



def emit_l1a(C, x_in, x_out, W, feats_scr):
    nc = C.nc
    uq = C.uniq()
    from contextlib import ExitStack
    with ExitStack() as ps:
        sb = lambda n, sh, dt: ps.enter_context(nc.sbuf_tensor(uq + n, sh, dt))
        xt = sb("q_xt", [128, D], F32)
        ybf = sb("q_ybf", [128, D], BF16)
        sq = sb("q_sq", [128, D], F32)
        st1 = {k: sb(f"q_s_{k}", [128, 1], F32 if k not in ("ia", "ib") else I32)[:]
               for k in ("mu", "var", "rstd", "nmr", "t1", "t2", "ia", "ib")}
        st1["sq"] = sq[:]
        bqkp = sb("q_bqkp", [128, 32], F32)
        r_bv = sb("q_rbv", [1, D], F32)
        r_bo = sb("q_rbo", [1, D], F32)
        r_bf = sb("q_rbf", [1, D], BF16)
        ones_bf = sb("q_ones", [1, 128], BF16)
        scl = sb("q_scl", [128, 1], F32)
        bvb = sb("q_bvb", [128, D], F32)
        bob = sb("q_bob", [128, D], F32)

        psc = [ps.enter_context(nc.psum_tensor(f"{uq}q_psc{i}", [128, 1024], F32))
               for i in range(2)]
        pf = ps.enter_context(nc.psum_tensor(uq + "q_pf", [96, 1024], F32))
        pnorm = ps.enter_context(nc.psum_tensor(uq + "q_pn", [128, 1024], BF16))
        prs = ps.enter_context(nc.psum_tensor(uq + "q_prs", [128, 512], BF16))

        g = nc.gpsimd
        C.inc(g.dma_start(bqkp[:, 0:16], W["bqp"][:]), "lw", 16)
        C.inc(g.dma_start(bqkp[:, 16:32], W["bkp"][:]), "lw", 16)
        C.inc(g.dma_start(r_bv[:], W["bv_row"][:]), "lw", 16)
        C.inc(g.dma_start(r_bo[:], W["bo_row"][:]), "lw", 16)
        C.inc(g.dma_start(ones_bf[:], W["ones"][:]), "lw", 16)
        if DEBUG_L1A:
            C.dbg_qT = nc.dram_tensor("dbg_qT", [128, 2 * 2048], BF16, kind="ExternalOutput").ap()
            C.dbg_kT = nc.dram_tensor("dbg_kT", [128, 2 * 2048], BF16, kind="ExternalOutput").ap()
            C.dbg_Vb = nc.dram_tensor("dbg_Vb", [128, 16 * 384], BF16, kind="ExternalOutput").ap()
            C.dbg_fm = nc.dram_tensor("dbg_fm", [64, 2048], BF16, kind="ExternalOutput").ap()
            C.dbg_rs = nc.dram_tensor("dbg_rs", [32, 2048], BF16, kind="ExternalOutput").ap()
            C.dbg_rsT = nc.dram_tensor("dbg_rsT", [128, 16], F32, kind="ExternalOutput").ap()
            C.dbg_rcT = nc.dram_tensor("dbg_rcT", [128, 16], BF16, kind="ExternalOutput").ap()
            C.dbg_ftg = nc.dram_tensor("dbg_ftg", [128, 16 * 256], BF16, kind="ExternalOutput").ap()
        C.wait(nc.vector, "lw")
        nc.vector.memset(scl[:], 1.0 / np.sqrt(32.0))
        nc.vector.drain()
        C.inc(nc.vector.tensor_copy(scl[:], scl[:]), "dv", 1)
        C.wait(nc.scalar, "dv")
        C.wait(nc.tensor, "dv")
        for row, dstb in ((r_bv, bvb), (r_bo, bob)):
            C.inc(nc.vector.tensor_copy(r_bf[:], row[:]), "dv", 1)
            C.wait(nc.tensor, "dv")
            instr = None
            for cc in range(2):
                instr = nc.tensor.matmul(psc[0][:, cc * 512:(cc + 1) * 512], ones_bf[:],
                                         r_bf[:, cc * 512:(cc + 1) * 512],
                                         start=True, stop=True)
                instr = nc.tensor.matmul(psc[1][:, cc * 512:(cc + 1) * 512], ones_bf[:],
                                         r_bf[:, 1024 + cc * 512:1024 + (cc + 1) * 512],
                                         start=True, stop=True)
            C.inc(instr, "pe", 1)
            C.wait(nc.vector, "pe")
            nc.vector.tensor_copy(dstb[:, 0:1024], psc[0][:])
            C.inc(nc.vector.tensor_copy(dstb[:, 1024:2048], psc[1][:]), "dv", 1)
            C.wait(nc.tensor, "dv")

        for b in range(B_CORE):
            with ExitStack() as bs:
                yT = bs.enter_context(nc.sbuf_tensor(f"{uq}q_yT_{b}", [128, 16 * 2048], BF16))
                qT = bs.enter_context(nc.sbuf_tensor(f"{uq}q_qT_{b}", [128, 2 * 2048], BF16))
                kT = bs.enter_context(nc.sbuf_tensor(f"{uq}q_kT_{b}", [128, 2 * 2048], BF16))
                Vb = bs.enter_context(nc.sbuf_tensor(f"{uq}q_Vb_{b}", [128, 16 * 384], BF16))
                pt = [bs.enter_context(nc.sbuf_tensor(f"{uq}q_pt{i}_{b}", [128, 1024], BF16))
                      for i in range(2)]
                fm = bs.enter_context(nc.sbuf_tensor(f"{uq}q_fm_{b}", [64, 2048], BF16))
                rs32 = bs.enter_context(nc.sbuf_tensor(f"{uq}q_rs32_{b}", [32, 2048], BF16))
                rsT = bs.enter_context(nc.sbuf_tensor(f"{uq}q_rsT_{b}", [128, 16], F32))
                rcT = bs.enter_context(nc.sbuf_tensor(f"{uq}q_rcT_{b}", [128, 16], F32))
                rcTb = bs.enter_context(nc.sbuf_tensor(f"{uq}q_rcTb_{b}", [128, 16], BF16))
                ftg = bs.enter_context(nc.sbuf_tensor(f"{uq}q_ftg_{b}", [128, 16 * 256], BF16))
                wq_b = bs.enter_context(nc.sbuf_tensor(f"{uq}q_wq_{b}", [128, 16 * 256], BF16))
                wk_b = bs.enter_context(nc.sbuf_tensor(f"{uq}q_wk_{b}", [128, 16 * 256], BF16))
                wv_b = bs.enter_context(nc.sbuf_tensor(f"{uq}q_wv_{b}", [128, 16 * 256], BF16))
                yT3 = yT[:].rearrange("p (k s) -> p k s", k=16)

                for t in range(16):
                    row = b * S + t * 128
                    C.wait(nc.sync, "ac")
                    C.inc(nc.sync.dma_start(xt[:], x_in[row:row + 128, :]), "lx", 16)
                    C.wait(nc.vector, "lx")
                    C.wait(nc.scalar, "lx")
                    ln_tile(C, xt[:], ybf[:], st1)
                    C.wait(nc.tensor, "ac")
                    C.wait(nc.tensor, "dv")
                    for grp in range(2):
                        instr = None
                        for k in range(8):
                            kk = grp * 8 + k
                            instr = nc.tensor.transpose(
                                pnorm[:, k * 128:(k + 1) * 128],
                                ybf[:, kk * 128:(kk + 1) * 128], C.idn_bf[:])
                        C.inc(instr, "pe", 1)
                        C.wait(nc.vector, "pe")
                        C.inc(nc.vector.tensor_copy(
                            yT3[:, grp * 8:(grp + 1) * 8, t * 128:(t + 1) * 128],
                            pnorm[:].rearrange("p (k m) -> p k m", k=8)), "dv", 1)
                        C.wait(nc.tensor, "dv")

                for gi in range(8):
                    col = gi * 256
                    C.wait(nc.gpsimd, "pe")
                    for wb_, wsrc in ((wq_b, W["wq"]), (wk_b, W["wk"]), (wv_b, W["wv"])):
                        C.inc(g.dma_start(
                            wb_[:],
                            bass.AP(wsrc.tensor, col,
                                    [[D, 128], [128 * D, 16], [1, 256]])), "lw", 16)
                    C.wait(nc.tensor, "lw")
                    qkv_evac_ac = []
                    qkv_ac_base = C.n["ac"]
                    gidx = 0
                    for dst, wb_, bcol in ((qT, wq_b, 0), (kT, wk_b, 16)):
                        for mc in range(2):
                            for ns in range(4):
                                C.wait(nc.tensor, "ac",
                                       qkv_evac_ac[gidx - 2] if gidx >= 2 else qkv_ac_base)
                                instr = None
                                for kc in range(16):
                                    instr = nc.tensor.matmul(
                                        psc[gidx % 2][:, 0:512],
                                        wb_[:, kc * 256 + mc * 128: kc * 256 + (mc + 1) * 128],
                                        yT[:, kc * 2048 + ns * 512: kc * 2048 + (ns + 1) * 512],
                                        start=(kc == 0), stop=(kc == 15))
                                C.inc(instr, "pe", 1)
                                C.wait(nc.scalar, "pe")
                                C.inc(nc.scalar.activation(
                                    dst[:, mc * 2048 + ns * 512: mc * 2048 + (ns + 1) * 512],
                                    psc[gidx % 2][:, 0:512], AF.Identity,
                                    bias=bqkp[:, bcol + 2 * gi + mc: bcol + 2 * gi + mc + 1]),
                                    "ac", 1)
                                qkv_evac_ac.append(C.n["ac"])
                                gidx += 1
                    C.wait(nc.tensor, "ac")
                    C.wait(nc.vector, "ac")
                    C.inc(nc.vector.memset(Vb[:], 1.0), "dv", 1)
                    C.wait(nc.tensor, "dv")
                    for t in range(16):
                        instr = None
                        for kc in range(16):
                            instr = nc.tensor.matmul(
                                psc[1][:, 0:256],
                                yT[:, kc * 2048 + t * 128: kc * 2048 + (t + 1) * 128],
                                wv_b[:, kc * 256:(kc + 1) * 256],
                                start=(kc == 0), stop=(kc == 15))
                        C.inc(instr, "pe", 1)
                        C.wait(nc.vector, "pe")
                        C.inc(nc.vector.tensor_tensor(
                            Vb[:].rearrange("p (t h m) -> p t h m", t=16, h=4)
                            [:, t, :, 0:64].rearrange("p h m -> p (h m)")
                            if False else
                            bass.AP(Vb, Vb[:].offset + t * 384,
                                    [[Vb[:].ap[0][0], 128], [96, 4], [1, 64]]),
                            psc[1][:, 0:256].rearrange("p (h m) -> p h m", h=4),
                            bvb[:, col:col + 256].rearrange("p (h m) -> p h m", h=4),
                            ALU.add), "dv", 1)
                        C.wait(nc.tensor, "dv")
                    if DEBUG_L1A and b == 0 and gi == 0:
                        C.wait(nc.sync, "dv")
                        C.wait(nc.sync, "ac")
                        C.inc(nc.sync.dma_start(C.dbg_qT[:], qT[:]), "sx", 16)
                        C.inc(nc.sync.dma_start(C.dbg_kT[:], kT[:]), "sx", 16)
                        C.inc(nc.sync.dma_start(C.dbg_Vb[:], Vb[:]), "sx", 16)
                        C.wait(nc.sync, "sx")
                    C.wait(nc.tensor, "dv")
                    for h in range(4):
                        mc = h // 2
                        kb = 64 * (h % 2)
                        for sh in range(2):
                            exp_ac = []
                            scores_pe = []
                            ac_base = C.n["ac"]

                            def emit_scores(tc):
                                C.wait(nc.tensor, "ac",
                                       exp_ac[tc - 2] if tc >= 2 else ac_base)
                                instr = None
                                for nn in range(2):
                                    instr = nc.tensor.matmul(
                                        psc[tc % 2][:, nn * 512:(nn + 1) * 512],
                                        kT[kb:kb + 64,
                                           mc * 2048 + tc * 128: mc * 2048 + (tc + 1) * 128],
                                        qT[kb:kb + 64,
                                           mc * 2048 + sh * 1024 + nn * 512:
                                           mc * 2048 + sh * 1024 + (nn + 1) * 512],
                                        start=True, stop=True)
                                C.inc(instr, "pe", 1)
                                scores_pe.append(C.n["pe"])

                            def emit_exp(tc):
                                C.wait(nc.scalar, "pe", scores_pe[tc])
                                C.inc(nc.scalar.activation(pt[tc % 2][:], psc[tc % 2][:],
                                                           AF.Exp, scale=scl[:]),
                                      "ac", 1)
                                exp_ac.append(C.n["ac"])

                            def emit_pv(tc):
                                C.wait(nc.tensor, "ac", exp_ac[tc])
                                instr = None
                                for nn in range(2):
                                    instr = nc.tensor.matmul(
                                        pf[:, nn * 512:(nn + 1) * 512],
                                        Vb[:, tc * 384 + h * 96: tc * 384 + (h + 1) * 96],
                                        pt[tc % 2][:, nn * 512:(nn + 1) * 512],
                                        start=(tc == 0), stop=(tc == 15))
                                C.inc(instr, "pe", 1)

                            for tc in range(16):
                                emit_scores(tc)
                                emit_exp(tc)
                                if tc >= 1:
                                    emit_pv(tc - 1)
                            emit_pv(15)
                            C.wait(nc.vector, "pe")
                            nc.vector.tensor_copy(fm[:, sh * 1024:(sh + 1) * 1024],
                                                  pf[0:64, :])
                            C.inc(nc.vector.tensor_copy(rs32[:, sh * 1024:(sh + 1) * 1024],
                                                        pf[64:96, :]), "dv", 1)
                            C.wait(nc.tensor, "dv")
                        for grp in range(2):
                            instr = None
                            for t in range(8):
                                tt = grp * 8 + t
                                instr = nc.tensor.transpose(
                                    pnorm[:, t * 128:t * 128 + 64],
                                    fm[:, tt * 128:(tt + 1) * 128],
                                    C.idn_bf[0:64, 0:64])
                            C.inc(instr, "pe", 1)
                            C.wait(nc.vector, "pe")
                            C.inc(nc.vector.tensor_copy(
                                bass.AP(ftg,
                                        ftg[:].offset + grp * 8 * 256 + h * 64,
                                        [[ftg[:].ap[0][0], 128], [256, 8], [1, 64]]),
                                pnorm[:].rearrange("p (t m) -> p t m", t=8)[:, :, 0:64]),
                                "dv", 1)
                            C.wait(nc.tensor, "dv")
                        if DEBUG_L1A and b == 0 and gi == 0 and h == 0:
                            C.wait(nc.sync, "dv")
                            C.inc(nc.sync.dma_start(C.dbg_fm[:], fm[:]), "sx", 16)
                            C.inc(nc.sync.dma_start(C.dbg_rs[:], rs32[:]), "sx", 16)
                            C.wait(nc.sync, "sx")
                        instr = None
                        for t in range(16):
                            instr = nc.tensor.transpose(
                                prs[:, t * 32:(t + 1) * 32],
                                rs32[:, t * 128:(t + 1) * 128],
                                C.idn_bf[0:32, 0:32])
                        C.inc(instr, "pe", 1)
                        C.wait(nc.vector, "pe")
                        nc.vector.tensor_copy(
                            rsT[:], prs[:].rearrange("p (t m) -> p t m", t=16)[:, :, 0:1]
                            .rearrange("p t m -> p (t m)"))
                        nc.vector.drain()
                        nc.vector.reciprocal(rcT[:], rsT[:])
                        nc.vector.drain()
                        C.inc(nc.vector.tensor_copy(rcTb[:], rcT[:]), "dv", 1)
                        nc.vector.drain()
                        if DEBUG_L1A and b == 0 and gi == 0 and h == 0:
                            C.wait(nc.sync, "dv")
                            C.inc(nc.sync.dma_start(C.dbg_rsT[:], rsT[:]), "sx", 16)
                            C.inc(nc.sync.dma_start(C.dbg_rcT[:], rcTb[:]), "sx", 16)
                            C.wait(nc.sync, "sx")
                        C.inc(nc.vector.tensor_tensor(
                            bass.AP(ftg, ftg[:].offset + h * 64,
                                    [[ftg[:].ap[0][0], 128], [256, 16], [1, 64]]),
                            bass.AP(ftg, ftg[:].offset + h * 64,
                                    [[ftg[:].ap[0][0], 128], [256, 16], [1, 64]]),
                            rcTb[:].unsqueeze(2).broadcast_to([128, 16, 64]),
                            ALU.mult), "dv", 1)
                        C.wait(nc.tensor, "dv")
                    if DEBUG_L1A and b == 0 and gi == 0:
                        C.wait(nc.sync, "dv")
                        C.inc(nc.sync.dma_start(C.dbg_ftg[:], ftg[:]), "sx", 16)
                        C.wait(nc.sync, "sx")
                    C.wait(nc.sync, "dv")
                    C.inc(nc.sync.dma_start(
                        bass.AP(feats_scr.tensor, b * S * D + col,
                                [[D, 128], [128 * D, 16], [1, 256]]),
                        ftg[:].rearrange("p (t m) -> p t m", t=16)), "sx", 16)
                    C.wait(nc.sync, "sx")
                    C.wait(nc.vector, "sx")

            with ExitStack() as bs:
                ftT = bs.enter_context(nc.sbuf_tensor(f"{uq}q_ftT_{b}", [128, 16 * 2048], BF16))
                wo_cc = bs.enter_context(nc.sbuf_tensor(f"{uq}q_wocc_{b}", [128, 16 * 512], BF16))
                ft_t = bs.enter_context(nc.sbuf_tensor(f"{uq}q_ftt_{b}", [128, 2048], BF16))
                xsl = [bs.enter_context(nc.sbuf_tensor(f"{uq}q_xsl{i}_{b}", [128, 512], F32))
                       for i in range(2)]
                osl = [bs.enter_context(nc.sbuf_tensor(f"{uq}q_osl{i}_{b}", [128, 512], F32))
                       for i in range(2)]
                fT3 = ftT[:].rearrange("p (k s) -> p k s", k=16)
                for t in range(16):
                    row = b * S + t * 128
                    C.wait(nc.sync, "dv")
                    C.inc(nc.sync.dma_start(ft_t[:], feats_scr[row:row + 128, :]),
                          "lx", 16)
                    C.wait(nc.tensor, "lx")
                    C.wait(nc.tensor, "dv")
                    for grp in range(2):
                        instr = None
                        for k in range(8):
                            kk = grp * 8 + k
                            instr = nc.tensor.transpose(
                                pnorm[:, k * 128:(k + 1) * 128],
                                ft_t[:, kk * 128:(kk + 1) * 128], C.idn_bf[:])
                        C.inc(instr, "pe", 1)
                        C.wait(nc.vector, "pe")
                        C.inc(nc.vector.tensor_copy(
                            fT3[:, grp * 8:(grp + 1) * 8, t * 128:(t + 1) * 128],
                            pnorm[:].rearrange("p (k m) -> p k m", k=8)), "dv", 1)
                        C.wait(nc.tensor, "dv")
                for cc in range(4):
                    C.wait(nc.gpsimd, "pe")
                    C.inc(g.dma_start(
                        wo_cc[:],
                        bass.AP(W["wo"].tensor, cc * 512,
                                [[D, 128], [128 * D, 16], [1, 512]])), "lw", 16)
                    C.wait(nc.tensor, "lw")
                    d_stt_dv = []
                    d_dv_base = C.n["dv"]
                    for t in range(16):
                        row = b * S + t * 128
                        C.wait(nc.sync, "dv")
                        C.inc(nc.sync.dma_start(
                            xsl[t % 2][:], x_in[row:row + 128, cc * 512:(cc + 1) * 512]),
                            "lx", 16)
                        C.wait(nc.tensor, "dv",
                               d_stt_dv[t - 2] if t >= 2 else d_dv_base)
                        instr = None
                        for kc in range(16):
                            instr = nc.tensor.matmul(
                                psc[t % 2][:, 0:512],
                                fT3[:, kc, t * 128:(t + 1) * 128],
                                wo_cc[:, kc * 512:(kc + 1) * 512],
                                start=(kc == 0), stop=(kc == 15))
                        C.inc(instr, "pe", 1)
                        C.wait(nc.vector, "pe")
                        C.wait(nc.vector, "lx")
                        C.wait(nc.vector, "sx")
                        nc.vector.tensor_add(osl[t % 2][:], xsl[t % 2][:],
                                             bob[:, cc * 512:(cc + 1) * 512])
                        nc.vector.drain()
                        C.inc(nc.vector.scalar_tensor_tensor(
                            osl[t % 2][:], psc[t % 2][:, 0:512], 1.0, osl[t % 2][:],
                            ALU.mult, ALU.add), "dv", 1)
                        d_stt_dv.append(C.n["dv"])
                        C.wait(nc.sync, "dv")
                        C.inc(nc.sync.dma_start(
                            x_out[row:row + 128, cc * 512:(cc + 1) * 512], osl[t % 2][:]),
                            "sx", 16)
                    C.wait(nc.sync, "sx")

        C.wait(nc.sync, "sx")



PHASES_ALL = ["l0f_a", "l0a", "l0f_b", "l1f_a", "l1a", "l1f_b"]


def build_program(phases=None):
    phases = phases or PHASES_ALL
    from contextlib import ExitStack
    es = ExitStack()
    nc = bass.Bass("TRN2", target_bir_lowering=False)

    x_ext = nc.dram_tensor("x", [NTOK, D], F32, kind="ExternalInput").ap()
    out = nc.dram_tensor("out", [NTOK, D], F32, kind="ExternalOutput").ap()

    need = set(phases)
    inputs_used = ["x", "c_ident_bf"]

    ident_bf = nc.dram_tensor("c_ident_bf", [128, 128], BF16, kind="ExternalInput").ap()

    if need & {"l1f_a", "l1f_b"}:
        if SKIP_PREP:
            wgT = nc.dram_tensor("l1f_wgTb", [64 * 128, D], BF16, kind="ExternalInput").ap()
            wuT = nc.dram_tensor("l1f_wuTb", [64 * 128, D], BF16, kind="ExternalInput").ap()
            wdT = nc.dram_tensor("l1f_wdTb", [8192, D], BF16, kind="ExternalInput").ap()
            inputs_used += ["l1f_wgTb", "l1f_wuTb", "l1f_wdTb"]
        else:
            wgT_f = nc.dram_tensor("l1f_wgT", [64 * 128, D], F32, kind="ExternalInput").ap()
            wuT_f = nc.dram_tensor("l1f_wuT", [64 * 128, D], F32, kind="ExternalInput").ap()
            wdT_f = nc.dram_tensor("l1f_wdT", [8192, D], F32, kind="ExternalInput").ap()
            wgT = nc.dram_tensor("s_wgT", [64 * 128, D], BF16, kind="Internal").ap()
            wuT = nc.dram_tensor("s_wuT", [64 * 128, D], BF16, kind="Internal").ap()
            wdT = nc.dram_tensor("s_wdT", [8192, D], BF16, kind="Internal").ap()
            inputs_used += ["l1f_wgT", "l1f_wuT", "l1f_wdT"]

    if need & {"l0f_a", "l0f_b"}:
        wg8p = nc.dram_tensor("l0f_wg8p", [128, 16, 8], F32, kind="ExternalInput").ap()
        oh8p = nc.dram_tensor("c_oh8", [8, 8 * 128], BF16, kind="ExternalInput").ap()
        wi01_f = nc.dram_tensor("l0f_wi01", [8, 8, 128, 2 * D], F32, kind="ExternalInput").ap()
        wo_f = nc.dram_tensor("l0f_wo", [8, 1024, D], F32, kind="ExternalInput").ap()
        wi01 = nc.dram_tensor("s_wi01", [8, 8, 128, 2 * D], BF16, kind="Internal").ap()
        wos = nc.dram_tensor("s_wo", [8, 1024, D], BF16, kind="Internal").ap()
        inputs_used += ["l0f_wg8p", "c_oh8", "l0f_wi01", "l0f_wo"]

    if "l0a" in need:
        l0a_W = {
            "winp": nc.dram_tensor("l0a_winp", [128, 16, 8], F32, kind="ExternalInput").ap(),
            "wdnp": nc.dram_tensor("l0a_wdnp", [128, 16, 32], F32, kind="ExternalInput").ap(),
            "gbp": nc.dram_tensor("l0a_gbp", [2, 128, 16], F32, kind="ExternalInput").ap(),
            "wout": nc.dram_tensor("l0a_wout", [8, D], F32, kind="ExternalInput").ap(),
            "wup": nc.dram_tensor("l0a_wup", [32, D], F32, kind="ExternalInput").ap(),
            "dlg": nc.dram_tensor("l0a_dlg", [8, 1], F32, kind="ExternalInput").ap(),
            "bin": nc.dram_tensor("l0a_bin", [8, 1], F32, kind="ExternalInput").ap(),
            "bdn": nc.dram_tensor("l0a_bdn", [32, 1], F32, kind="ExternalInput").ap(),
            "rows": nc.dram_tensor("l0a_rows", [5, D], F32, kind="ExternalInput").ap(),
            "ones": nc.dram_tensor("c_ones_bf", [1, 128], BF16, kind="ExternalInput").ap(),
        }
        inputs_used += ["l0a_winp", "l0a_wdnp", "l0a_gbp", "l0a_wout", "l0a_wup",
                        "l0a_dlg", "l0a_bin", "l0a_bdn", "l0a_rows", "c_ones_bf"]

    if "l1a" in need:
        l1a_W = {
            "wq": nc.dram_tensor("l1a_wq", [D, D], F32, kind="ExternalInput").ap(),
            "wk": nc.dram_tensor("l1a_wk", [D, D], F32, kind="ExternalInput").ap(),
            "wv": nc.dram_tensor("l1a_wv", [D, D], F32, kind="ExternalInput").ap(),
            "wo": nc.dram_tensor("l1a_wo", [D, D], F32, kind="ExternalInput").ap(),
            "bqp": nc.dram_tensor("l1a_bqp", [128, 16], F32, kind="ExternalInput").ap(),
            "bkp": nc.dram_tensor("l1a_bkp", [128, 16], F32, kind="ExternalInput").ap(),
            "bv_row": nc.dram_tensor("l1a_bv", [1, D], F32, kind="ExternalInput").ap(),
            "bo_row": nc.dram_tensor("l1a_bo", [1, D], F32, kind="ExternalInput").ap(),
            "ones": nc.dram_tensor("c_ones1a", [1, 128], BF16, kind="ExternalInput").ap(),
        }
        feats_scr = nc.dram_tensor("s_feats", [NTOK, D], BF16,
                                   kind="ExternalOutput" if DEBUG_FEATS else "Internal").ap()
        if DEBUG_L1A:
            pass
        inputs_used += ["l1a_wq", "l1a_wk", "l1a_wv", "l1a_wo", "l1a_bqp",
                        "l1a_bkp", "l1a_bv", "l1a_bo", "c_ones1a"]

    xA = nc.dram_tensor("s_xA", [NTOK, D], F32, kind="Internal").ap()
    xB = nc.dram_tensor("s_xB", [NTOK, D], F32, kind="Internal").ap()

    with es:
        C = Ctx(nc, es)
        C.idn_bf = es.enter_context(nc.sbuf_tensor("c_idn", [128, 128], BF16))
        C.inc(nc.sync.dma_start(C.idn_bf[:], ident_bf[:]), "lx", 16)
        if need & {"l0f_a", "l0f_b"}:
            C.oh8_bf = es.enter_context(nc.sbuf_tensor("c_oh8s", [8, 8 * 128], BF16))
            C.wg8_bf = es.enter_context(nc.sbuf_tensor("c_wg8s", [128, 128], BF16))
            C.inc(nc.sync.dma_start(C.oh8_bf[:], oh8p[:]), "lx", 16)
            C.inc(nc.gpsimd.dma_start(
                C.wg8_bf[:], wg8p[:].rearrange("p a b -> p (a b)")), "lw", 16)
            C.wait(nc.tensor, "lw")
        C.wait(nc.tensor, "lx")
        C.barrier()

        jobs = []
        if need & {"l1f_a", "l1f_b"} and not SKIP_PREP:
            jobs += [(wgT_f, wgT), (wuT_f, wuT), (wdT_f, wdT)]
        if need & {"l0f_a", "l0f_b"}:
            jobs += [(wi01_f.rearrange("a b p d -> (a b p) d"),
                      wi01.rearrange("a b p d -> (a b p) d")),
                     (wo_f.rearrange("a m d -> (a m) d"),
                      wos.rearrange("a m d -> (a m) d"))]
        if DUMMY_PREP:
            dmy = nc.dram_tensor("s_dmy", [NTOK, D], BF16, kind="Internal").ap()
            jobs += [(x_ext, dmy)] * 3
        if jobs:
            emit_prep(C, jobs)
            C.barrier()

        cur = x_ext
        nxt = [xA, xB]
        ni = 0
        for ph in phases:
            dst = out if ph == phases[-1] else nxt[ni]
            if ph in ("l1f_a", "l1f_b"):
                emit_l1ffn(C, cur, dst, wgT, wuT, wdT,
                           wsem="lw" if ph == "l1f_a" else "g2")
            elif ph in ("l0f_a", "l0f_b"):
                emit_moe(C, cur, dst, wi01, wos)
            elif ph == "l0a":
                emit_l0a(C, cur, dst, l0a_W)
            elif ph == "l1a":
                emit_l1a(C, cur, dst, l1a_W, feats_scr)
            else:
                raise NotImplementedError(ph)
            C.barrier()
            cur = dst
            ni ^= 1

    return nc, inputs_used



def swizzle_gu(w):
    wT = w.T.reshape(16, 128, 64, 128)
    return np.ascontiguousarray(wT.transpose(2, 1, 0, 3).reshape(64 * 128, 2048))


def swizzle_wi(w):
    w5 = w.reshape(8, 16, 128, 8, 128)
    return np.ascontiguousarray(
        w5.transpose(0, 3, 2, 1, 4).reshape(8, 8, 128, 2048))


def make_in_maps(inputs, inputs_used, phases=None):
    x_full = np.asarray(inputs["L0_attn_x"], dtype=np.float32)
    base = {}
    base["c_ident_bf"] = np.eye(128, dtype=BF)
    if "l1f_wgT" in inputs_used:
        base["l1f_wgT"] = swizzle_gu(np.asarray(inputs["L1_ffn_w_gate"], np.float32))
        base["l1f_wuT"] = swizzle_gu(np.asarray(inputs["L1_ffn_w_up"], np.float32))
        base["l1f_wdT"] = np.ascontiguousarray(
            np.asarray(inputs["L1_ffn_w_down"], np.float32).T)
    if "l0a_winp" in inputs_used:
        pk = lambda v, w: np.ascontiguousarray(
            np.asarray(v, np.float32).reshape(16, 128, w).transpose(1, 0, 2))
        base["l0a_winp"] = pk(inputs["L0_attn_W_in"], 8)
        base["l0a_wdnp"] = pk(inputs["L0_attn_W_down"], 32)
        gm = np.asarray(inputs["L0_attn_ln_gamma"], np.float32)
        bt = np.asarray(inputs["L0_attn_ln_beta"], np.float32)
        base["l0a_gbp"] = np.ascontiguousarray(np.stack(
            [gm.reshape(16, 128).T, bt.reshape(16, 128).T]))
        base["l0a_wout"] = np.ascontiguousarray(np.asarray(inputs["L0_attn_W_out"], np.float32))
        base["l0a_wup"] = np.ascontiguousarray(np.asarray(inputs["L0_attn_W_up"], np.float32))
        base["l0a_dlg"] = np.asarray(inputs["L0_attn_decay_logit"], np.float32).reshape(8, 1)
        base["l0a_bin"] = np.asarray(inputs["L0_attn_b_in"], np.float32).reshape(8, 1)
        base["l0a_bdn"] = np.asarray(inputs["L0_attn_b_down"], np.float32).reshape(32, 1)
        base["l0a_rows"] = np.ascontiguousarray(np.stack([
            np.asarray(inputs["L0_attn_b_out"], np.float32),
            np.asarray(inputs["L0_attn_D_skip"], np.float32),
            np.asarray(inputs["L0_attn_b_up"], np.float32),
            gm, bt]))
        base["c_ones_bf"] = np.ones((1, 128), dtype=BF)
    if "l1a_wq" in inputs_used:
        base["l1a_wq"] = np.ascontiguousarray(np.asarray(inputs["L1_attn_Wq"], np.float32))
        base["l1a_wk"] = np.ascontiguousarray(np.asarray(inputs["L1_attn_Wk"], np.float32))
        base["l1a_wv"] = np.ascontiguousarray(np.asarray(inputs["L1_attn_Wv"], np.float32))
        base["l1a_wo"] = np.ascontiguousarray(np.asarray(inputs["L1_attn_Wo"], np.float32))
        pk16 = lambda v: np.ascontiguousarray(
            np.asarray(v, np.float32).reshape(16, 128).T)
        base["l1a_bqp"] = pk16(inputs["L1_attn_bq"])
        base["l1a_bkp"] = pk16(inputs["L1_attn_bk"])
        base["l1a_bv"] = np.asarray(inputs["L1_attn_bv"], np.float32).reshape(1, D)
        base["l1a_bo"] = np.asarray(inputs["L1_attn_bo"], np.float32).reshape(1, D)
        base["c_ones1a"] = np.ones((1, 128), dtype=BF)
    if "l0f_wi01" in inputs_used:
        wg = np.asarray(inputs["L0_ffn_w_gate"], np.float32)
        base["l0f_wg8p"] = np.ascontiguousarray(
            wg.reshape(16, 128, 8).transpose(1, 0, 2))
        oh = np.zeros((8, 8, 128), dtype=BF)
        for e in range(8):
            oh[e, e, :] = 1
        base["c_oh8"] = np.ascontiguousarray(oh.transpose(1, 0, 2).reshape(8, 8 * 128))
        base["l0f_wi01"] = np.ascontiguousarray(np.concatenate(
            [swizzle_wi(np.asarray(inputs["L0_ffn_wi_0"], np.float32)),
             swizzle_wi(np.asarray(inputs["L0_ffn_wi_1"], np.float32))], axis=3))
        base["l0f_wo"] = np.ascontiguousarray(np.asarray(inputs["L0_ffn_wo"], np.float32))
    in_maps = []
    for c in range(8):
        m = dict(base)
        m["x"] = np.ascontiguousarray(
            x_full[c * B_CORE:(c + 1) * B_CORE].reshape(NTOK, D))
        in_maps.append(m)
    return in_maps


def kernel(**inputs):
    nc, inputs_used = build_program()
    in_maps = make_in_maps(inputs, inputs_used)
    res = run_bass_kernel_spmd(nc, in_maps, core_ids=list(range(8)))
    outs = [r["out"].reshape(B_CORE, S, D) for r in res.results]
    return np.concatenate(outs, axis=0).astype(np.float32)
